# version 25
# baseline (speedup 1.0000x reference)
import sys
sys.path.insert(0, "/opt/trn_rl_repo")
import math
from contextlib import ExitStack

import numpy as np
import ml_dtypes

import concourse.bass as bass
import concourse.mybir as mybir
from concourse.bass_utils import run_bass_kernel_spmd

F32 = mybir.dt.float32
BF16 = mybir.dt.bfloat16
I32 = mybir.dt.int32
AF = mybir.ActivationFunctionType
ALU = mybir.AluOpType
AX = mybir.AxisListType

MAGIC = 1.5 * 2.0 ** 23
ISQ32 = 1.0 / math.sqrt(32.0)
DEBUG_TAPS = False

B, T, D = 4, 2048, 2048
H, HKV = 16, 4
NPOOL, NKEY = 16, 1040
FF = 8192

SET_A = [0, 14, 11, 10, 7, 6, 3, 2]
SET_B = [15, 13, 12, 9, 8, 5, 4, 1]
SLOT_W = [1040, 912, 656, 400, 144, 14, 10, 6]
MASK_COLS = sum(SLOT_W)
SLOT_OFF = [0]
for _w in SLOT_W:
    SLOT_OFF.append(SLOT_OFF[-1] + _w)

ENGINES = ("sync", "vector", "scalar", "tensor", "gpsimd")


class _Sem:
    __slots__ = ("name", "h")

    def __init__(self, name, h):
        self.name, self.h = name, h


class Plan:
    SELF_SYNC = ("vector", "scalar", "gpsimd")

    def __init__(self, nc, sem_stack):
        self.nc = nc
        self.sem_stack = sem_stack
        self.steps = {e: [] for e in ENGINES}
        self.counts = {}
        self.self_sems = {}

    def sem(self, name):
        h = self.sem_stack.enter_context(self.nc.semaphore(name))
        self.counts[name] = 0
        return _Sem(name, h)

    def set_self_sems(self, mapping):
        self.self_sems = mapping

    def step(self, eng, emit, waits=(), incs=(), dma=False):
        incs = list(incs)
        n_caller = len(incs)
        if any(n >= 16 for _s, n in incs):
            dma = True
        if eng in self.SELF_SYNC and not dma:
            ssem = self.self_sems[eng]
            if not any(s is ssem for s, n in incs):
                incs.append((ssem, 1))
        tickets = []
        for s, n in incs:
            self.counts[s.name] += n
            tickets.append(self.counts[s.name])
        self.steps[eng].append((emit, tuple(waits), tuple(incs)))
        if n_caller <= 1:
            return tickets[0] if tickets else 0
        return tickets[:n_caller]

    def now(self, sem):
        return self.counts[sem.name]

    def emit_all(self, block):
        def run(eng_name, handle):
            ssem = self.self_sems.get(eng_name)
            running = 0
            for emit, waits, incs in self.steps[eng_name]:
                if ssem is not None and running > 0:
                    handle.wait_ge(ssem.h, running)
                for s, v in waits:
                    if v > 0:
                        handle.wait_ge(s.h, v)
                inst = emit(handle)
                for s, n in incs:
                    inst.then_inc(s.h, n)
                    if s is ssem:
                        running += n

        @block.sync
        def _(e):
            run("sync", e)

        @block.vector
        def _(e):
            run("vector", e)

        @block.scalar
        def _(e):
            run("scalar", e)

        @block.tensor
        def _(e):
            run("tensor", e)

        @block.gpsimd
        def _(e):
            run("gpsimd", e)


def build_program():
    nc = bass.Bass("TRN2", target_bir_lowering=False, debug=False)

    dxf = nc.dram_tensor("x_full", [T, D], F32, kind="ExternalInput").ap()
    dxo = nc.dram_tensor("x_own", [1024, D], F32, kind="ExternalInput").ap()
    dWkv = nc.dram_tensor("wkv", [D, 512], F32, kind="ExternalInput").ap()
    dWq = nc.dram_tensor("wq", [D, 1024], F32, kind="ExternalInput").ap()
    dWo = nc.dram_tensor("wo", [1024, D], F32, kind="ExternalInput").ap()
    dls = nc.dram_tensor("ls", [1, H], F32, kind="ExternalInput").ap()
    dgsub = nc.dram_tensor("gsub", [1, FF], F32, kind="ExternalInput").ap()
    dcosq = nc.dram_tensor("d_cosq", [128, 8 * 256], F32, kind="ExternalInput").ap()
    dsinq = nc.dram_tensor("d_sinq", [128, 8 * 256], F32, kind="ExternalInput").ap()
    dcosk = nc.dram_tensor("d_cosk", [128, 16 * 64], F32, kind="ExternalInput").ap()
    dsink = nc.dram_tensor("d_sink", [128, 16 * 64], F32, kind="ExternalInput").ap()
    dmask = nc.dram_tensor("d_masks", [128, MASK_COLS], BF16, kind="ExternalInput").ap()
    dident = nc.dram_tensor("d_ident", [128, 128], BF16, kind="ExternalInput").ap()
    dpmat = nc.dram_tensor("d_pmat", [1024, NPOOL], BF16, kind="ExternalInput").ap()
    dconsts = nc.dram_tensor("d_consts", [128, 16], F32, kind="ExternalInput").ap()
    dones = nc.dram_tensor("ones_row", [1, 128], F32, kind="ExternalInput").ap()
    dshg = nc.dram_tensor("shard_g", [256, FF], F32, kind="ExternalInput").ap()
    dshu = nc.dram_tensor("shard_u", [256, FF], F32, kind="ExternalInput").ap()
    dshd = nc.dram_tensor("shard_d", [1024, D], F32, kind="ExternalInput").ap()

    dout = nc.dram_tensor("y", [1024, D], F32, kind="ExternalOutput").ap()
    if DEBUG_TAPS:
        d_x1 = nc.dram_tensor("dbg_x1", [1024, D], F32, kind="ExternalOutput").ap()
        d_ao = nc.dram_tensor("dbg_ao", [1024, 1024], BF16, kind="ExternalOutput").ap()
        d_qt = nc.dram_tensor("dbg_qt", [128, 8 * 1024], BF16, kind="ExternalOutput").ap()
        d_kt = nc.dram_tensor("dbg_kt", [128, 4 * NKEY], BF16, kind="ExternalOutput").ap()
        d_pool = nc.dram_tensor("dbg_pool", [NPOOL, 512], BF16, kind="ExternalOutput").ap()
        d_slab = nc.dram_tensor("dbg_slab", [128, 4 * FF], BF16, kind="ExternalOutput").ap()
        d_aq = nc.dram_tensor("dbg_aq", [128, 4 * FF], BF16, kind="ExternalOutput").ap()
        d_y2 = nc.dram_tensor("dbg_y2", [128, 4 * D], F32, kind="ExternalOutput").ap()
        d_rs = nc.dram_tensor("dbg_rs", [128, 16], F32, kind="ExternalOutput").ap()
        d_gup = nc.dram_tensor("dbg_gup", [128, 256], F32, kind="ExternalOutput").ap()
        d_qg = nc.dram_tensor("dbg_qg", [1024, 256], BF16, kind="ExternalOutput").ap()
        d_st = nc.dram_tensor("dbg_st", [128, 96], F32, kind="ExternalOutput").ap()
        d_mw = nc.dram_tensor("dbg_mw", [128, 8], F32, kind="ExternalOutput").ap()
        d_qd = nc.dram_tensor("dbg_qd", [1024, 512], BF16, kind="ExternalOutput").ap()

    ag_g_in = nc.dram_tensor("ag_g_in", [8192, 256], BF16)
    ag_u_in = nc.dram_tensor("ag_u_in", [8192, 256], BF16)
    ag_d_in = nc.dram_tensor("ag_d_in", [4096, 512], BF16)
    qgT = nc.dram_tensor("qgT", [65536, 256], BF16, addr_space="Shared")
    quT = nc.dram_tensor("quT", [65536, 256], BF16, addr_space="Shared")
    qdT = nc.dram_tensor("qdT", [32768, 512], BF16, addr_space="Shared")
    ar_in = nc.dram_tensor("ar_in", [1, 3], F32)
    ar_out = nc.dram_tensor("ar_out", [1, 3], F32, addr_space="Shared")
    ar_out2 = nc.dram_tensor("ar_out2", [1, 3], F32, addr_space="Shared")
    x1_spill = nc.dram_tensor("x1_spill", [1024, D], F32)
    xqt_spill = nc.dram_tensor("xqt_spill", [128, 8 * 2048], BF16)

    sem_stack = ExitStack()
    p = Plan(nc, sem_stack)
    s_pe = p.sem("pe")
    s_dve = p.sem("dve")
    s_act = p.sem("act")
    s_pool = p.sem("pool")
    p.set_self_sems({"vector": s_dve, "scalar": s_act, "gpsimd": s_pool})
    s_ldc1 = p.sem("ldc1")
    s_ldc2 = p.sem("ldc2")
    s_ldc3 = [p.sem("ldc3a"), p.sem("ldc3b")]
    s_ldc4 = p.sem("ldc4")
    s_ldxa = [p.sem("ldxa0"), p.sem("ldxa1")]
    s_ldxb = p.sem("ldxb")
    s_ldwa = [p.sem("ldwa0"), p.sem("ldwa1")]
    s_ldwb = [p.sem("ldwb0"), p.sem("ldwb1")]
    s_lwo = p.sem("lwo")
    s_ktd = p.sem("ktd")
    s_ldsa = [p.sem("ldsa0"), p.sem("ldsa1")]
    s_ldsb = p.sem("ldsb")
    s_car = p.sem("car")
    s_cswm = p.sem("cswm")
    s_cmask = p.sem("cmask")
    s_cag = [p.sem("cag0"), p.sem("cag1")]
    s_ccar = p.sem("ccar")
    s_ccg = p.sem("ccg")
    s_ccu = p.sem("ccu")
    s_ccd = p.sem("ccd")
    s_ccf = p.sem("ccf")
    s_spx = p.sem("spx")
    s_sp1 = p.sem("sp1")
    s_spq = p.sem("spq")
    s_xqt = p.sem("xqt")
    s_lg = [p.sem("lg0"), p.sem("lg1")]
    s_lu = [p.sem("lu0"), p.sem("lu1")]
    s_lwd = [p.sem(f"lwd{i}") for i in range(4)]
    s_lx = [p.sem("lx0"), p.sem("lx1")]
    s_o = [p.sem(f"out{i}") for i in range(4)]

    CONST_VALS = [MAGIC, 1.0, 0.0, 1e-5, 1e-6, -0.5, 0.5, 127.0, -MAGIC,
                  1.0 / D, 30000.0, 1.49, -1.49, 1.0 / FF, 2.0, -1.0]

    ctx = ExitStack()
    with ctx, sem_stack:
        e = ctx.enter_context

        consts = e(nc.sbuf_tensor("consts", [128, 16], F32))
        for j, v in enumerate(CONST_VALS):
            nc.const_aps.register(F32, float(v), consts[:, j:j + 1])
        idt = e(nc.sbuf_tensor("idt", [128, 128], BF16))
        ones_r = e(nc.sbuf_tensor("ones_r", [1, 128], F32))
        ones_c = e(nc.sbuf_tensor("ones_c", [128, 1], F32))
        els_raw = e(nc.sbuf_tensor("els_raw", [1, H], F32))
        els_rep = e(nc.sbuf_tensor("els_rep", [128, H], F32))
        swm = e(nc.sbuf_tensor("swm", [1, 8], F32))
        shsum = e(nc.sbuf_tensor("shsum", [3, 1], F32))
        sw_rep = e(nc.sbuf_tensor("sw_rep", [128, 4], F32))
        mw_rep = e(nc.sbuf_tensor("mw_rep", [128, 4], F32))
        pooled = e(nc.sbuf_tensor("pooled", [NPOOL, 512], BF16))
        partials = e(nc.sbuf_tensor("partials", [128, 52], F32))
        st_s1 = e(nc.sbuf_tensor("st_s1", [128, 8], F32))
        st_rowS = e(nc.sbuf_tensor("st_rowS", [128, 8], F32))
        st_drow = e(nc.sbuf_tensor("st_drow", [128, 8], F32))
        gup_bn = e(nc.sbuf_tensor("gup_bn", [128, 8 * 16 * 6], F32))
        bnln = e(nc.sbuf_tensor("bnln", [128, 24], F32))
        st_st = e(nc.sbuf_tensor("st_st", [128, 96], F32))
        bnag = e(nc.sbuf_tensor("bnag", [128, 2], F32))
        gup_mxg = e(nc.sbuf_tensor("gup_mxg", [128, 8 * 16], F32))
        sc_a = e(nc.sbuf_tensor("sc_a", [128, 8], F32))
        sc_b = e(nc.sbuf_tensor("sc_b", [128, 8], F32))
        sc_c = e(nc.sbuf_tensor("sc_c", [128, 8], F32))
        flb = e(nc.sbuf_tensor("flb", [128, 4], F32))
        flb_init = [0]

        def dve_flush(n=1):
            t = 0
            for _ in range(n):
                t = p.step("vector", lambda E: E.tensor_copy(flb[:, 1:2], flb[:, 0:1]),
                           incs=[(s_dve, 1)])
            return t

        def act_flush(n=1):
            t = 0
            for _ in range(n):
                t = p.step("scalar", lambda E: E.activation(
                    flb[:, 3:4], flb[:, 2:3], AF.Copy, bias=0.0, scale=1.0),
                    waits=[(s_dve, flb_init[0])], incs=[(s_act, 1)])
            return t

        pfl_scope = ExitStack()
        pFL = pfl_scope.enter_context(nc.psum_tensor("pFL", [128, 4], F32))
        pbc_scope = ExitStack()
        pbc = pbc_scope.enter_context(nc.psum_tensor("pbc", [128, 512], F32))

        def pe_flush():
            return p.step("tensor", lambda E: E.matmul(
                pFL[0:1, 0:1], ones_r[0:1, 0:1], ones_r[0:1, 0:1],
                start=True, stop=True), incs=[(s_pe, 1)])

        for dst, src in ((consts[:], dconsts), (idt[:], dident), (ones_r[:], dones),
                         (els_raw[:], dls)):
            p.step("sync", lambda E, dst=dst, src=src: E.dma_start(out=dst, in_=src),
                   incs=[(s_ldc1, 16)])
        t_ldc4 = p.now(s_ldc1)
        p.step("vector", lambda E: E.memset(ones_c[:], 1.0), incs=[(s_dve, 1)])
        t_onec = p.now(s_dve)
        flb_init[0] = p.step("vector", lambda E: E.memset(flb[:], 0.0),
                             incs=[(s_dve, 1)])

        p.step("scalar", lambda E: E.activation(els_raw[:], els_raw[:], AF.Exp,
                                                bias=0.0, scale=1.0),
               waits=[(s_ldc1, t_ldc4)], incs=[(s_act, 1)])
        p.step("vector", lambda E: E.tensor_scalar(els_raw[:], els_raw[:], ISQ32, None,
                                                   ALU.mult),
               waits=[(s_act, p.now(s_act))], incs=[(s_dve, 1)])
        dve_flush(2)
        tt = p.step("tensor", lambda E: E.matmul(pbc[:, 0:H], ones_r[:], els_raw[:],
                                                 start=True, stop=True),
                    waits=[(s_dve, p.now(s_dve))], incs=[(s_pe, 1)])
        tt = pe_flush()
        tt = pe_flush()
        t_elsrep = p.step("vector", lambda E: E.tensor_copy(els_rep[:], pbc[:, 0:H]),
                          waits=[(s_pe, tt)], incs=[(s_dve, 1)])

        sx = ExitStack()
        x_res = sx.enter_context(nc.sbuf_tensor("x_res", [128, 8 * D], F32))

        skv = ExitStack()
        KV = [skv.enter_context(nc.sbuf_tensor(f"kv{c}", [128, 512], BF16)) for c in range(16)]
        KT = [skv.enter_context(nc.sbuf_tensor(f"kt{k}", [128, NKEY], BF16)) for k in range(HKV)]
        QT = [skv.enter_context(nc.sbuf_tensor(f"qt{s}", [128, 1024], BF16)) for s in range(8)]
        squant = ExitStack()
        shstage = [squant.enter_context(nc.sbuf_tensor(f"shst{i}", [128, 1024], F32)) for i in range(2)]
        shq1s = [squant.enter_context(nc.sbuf_tensor(f"shq1{i}", [128, 1024], F32))
                 for i in range(2)]
        shi = squant.enter_context(nc.sbuf_tensor("shi", [128, 1024], I32))
        shbs = [squant.enter_context(nc.sbuf_tensor(f"shb{i}", [128, 1024], BF16))
                for i in range(2)]

        shard_tiles = []
        for widx, dsh in enumerate((dshg, dshu, dshd)):
            v = dsh.rearrange("(r p) f -> r p f", p=128)
            nr = 2 if widx < 2 else 8
            ncb = 8 if widx < 2 else 2
            for r in range(nr):
                for cb in range(ncb):
                    shard_tiles.append((v[r, :, cb * 1024:(cb + 1) * 1024], widx, r, cb))

        def quant_gen():
            last_red = {0: 0, 1: 0}
            for i, (ap_, widx, r, cb) in enumerate(shard_tiles):
                par = i % 2
                tld = p.step("gpsimd", lambda E, ap_=ap_, par=par: E.dma_start(
                    out=shstage[par][:], in_=ap_),
                    waits=[(s_dve, last_red[par])], incs=[(s_ldsa[par], 16)])
                last_red[par] = p.step(
                    "vector", lambda E, i=i, par=par: E.reduce_sum(
                        partials[:, i:i + 1], shstage[par][:], AX.X,
                        apply_absolute_value=True),
                    waits=[(s_ldsa[par], tld)], incs=[(s_dve, 1)], dma=True)
                yield
            for widx in range(3):
                lo, hi = widx * 16, widx * 16 + 16
                p.step("vector", lambda E, widx=widx, lo=lo, hi=hi: E.reduce_sum(
                    sc_a[:, widx:widx + 1], partials[:, lo:hi], AX.X), incs=[(s_dve, 1)])
            t_pred = dve_flush(2)
            tt = p.step("tensor", lambda E: E.matmul(pbc[0:3, 16:17], sc_a[:, 0:3], ones_c[:],
                                                     start=True, stop=True),
                        waits=[(s_dve, t_pred), (s_dve, t_onec)], incs=[(s_pe, 1)])
            tt = pe_flush()
            tt = pe_flush()
            t_shsum = p.step("vector", lambda E: E.tensor_copy(shsum[:], pbc[0:3, 16:17]),
                             waits=[(s_pe, tt)], incs=[(s_dve, 1)])
            t_shsum = dve_flush(2)
            p.step("gpsimd", lambda E: E.dma_start(
                out=ar_in.ap().rearrange("a b -> b a"), in_=shsum[:]),
                waits=[(s_dve, t_shsum)], incs=[(s_car, 16)], dma=True)
            p.step("gpsimd", lambda E: E.collective_compute(
                "AllReduce", ALU.add, replica_groups=[list(range(8))],
                ins=[ar_in.ap().opt()], outs=[ar_out.ap().opt()]),
                waits=[(s_car, 16)], incs=[(s_ccar, 1)], dma=True)
            p.step("gpsimd", lambda E: E.dma_start(out=swm[:, 4:7], in_=ar_out.ap()),
                   waits=[(s_ccar, 1)], incs=[(s_cswm, 16)], dma=True)
            p.step("vector", lambda E: E.tensor_scalar(swm[:, 0:3], swm[:, 4:7],
                                                       1.0 / (2048.0 * 8192.0), 1e-5,
                                                       ALU.mult, ALU.max),
                   waits=[(s_cswm, 16)], incs=[(s_dve, 1)])
            p.step("vector", lambda E: E.reciprocal(swm[:, 4:7], swm[:, 0:3]),
                   incs=[(s_dve, 1)])
            t_sw = dve_flush(2)
            tt = p.step("tensor", lambda E: E.matmul(pbc[:, 20:23], ones_r[:], swm[:, 0:3],
                                                     start=True, stop=True),
                        waits=[(s_dve, t_sw)], incs=[(s_pe, 1)])
            tt2 = p.step("tensor", lambda E: E.matmul(pbc[:, 24:27], ones_r[:], swm[:, 4:7],
                                                      start=True, stop=True), incs=[(s_pe, 1)])
            tt2 = pe_flush()
            tt2 = pe_flush()
            p.step("vector", lambda E: E.tensor_copy(mw_rep[:, 0:3], pbc[:, 20:23]),
                   waits=[(s_pe, tt2)], incs=[(s_dve, 1)])
            t_swrep = p.step("vector", lambda E: E.tensor_copy(sw_rep[:, 0:3], pbc[:, 24:27]),
                             waits=[(s_pe, tt2)], incs=[(s_dve, 1)])

            def ag_dst(widx, r, cb):
                if widx < 2:
                    agt = ag_g_in if widx == 0 else ag_u_in
                    return agt.ap().rearrange("(fb c p) n -> c p fb n", c=2, p=128)[
                        r, :, cb * 4:(cb + 1) * 4, :]
                return ag_d_in.ap().rearrange("(d fl p) n -> fl p d n", fl=8, p=128)[
                    r, :, cb * 2:(cb + 1) * 2, :]

            last_q1 = {0: last_red[0], 1: last_red[1]}
            q3_tick = {0: 0, 1: 0}
            store_tick = {0: 0, 1: 0}
            for i, (ap_, widx, r, cb) in enumerate(shard_tiles):
                par = i % 2
                shb = shbs[par]
                shq1 = shq1s[par]
                tld = p.step("gpsimd", lambda E, ap_=ap_, par=par: E.dma_start(
                    out=shstage[par][:], in_=ap_),
                    waits=[(s_dve, last_q1[par])], incs=[(s_ldsa[par], 16)], dma=True)
                p.step("vector", lambda E, widx=widx, par=par, shq1=shq1: E.tensor_scalar(
                    shq1[:], shstage[par][:], sw_rep[:, widx:widx + 1], 1.49,
                    ALU.mult, ALU.min),
                    waits=[(s_ldsa[par], tld), (s_dve, t_swrep)], incs=[(s_dve, 1)])
                last_q1[par] = p.step("vector", lambda E, shq1=shq1: E.tensor_scalar(
                    shq1[:], shq1[:], -1.49, MAGIC, ALU.max, ALU.add),
                    incs=[(s_dve, 1)])
                q3_tick[par] = p.step("vector", lambda E, shq1=shq1: E.tensor_scalar(
                    shi[:], shq1[:], -MAGIC, None, ALU.add),
                    incs=[(s_dve, 1)])
                tq = p.step("vector", lambda E, shb=shb: E.tensor_copy(shb[:], shi[:]),
                            waits=[(s_cag[par], store_tick[par])], incs=[(s_dve, 1)])
                if widx < 2:
                    src = shb[:].rearrange("p (fb n) -> p fb n", n=256)
                else:
                    src = shb[:].rearrange("p (d n) -> p d n", n=512)
                p.step("gpsimd", lambda E, dst=ag_dst(widx, r, cb), src=src: E.dma_start(
                    out=dst, in_=src),
                    waits=[(s_dve, tq)], incs=[(s_cag[par], 16)], dma=True)
                store_tick[par] = p.now(s_cag[par])
                yield
            t_ag_all = [p.now(s_cag[0]), p.now(s_cag[1])]
            _cc['g'] = p.step("gpsimd", lambda E: E.collective_compute(
                "AllGather", ALU.bypass, replica_groups=[list(range(8))],
                ins=[ag_g_in.ap().opt()], outs=[qgT.ap().opt()]),
                waits=[(s_cag[0], t_ag_all[0]), (s_cag[1], t_ag_all[1])],
                incs=[(s_ccg, 1)], dma=True)
            _cc['u'] = p.step("gpsimd", lambda E: E.collective_compute(
                "AllGather", ALU.bypass, replica_groups=[list(range(8))],
                ins=[ag_u_in.ap().opt()], outs=[quT.ap().opt()]), incs=[(s_ccu, 1)], dma=True)
            _cc['d'] = p.step("gpsimd", lambda E: E.collective_compute(
                "AllGather", ALU.bypass, replica_groups=[list(range(8))],
                ins=[ag_d_in.ap().opt()], outs=[qdT.ap().opt()]), incs=[(s_ccd, 1)], dma=True)
            _cc['fl'] = p.step("gpsimd", lambda E: E.collective_compute(
                "AllReduce", ALU.add, replica_groups=[list(range(8))],
                ins=[ar_in.ap().opt()], outs=[ar_out2.ap().opt()]),
                incs=[(s_ccf, 1)], dma=True)

            while True:
                yield

        qgen = quant_gen()
        _cc = {}
        def _qset():
            pass
        sw1 = ExitStack()
        Wkv_b = [sw1.enter_context(nc.sbuf_tensor(f"wkvb{c}", [128, 512], BF16))
                 for c in range(16)]
        cosk = sw1.enter_context(nc.sbuf_tensor("cosk", [128, 16 * 64], F32))
        sink = sw1.enter_context(nc.sbuf_tensor("sink", [128, 16 * 64], F32))
        pmat = sw1.enter_context(nc.sbuf_tensor("pmat", [128, 8 * NPOOL], BF16))
        wstage1 = [sw1.enter_context(nc.sbuf_tensor(f"wst{i}", [128, 512], F32))
                  for i in range(2)]
        xstage1 = [sw1.enter_context(nc.sbuf_tensor(f"xst{i}", [128, D], F32))
                  for i in range(2)]
        xbst1 = sw1.enter_context(nc.sbuf_tensor("xbst1", [128, D], BF16))
        xtst1 = sw1.enter_context(nc.sbuf_tensor("xtst1", [128, D], BF16))
        rtmpk1 = sw1.enter_context(nc.sbuf_tensor("rtmpk1", [128, 256], BF16))
        rtmpk2 = sw1.enter_context(nc.sbuf_tensor("rtmpk2", [128, 256], BF16))

        for dst, src in ((cosk[:], dcosk), (sink[:], dsink),
                         (pmat[:].rearrange("p (c n) -> p c n", n=NPOOL),
                          dpmat.rearrange("(c p) n -> p c n", p=128))):
            p.step("sync", lambda E, dst=dst, src=src: E.dma_start(out=dst, in_=src),
                   incs=[(s_ldc2, 16)])
        t_ldck = p.now(s_ldc2)

        conv_t = [0, 0]
        for c in range(16):
            par = c % 2
            tld = p.step("sync", lambda E, c=c, par=par: E.dma_start(
                out=wstage1[par][:], in_=dWkv[c * 128:(c + 1) * 128, :]),
                waits=[(s_act, conv_t[par])], incs=[(s_ldwa[par], 16)])
            conv_t[par] = p.step("scalar", lambda E, c=c, par=par: E.activation(
                Wkv_b[c][:], wstage1[par][:], AF.Copy, bias=0.0, scale=1.0),
                waits=[(s_ldwa[par], tld)], incs=[(s_act, 1)])
        t_wkv = p.now(s_act)

        ps1 = ExitStack()
        pQKV1 = [ps1.enter_context(nc.psum_tensor(f"pqkv{i}", [128, 512], F32))
                for i in range(2)]
        pTr1 = ps1.enter_context(nc.psum_tensor("pTr1", [128, 2048], BF16))
        pPool = ps1.enter_context(nc.psum_tensor("pPool", [NPOOL, 512], F32))

        def rope_steps(view_base, cosv, sinv, wait_t, rA, rB, wait_sem=None):
            lo, hi = view_base[:, :, 32:48], view_base[:, :, 48:64]
            nh = lo.shape[1]
            t1v = rA[:, 0:nh * 16].rearrange("p (h d) -> p h d", d=16)
            t2v = rB[:, 0:nh * 16].rearrange("p (h d) -> p h d", d=16)
            p.step("vector", lambda E: E.tensor_tensor(t1v, lo, sinv, ALU.mult),
                   waits=[(wait_sem or s_dve, wait_t)], incs=[(s_dve, 1)])
            p.step("vector", lambda E: E.tensor_tensor(t2v, lo, cosv, ALU.mult),
                   incs=[(s_dve, 1)])
            p.step("vector", lambda E: E.scalar_tensor_tensor(
                lo, hi, -1.0, sinv, ALU.mult, ALU.mult), incs=[(s_dve, 1)])
            p.step("vector", lambda E: E.tensor_tensor(lo, lo, t2v, ALU.add),
                   incs=[(s_dve, 1)])
            p.step("vector", lambda E: E.tensor_tensor(hi, hi, cosv, ALU.mult),
                   incs=[(s_dve, 1)])
            return p.step("vector", lambda E: E.tensor_tensor(hi, hi, t1v, ALU.add),
                          incs=[(s_dve, 1)])

        xst_free = [0, 0]
        xb_free = 0
        xt_free = 0
        pTr_free = 0
        pq_free = [0, 0]
        kv_rope = [0] * 16
        for c in range(16):
            par = c % 2
            tld = p.step("sync", lambda E, c=c, par=par: E.dma_start(
                out=xstage1[par][:], in_=dxf[c * 128:(c + 1) * 128, :]),
                waits=[(s_act, xst_free[par])], incs=[(s_ldxa[par], 16)])
            tcv = p.step("scalar", lambda E, par=par: E.activation(
                xbst1[:], xstage1[par][:], AF.Copy, bias=0.0, scale=1.0),
                waits=[(s_ldxa[par], tld), (s_pe, xb_free)], incs=[(s_act, 1)])
            xst_free[par] = tcv
            for c2 in range(16):
                tt = p.step("tensor", lambda E, c2=c2: E.transpose(
                    pTr1[:, c2 * 128:(c2 + 1) * 128], xbst1[:, c2 * 128:(c2 + 1) * 128],
                    idt[:]),
                    waits=([(s_act, tcv), (s_act, pTr_free)] if c2 == 0 else ()),
                    incs=[(s_pe, 1)])
            xb_free = tt
            tev = p.step("scalar", lambda E: E.activation(
                xtst1[:], pTr1[:], AF.Copy, bias=0.0, scale=1.0),
                waits=[(s_pe, tt), (s_pe, xt_free)], incs=[(s_act, 1)])
            pTr_free = tev
            for c2 in range(16):
                w = ([(s_act, tev), (s_act, t_wkv), (s_act, pq_free[par])]
                     if c2 == 0 else ())
                tmm = p.step("tensor", lambda E, c2=c2, par=par: E.matmul(
                    pQKV1[par][:], xtst1[:, c2 * 128:(c2 + 1) * 128], Wkv_b[c2][:],
                    start=(c2 == 0), stop=(c2 == 15)),
                    waits=w, incs=[(s_pe, 1)])
            xt_free = tmm
            tev2 = p.step("scalar", lambda E, c=c, par=par: E.activation(
                KV[c][:], pQKV1[par][:], AF.Copy, bias=0.0, scale=1.0),
                waits=[(s_pe, tmm), (s_ldc2, t_ldck)], incs=[(s_act, 1)])
            pq_free[par] = tev2
            kvv = KV[c][:, 0:256].rearrange("p (h d) -> p h d", d=64)
            ck = cosk[:, c * 64:(c + 1) * 64].rearrange("p (h d) -> p h d", d=16)
            sk = sink[:, c * 64:(c + 1) * 64].rearrange("p (h d) -> p h d", d=16)
            kv_rope[c] = rope_steps(kvv, ck, sk, tev2, rtmpk1, rtmpk2,
                                    wait_sem=s_act)
            for _ in range(4):
                next(qgen)

        for c in range(8):
            w = [(s_dve, kv_rope[7])] if c == 0 else ()
            tpool = p.step("tensor", lambda E, c=c: E.matmul(
                pPool[:], pmat[:, c * NPOOL:(c + 1) * NPOOL], KV[c][:],
                start=(c == 0), stop=(c == 7)),
                waits=w, incs=[(s_pe, 1)])
        tpool = pe_flush()
        t_pooled = p.step("scalar", lambda E: E.activation(
            pooled[:], pPool[:], AF.Copy, bias=0.0, scale=1.0),
            waits=[(s_pe, tpool)], incs=[(s_act, 1)])

        kt_dup = [0] * HKV
        for k in range(HKV):
            p.step("tensor", lambda E, k=k: E.transpose(
                pTr1[0:64, 1024:1040], pooled[:, k * 64:(k + 1) * 64],
                idt[0:NPOOL, 0:NPOOL]),
                waits=[(s_act, t_pooled), (s_act, pTr_free)] +
                      ([(s_dve, kv_rope[15])] if k == 0 else []),
                incs=[(s_pe, 1)])
            for c in range(8, 16):
                p.step("tensor", lambda E, k=k, c=c: E.transpose(
                    pTr1[0:64, (c - 8) * 128:(c - 7) * 128],
                    KV[c][:, k * 64:(k + 1) * 64], idt[:]),
                    incs=[(s_pe, 1)])
            tt = pe_flush()
            p.step("scalar", lambda E, k=k: E.activation(
                KT[k][0:64, 0:16], pTr1[0:64, 1024:1040], AF.Copy,
                bias=0.0, scale=1.0),
                waits=[(s_pe, tt)], incs=[(s_act, 1)])
            pTr_free = p.step("scalar", lambda E, k=k: E.activation(
                KT[k][0:64, 16:1040], pTr1[0:64, 0:1024], AF.Copy,
                bias=0.0, scale=1.0),
                incs=[(s_act, 1)])
            kt_dup[k] = p.step("sync", lambda E, k=k: E.dma_start(
                out=KT[k][64:128, :], in_=KT[k][0:64, :]),
                waits=[(s_act, pTr_free)], incs=[(s_ktd, 16)])

        t1a_gate = [(s_dve, p.now(s_dve)), (s_pe, p.now(s_pe)),
                    (s_act, p.now(s_act))]
        ps1.close()
        sw1.close()

        sw2 = ExitStack()
        Wq_b = [sw2.enter_context(nc.sbuf_tensor(f"wqb{c}", [128, 1024], BF16))
                for c in range(16)]
        cosqs = [sw2.enter_context(nc.sbuf_tensor(f"cosq{i}", [128, 256], F32))
                 for i in range(2)]
        sinqs = [sw2.enter_context(nc.sbuf_tensor(f"sinq{i}", [128, 256], F32))
                 for i in range(2)]
        wstage2 = [sw2.enter_context(nc.sbuf_tensor(f"wst2{i}", [128, 1024], F32))
                  for i in range(2)]
        xstage2 = sw2.enter_context(nc.sbuf_tensor("xst2", [128, D], F32))
        xbst2 = sw2.enter_context(nc.sbuf_tensor("xbst2", [128, D], BF16))
        xtst2 = sw2.enter_context(nc.sbuf_tensor("xtst2", [128, D], BF16))
        Qb = sw2.enter_context(nc.sbuf_tensor("qb", [128, 1024], BF16))
        rtmpq1 = sw2.enter_context(nc.sbuf_tensor("rtmpa", [128, 256], BF16))
        rtmpq2 = sw2.enter_context(nc.sbuf_tensor("rtmpb", [128, 256], BF16))

        rope_t = [0, 0]

        conv_t = [0, 0]
        for c in range(16):
            par = c % 2
            tld = p.step("sync", lambda E, c=c, par=par: E.dma_start(
                out=wstage2[par][:], in_=dWq[c * 128:(c + 1) * 128, :]),
                waits=[(s_act, conv_t[par])] + (t1a_gate if c < 2 else []),
                incs=[(s_ldwb[par], 16)])
            conv_t[par] = p.step("scalar", lambda E, c=c, par=par: E.activation(
                Wq_b[c][:], wstage2[par][:], AF.Copy, bias=0.0, scale=1.0),
                waits=[(s_ldwb[par], tld)], incs=[(s_act, 1)])
        t_wq = p.now(s_act)

        ps2 = ExitStack()
        pQKV2 = [ps2.enter_context(nc.psum_tensor(f"pq2{i}", [128, 512], F32))
                for i in range(2)]
        pTr2x = ps2.enter_context(nc.psum_tensor("pTr2", [128, 2048], BF16))

        xst2_free = 0
        xb_free = 0
        xt_free = 0
        pTr_free = 0
        pq_free = [0, 0]
        for s in range(8):
            tld = p.step("sync", lambda E, s=s: E.dma_start(
                out=xstage2[:], in_=dxo[s * 128:(s + 1) * 128, :]),
                waits=[(s_act, xst2_free), (s_spx, 16 * s)], incs=[(s_ldxb, 16)])
            p.step("sync", lambda E, s=s: E.dma_start(
                out=x_res[:, s * D:(s + 1) * D], in_=xstage2[:]),
                waits=[(s_ldxb, tld)], incs=[(s_spx, 16)])
            tcv = p.step("scalar", lambda E: E.activation(
                xbst2[:], xstage2[:], AF.Copy, bias=0.0, scale=1.0),
                waits=[(s_ldxb, tld), (s_pe, xb_free)], incs=[(s_act, 1)])
            xst2_free = tcv
            for c2 in range(16):
                tt = p.step("tensor", lambda E, c2=c2: E.transpose(
                    pTr2x[:, c2 * 128:(c2 + 1) * 128], xbst2[:, c2 * 128:(c2 + 1) * 128],
                    idt[:]),
                    waits=([(s_act, tcv), (s_act, pTr_free)] if c2 == 0 else ()),
                    incs=[(s_pe, 1)])
            xb_free = tt
            tev = p.step("scalar", lambda E: E.activation(
                xtst2[:], pTr2x[:], AF.Copy, bias=0.0, scale=1.0),
                waits=[(s_pe, tt), (s_pe, xt_free)], incs=[(s_act, 1)])
            pTr_free = tev
            tq_ev = [0, 0]
            for g in range(2):
                for c2 in range(16):
                    w = ([(s_act, tev), (s_act, t_wq), (s_act, pq_free[g])]
                         if c2 == 0 else ())
                    tmm = p.step("tensor", lambda E, c2=c2, g=g: E.matmul(
                        pQKV2[g][:], xtst2[:, c2 * 128:(c2 + 1) * 128],
                        Wq_b[c2][:, g * 512:(g + 1) * 512],
                        start=(c2 == 0), stop=(c2 == 15)),
                        waits=w, incs=[(s_pe, 1)])
                if g == 1:
                    xt_free = tmm
                tq_ev[g] = p.step("scalar", lambda E, g=g: E.activation(
                    Qb[:, g * 512:(g + 1) * 512], pQKV2[g][:], AF.Copy,
                    bias=0.0, scale=1.0),
                    waits=[(s_pe, tmm)], incs=[(s_act, 1)])
                pq_free[g] = tq_ev[g]
            spar = s % 2
            tld_cq = p.step("sync", lambda E, s=s, spar=spar: E.dma_start(
                out=cosqs[spar][:], in_=dcosq[:, s * 256:(s + 1) * 256]),
                waits=[(s_dve, rope_t[spar])] + (t1a_gate if s < 2 else []),
                incs=[(s_ldc3[spar], 16)])
            tld_cq = p.step("sync", lambda E, s=s, spar=spar: E.dma_start(
                out=sinqs[spar][:], in_=dsinq[:, s * 256:(s + 1) * 256]),
                incs=[(s_ldc3[spar], 16)])
            qv = Qb[:].rearrange("p (h d) -> p h d", d=64)
            cq = cosqs[spar][:].rearrange("p (h d) -> p h d", d=16)
            sq = sinqs[spar][:].rearrange("p (h d) -> p h d", d=16)
            p.step("vector", lambda E, spar=spar: E.nop(),
                   waits=[(s_ldc3[spar], 32 * (s // 2 + 1))], incs=[(s_dve, 1)])
            t_rope = rope_steps(qv, cq, sq, tq_ev[1], rtmpq1, rtmpq2,
                                wait_sem=s_act)
            rope_t[spar] = t_rope
            for _ in range(5):
                next(qgen)
            for hp in range(8):
                tt = p.step("tensor", lambda E, hp=hp: E.transpose(
                    pTr2x[:, hp * 128:(hp + 1) * 128], Qb[:, hp * 128:(hp + 1) * 128],
                    idt[:]),
                    waits=([(s_dve, t_rope), (s_act, pTr_free)] if hp == 0 else ()),
                    incs=[(s_pe, 1)])
            pTr_free = p.step("scalar", lambda E, s=s: E.activation(
                QT[s][:], pTr2x[:, 0:1024], AF.Copy, bias=0.0, scale=1.0),
                waits=[(s_pe, tt)], incs=[(s_act, 1)])
        for _ in range(120):
            next(qgen)
        t_qt = p.now(s_act)
        t1b_gate = [(s_dve, p.now(s_dve)), (s_pe, p.now(s_pe)), (s_spx, 16 * 8),
                    (s_cag[0], 24 * 16), (s_cag[1], 24 * 16),
                    (s_act, p.now(s_act))]

        ps2.close()
        sw2.close()

        squant.close()
        st3 = ExitStack()
        Wo_b = [st3.enter_context(nc.sbuf_tensor(f"wob{c}", [128, 2048], BF16))
                for c in range(8)]
        wostage = st3.enter_context(nc.sbuf_tensor("wost", [128, 2048], F32))
        masks = st3.enter_context(nc.sbuf_tensor("masks", [128, MASK_COLS], BF16))
        scobufs = [st3.enter_context(nc.sbuf_tensor(f"scobuf{i}", [128, NKEY], F32))
                   for i in range(2)]
        probs2 = [st3.enter_context(nc.sbuf_tensor(f"probs{i}", [128, NKEY], BF16))
                  for i in range(2)]
        probsT = st3.enter_context(nc.sbuf_tensor("probsT", [128, 1152], BF16))
        AO = [st3.enter_context(nc.sbuf_tensor(f"ao{s}", [128, 1024], BF16))
              for s in range(8)]
        aoT = st3.enter_context(nc.sbuf_tensor("aoT", [128, 1024], BF16))

        p.step("gpsimd", lambda E: E.dma_start(out=masks[:], in_=dmask),
               waits=t1b_gate, incs=[(s_cmask, 16)], dma=True)
        last_conv = 0
        for c in range(8):
            tld = p.step("sync", lambda E, c=c: E.dma_start(
                out=wostage[:], in_=dWo[c * 128:(c + 1) * 128, :]),
                waits=[(s_act, last_conv)] + (t1b_gate if c == 0 else []),
                incs=[(s_lwo, 16)])
            last_conv = p.step("scalar", lambda E, c=c: E.activation(
                Wo_b[c][:], wostage[:], AF.Copy, bias=0.0, scale=1.0),
                waits=[(s_lwo, tld)], incs=[(s_act, 1)])
        t_wo = p.now(s_act)

        pbc_scope.close()
        ps3 = ExitStack()
        pSL2 = [ps3.enter_context(nc.psum_tensor(f"psl{i}", [128, 1024], F32))
                for i in range(2)]
        pMISC = ps3.enter_context(nc.psum_tensor("pmisc", [128, 512], F32))
        pT3 = ps3.enter_context(nc.psum_tensor("pT3", [128, 1024], BF16))
        pT3b = ps3.enter_context(nc.psum_tensor("pT3b", [128, 128], BF16))

        att = dict(psp_free=[0, 0], psl_free=[0, 0], sco_free=[0, 0],
                   probs_dve=[0, 0], pt3_free=0, ao_free=0)

        def att_scores(z):
            par, wp, wl, moff = z["par"], z["wp"], z["wl"], z["moff"]
            kvh, base, lhs = z["kvh"], z["base"], z["lhs"]
            scobuf = scobufs[par]
            w0 = [(s_act, t_qt), (s_dve, att["psp_free"][par]), (s_cmask, 16),
                  (s_dve, t_elsrep), (s_ktd, 64)]
            tsp = p.step("tensor", lambda E, lhs=lhs, kvh=kvh, wp=wp, base=base, par=par: E.matmul(
                pMISC[:, par * 16:par * 16 + wp], lhs,
                KT[kvh][base:base + 64, 0:wp],
                start=True, stop=True),
                waits=w0, incs=[(s_pe, 1)])
            tsl = None
            if wl > 0:
                nmm = (wl + 511) // 512
                for m in range(nmm):
                    cw = min(512, wl - m * 512)
                    w1 = [(s_dve, att["psl_free"][par])] if m == 0 else ()
                    tsl = p.step("tensor", lambda E, lhs=lhs, kvh=kvh, m=m, cw=cw, base=base, par=par: E.matmul(
                        pSL2[par][:, m * 512:m * 512 + cw], lhs,
                        KT[kvh][base:base + 64, 16 + m * 512:16 + m * 512 + cw],
                        start=True, stop=True),
                        waits=w1, incs=[(s_pe, 1)])
            att["psp_free"][par] = p.step("vector", lambda E, wp=wp, moff=moff, scobuf=scobuf, par=par: E.scalar_tensor_tensor(
                scobuf[:, 0:wp], pMISC[:, par * 16:par * 16 + wp], 30000.0,
                masks[:, moff:moff + wp], ALU.add, ALU.mult),
                waits=[(s_pe, tsp), (s_act, att["sco_free"][par])],
                incs=[(s_dve, 1)])
            if wl > 0:
                att["psl_free"][par] = p.step("vector", lambda E, wp=wp, wl=wl, moff=moff, scobuf=scobuf, par=par: E.scalar_tensor_tensor(
                    scobuf[:, wp:wp + wl], pSL2[par][:, 0:wl], 30000.0,
                    masks[:, moff + wp:moff + wp + wl], ALU.add, ALU.mult),
                    waits=[(s_pe, tsl)], incs=[(s_dve, 1)])

        def att_softmax(z):
            par, W, h = z["par"], z["W"], z["h"]
            scobuf = scobufs[par]
            probs = probs2[par]
            tnm = p.step("vector", lambda E, W=W, scobuf=scobuf: E.reduce_max(
                sc_a[:, 0:1], scobuf[:, 0:W], AX.X, negate=True), incs=[(s_dve, 1)])
            p.step("vector", lambda E, h=h, par=par: E.tensor_tensor(
                sc_b[:, par:par + 1], sc_a[:, 0:1], els_rep[:, h:h + 1], ALU.mult),
                waits=[(s_dve, tnm)], incs=[(s_dve, 1)])
            tbq = dve_flush(1)
            z["tex"] = p.step("scalar", lambda E, h=h, W=W, par=par, probs=probs, scobuf=scobuf: E.activation(
                probs[:, 0:W], scobuf[:, 0:W], AF.Exp,
                bias=sc_b[:, par:par + 1], scale=els_rep[:, h:h + 1]),
                waits=[(s_dve, tbq), (s_dve, att["probs_dve"][par])],
                incs=[(s_act, 1)])
            att["sco_free"][par] = z["tex"]

        def att_sumrecip(z):
            par, W = z["par"], z["W"]
            sl = (z["i"] % 4) * 2
            probs = probs2[par]
            att["probs_dve"][par] = p.step("vector", lambda E, W=W, probs=probs, sl=sl: E.reduce_sum(
                sc_c[:, sl:sl + 1], probs[:, 0:W], AX.X),
                waits=[(s_act, z["tex"])], incs=[(s_dve, 1)])
            p.step("vector", lambda E, sl=sl: E.reciprocal(
                sc_c[:, sl + 1:sl + 2], sc_c[:, sl:sl + 1]),
                incs=[(s_dve, 1)])

        def att_tail1(z):
            wp, wl, nl, par = z["wp"], z["wl"], z["nl"], z["par"]
            probs = probs2[par]
            p.step("tensor", lambda E, wp=wp, probs=probs: E.transpose(
                pT3b[0:wp, 0:128], probs[:, 0:wp], idt[:]),
                waits=[(s_act, z["tex"]), (s_act, att["pt3_free"])],
                incs=[(s_pe, 1)])
            for m in range(nl):
                p.step("tensor", lambda E, wp=wp, m=m, probs=probs: E.transpose(
                    pT3[:, m * 128:(m + 1) * 128],
                    probs[:, wp + m * 128:wp + (m + 1) * 128], idt[:]),
                    incs=[(s_pe, 1)])
            fl = pe_flush()
            if nl > 0:
                p.step("scalar", lambda E, nl=nl: E.activation(
                    probsT[:, 0:nl * 128], pT3[:, 0:nl * 128], AF.Copy,
                    bias=0.0, scale=1.0),
                    waits=[(s_pe, fl)], incs=[(s_act, 1)])
            z["evac"] = p.step("scalar", lambda E, wp=wp: E.activation(
                probsT[0:wp, 1024:1152], pT3b[0:wp, 0:128], AF.Copy,
                bias=0.0, scale=1.0),
                waits=[(s_pe, fl)], incs=[(s_act, 1)])
            att["pt3_free"] = z["evac"]

        def att_tail2(z):
            s_, h, wp, wl, nl, kvh = z["s"], z["h"], z["wp"], z["wl"], z["nl"], z["kvh"]
            w2 = [(s_act, z["evac"]), (s_dve, att["ao_free"])]
            p.step("tensor", lambda E, wp=wp, kvh=kvh, wl=wl: E.matmul(
                pMISC[:, 64:128], probsT[0:wp, 1024:1152],
                pooled[0:wp, 256 + kvh * 64:256 + (kvh + 1) * 64],
                start=True, stop=(wl == 0)),
                waits=w2, incs=[(s_pe, 1)])
            for m in range(nl):
                p.step("tensor", lambda E, m=m, kvh=kvh, nl=nl: E.matmul(
                    pMISC[:, 64:128], probsT[:, m * 128:(m + 1) * 128],
                    KV[8 + m][:, 256 + kvh * 64:256 + (kvh + 1) * 64],
                    start=False, stop=(m == nl - 1)),
                    incs=[(s_pe, 1)])
            fl = pe_flush()
            sl = (z["i"] % 4) * 2
            att["ao_free"] = p.step("vector", lambda E, s_=s_, h=h, sl=sl: E.tensor_scalar(
                AO[s_][:, h * 64:(h + 1) * 64], pMISC[:, 64:128],
                sc_c[:, sl + 1:sl + 2], None, ALU.mult),
                waits=[(s_pe, fl)], incs=[(s_dve, 1)])

        Z = []
        for s_ in range(8):
            W = SLOT_W[s_]
            wp = min(W, 16)
            wl = W - wp
            nl = wl // 128
            moff = SLOT_OFF[s_]
            for h in range(H):
                i = len(Z)
                z = dict(i=i, s=s_, h=h, W=W, wp=wp, wl=wl, nl=nl, moff=moff,
                         par=i % 2, kvh=h // 4, base=64 * (h % 2))
                z["lhs"] = QT[s_][z["base"]:z["base"] + 64,
                                  (h // 2) * 128:(h // 2 + 1) * 128]
                Z.append(z)
                att_scores(z)
                if i >= 3:
                    att_tail2(Z[i - 3])
                if i >= 2:
                    att_tail1(Z[i - 2])
                att_softmax(z)
                if i >= 1:
                    att_sumrecip(Z[i - 1])
        N = len(Z)
        att_sumrecip(Z[N - 1])
        att_tail2(Z[N - 3])
        att_tail1(Z[N - 2])
        att_tail2(Z[N - 2])
        att_tail1(Z[N - 1])
        att_tail2(Z[N - 1])
        t_attn_done = p.now(s_dve)
        if DEBUG_TAPS:
            for s_ in range(8):
                p.step("sync", lambda E, s_=s_: E.dma_start(
                    out=d_ao[s_ * 128:(s_ + 1) * 128, :], in_=AO[s_][:]),
                    waits=[(s_dve, t_attn_done)], incs=[(s_o[1], 16)])
                p.step("sync", lambda E, s_=s_: E.dma_start(
                    out=d_qt[:, s_ * 1024:(s_ + 1) * 1024], in_=QT[s_][:]),
                    waits=[(s_dve, t_attn_done)], incs=[(s_o[1], 16)])
            for k_ in range(HKV):
                p.step("sync", lambda E, k_=k_: E.dma_start(
                    out=d_kt[:, k_ * NKEY:(k_ + 1) * NKEY], in_=KT[k_][:]),
                    waits=[(s_dve, t_attn_done)], incs=[(s_o[1], 16)])
            p.step("sync", lambda E: E.dma_start(out=d_pool, in_=pooled[:]),
                   waits=[(s_dve, t_attn_done)], incs=[(s_o[1], 16)])
        ps3.close()

        ps3b = ExitStack()
        pWo = ps3b.enter_context(nc.psum_tensor("pWo", [128, 2048], F32))
        pTwo = ps3b.enter_context(nc.psum_tensor("pTwo", [128, 1024], BF16))

        ptwo_free = 0
        pwo_free = [0] * 4
        x1_done = [0] * 8
        s1_done = [0] * 8
        for s in range(8):
            for hp in range(8):
                tt = p.step("tensor", lambda E, s=s, hp=hp: E.transpose(
                    pTwo[:, hp * 128:(hp + 1) * 128], AO[s][:, hp * 128:(hp + 1) * 128],
                    idt[:]),
                    waits=([(s_dve, t_attn_done), (s_act, ptwo_free)] if hp == 0 else ()),
                    incs=[(s_pe, 1)])
            ptwo_free = p.step("scalar", lambda E: E.activation(
                aoT[:], pTwo[:], AF.Copy, bias=0.0, scale=1.0),
                waits=[(s_pe, tt)], incs=[(s_act, 1)])
            for dmq in range(4):
                for c in range(8):
                    w = ([(s_act, ptwo_free), (s_dve, pwo_free[dmq]), (s_act, t_wo)]
                         if c == 0 else ())
                    tmm = p.step("tensor", lambda E, dmq=dmq, c=c: E.matmul(
                        pWo[:, dmq * 512:(dmq + 1) * 512], aoT[:, c * 128:(c + 1) * 128],
                        Wo_b[c][:, dmq * 512:(dmq + 1) * 512],
                        start=(c == 0), stop=(c == 7)),
                        waits=w, incs=[(s_pe, 1)])
                pwo_free[dmq] = p.step("vector", lambda E, s=s, dmq=dmq: E.scalar_tensor_tensor(
                    x_res[:, s * D + dmq * 512:s * D + (dmq + 1) * 512],
                    pWo[:, dmq * 512:(dmq + 1) * 512], 1.0,
                    x_res[:, s * D + dmq * 512:s * D + (dmq + 1) * 512],
                    ALU.mult, ALU.add),
                    waits=[(s_pe, tmm), (s_spx, 16 * 8)], incs=[(s_dve, 1)])
            xs = x_res[:, s * D:(s + 1) * D]
            for q4 in range(4):
                p.step("vector", lambda E, xs=xs, q4=q4: E.bn_stats(
                    bnln[:, q4 * 6:(q4 + 1) * 6], xs[:, q4 * 512:(q4 + 1) * 512]),
                    waits=[(s_dve, pwo_free[3])], incs=[(s_dve, 1)])
            p.step("vector", lambda E: E.bn_aggr(bnag[:], bnln[:, 0:24]),
                   incs=[(s_dve, 1)])
            p.step("vector", lambda E: E.tensor_copy(sc_a[:, 1:2], bnag[:, 0:1]),
                   incs=[(s_dve, 1)])
            p.step("vector", lambda E: E.tensor_scalar(
                sc_a[:, 4:5], bnag[:, 1:2], 1e-5, None, ALU.add), incs=[(s_dve, 1)])
            t_var = p.now(s_dve)
            p.step("scalar", lambda E: E.activation(
                sc_a[:, 5:6], sc_a[:, 4:5], AF.Ln, bias=0.0, scale=1.0),
                waits=[(s_dve, t_var)], incs=[(s_act, 1)])
            t_rst = p.step("scalar", lambda E: E.activation(
                sc_a[:, 6:7], sc_a[:, 5:6], AF.Exp, bias=0.0, scale=-0.5),
                incs=[(s_act, 1)])
            x1_done[s] = p.step("vector", lambda E, xs=xs: E.tensor_scalar(
                xs, xs, sc_a[:, 1:2], sc_a[:, 6:7], ALU.subtract, ALU.mult),
                waits=[(s_act, t_rst)], incs=[(s_dve, 1)])
            p.step("vector", lambda E, xs=xs: E.reduce_max(
                sc_a[:, 7:8], xs, AX.X, apply_absolute_value=True),
                waits=[(s_dve, x1_done[s])], incs=[(s_dve, 1)])
            p.step("vector", lambda E: E.tensor_scalar(
                sc_a[:, 7:8], sc_a[:, 7:8], 1e-5, None, ALU.max), incs=[(s_dve, 1)])
            p.step("vector", lambda E: E.reciprocal(sc_b[:, 0:1], sc_a[:, 7:8]),
                   incs=[(s_dve, 1)])
            p.step("vector", lambda E, s=s: E.tensor_scalar(
                st_s1[:, s:s + 1], sc_b[:, 0:1], 127.0, None, ALU.mult),
                incs=[(s_dve, 1)])
            s1_done[s] = dve_flush(2)
        ps3b.close()
        st3.close()

        st4 = ExitStack()
        t4f = st4.enter_context(nc.sbuf_tensor("t4f", [128, D], F32))
        t4i = st4.enter_context(nc.sbuf_tensor("t4i", [128, D], I32))
        t4b = st4.enter_context(nc.sbuf_tensor("t4b", [128, D], BF16))
        t4t = st4.enter_context(nc.sbuf_tensor("t4t", [128, D], BF16))
        ps4 = ExitStack()
        pT4 = ps4.enter_context(nc.psum_tensor("pT4", [128, 2048], BF16))

        t4t_free = 0
        for s in range(8):
            xs = x_res[:, s * D:(s + 1) * D]
            p.step("sync", lambda E, s=s, xs=xs: E.dma_start(
                out=x1_spill.ap()[s * 128:(s + 1) * 128, :], in_=xs),
                waits=[(s_dve, x1_done[s])], incs=[(s_sp1, 16)])
            if DEBUG_TAPS:
                p.step("sync", lambda E, s=s, xs=xs: E.dma_start(
                    out=d_x1[s * 128:(s + 1) * 128, :], in_=xs),
                    waits=[(s_dve, x1_done[s])], incs=[(s_o[0], 16)])
            tma = p.step("scalar", lambda E, s=s, xs=xs: E.activation(
                t4f[:], xs, AF.Copy, bias=MAGIC, scale=st_s1[:, s:s + 1]),
                waits=[(s_dve, s1_done[s]), (s_dve, t4t_free)], incs=[(s_act, 1)])
            p.step("vector", lambda E: E.tensor_scalar(
                t4i[:], t4f[:], -MAGIC, None, ALU.add),
                waits=[(s_act, tma)], incs=[(s_dve, 1)])
            tqb = p.step("vector", lambda E: E.tensor_copy(t4b[:], t4i[:]),
                         incs=[(s_dve, 1)])
            for c2 in range(16):
                tt = p.step("tensor", lambda E, c2=c2: E.transpose(
                    pT4[:, c2 * 128:(c2 + 1) * 128], t4b[:, c2 * 128:(c2 + 1) * 128],
                    idt[:]),
                    waits=([(s_dve, tqb)] if c2 == 0 else ()), incs=[(s_pe, 1)])
            t4t_free = p.step("vector", lambda E: E.tensor_copy(t4t[:], pT4[:]),
                              waits=[(s_pe, tt), (s_spq, p.now(s_spq))],
                              incs=[(s_dve, 1)])
            p.step("sync", lambda E, s=s: E.dma_start(
                out=xqt_spill.ap()[:, s * 2048:(s + 1) * 2048], in_=t4t[:]),
                waits=[(s_dve, t4t_free)], incs=[(s_spq, 16)])
        gate_w = [(s_sp1, p.now(s_sp1)), (s_spq, p.now(s_spq)),
                  (s_dve, p.now(s_dve)), (s_act, p.now(s_act)),
                  (s_pe, p.now(s_pe)), (s_pool, p.now(s_pool))]
        if DEBUG_TAPS:
            gate_w = gate_w + [(s_o[i], p.now(s_o[i])) for i in range(4)]
        ps4.close()
        st4.close()
        skv.close()
        sx.close()

        s2 = ExitStack()
        SLAB = [s2.enter_context(nc.sbuf_tensor(f"slab{hf}", [128, 4 * FF], BF16))
                for hf in range(2)]
        g_rep = s2.enter_context(nc.sbuf_tensor("g_rep", [128, FF], BF16))
        sgbuf = s2.enter_context(nc.sbuf_tensor("sgbuf", [128, 512], BF16))
        aqi = s2.enter_context(nc.sbuf_tensor("aqi", [128, 512], I32))
        aqbs = [s2.enter_context(nc.sbuf_tensor(f"aqb{i}", [128, 512], BF16))
                for i in range(2)]
        ps2c = ExitStack()
        pbc2 = ps2c.enter_context(nc.psum_tensor("pbc2", [128, 512], F32))
        sgr = ExitStack()
        gsub_sb = sgr.enter_context(nc.sbuf_tensor("gsub_sb", [1, FF], F32))
        p.step("sync", lambda E: E.dma_start(out=gsub_sb[:], in_=dgsub),
               waits=gate_w, incs=[(s_ldc4, 16)])
        for gblk in range(16):
            tt = p.step("tensor", lambda E, gblk=gblk: E.matmul(
                pbc2[:], ones_r[:], gsub_sb[:, gblk * 512:(gblk + 1) * 512],
                start=True, stop=True),
                waits=[(s_ldc4, 16)] + ([(s_dve, 0)] if gblk == 0 else
                                        [(s_dve, p.now(s_dve))]),
                incs=[(s_pe, 1)])
            tt = pe_flush()
            p.step("vector", lambda E, gblk=gblk: E.tensor_copy(
                g_rep[:, gblk * 512:(gblk + 1) * 512], pbc2[:]),
                waits=[(s_pe, tt)], incs=[(s_dve, 1)])
        t_grep = p.now(s_dve)
        g_gate = [(s_pe, p.now(s_pe)), (s_dve, t_grep)]
        sgr.close()

        s2a = ExitStack()
        XQT = s2a.enter_context(nc.sbuf_tensor("XQT", [128, 4 * 2048], BF16))
        gringM = s2a.enter_context(nc.sbuf_tensor("gringM", [128, 16 * 512], BF16))
        uringM = s2a.enter_context(nc.sbuf_tensor("uringM", [128, 16 * 512], BF16))
        rgbuf = s2a.enter_context(nc.sbuf_tensor("rgbuf", [128, 512], F32))
        rg2buf = s2a.enter_context(nc.sbuf_tensor("rg2buf", [128, 512], F32))

        ps2f = ExitStack()
        pTaq = ps2f.enter_context(nc.psum_tensor("pTaq", [128, 1024], BF16))
        pQ2 = ps2f.enter_context(nc.psum_tensor("pQ2", [128, 512], F32))
        psgu = ExitStack()
        pG = [psgu.enter_context(nc.psum_tensor(f"pG{i}", [128, 512], F32)) for i in range(2)]
        pU = [psgu.enter_context(nc.psum_tensor(f"pU{i}", [128, 512], F32)) for i in range(2)]

        gring_free = [0, 0]
        uring_free = [0, 0]
        pg_free = [0, 0]
        pu_free = [0, 0]
        xqt_free = 0
        prod_end = {}
        rowS_done = [0] * 8
        drow_done = [0] * 8
        aq_last_write = [0, 0]
        slab_prod = [[0] * 64 for _ in range(2)]
        rg_free = 0
        rg2_free = 0
        sg_free = 0
        tg_free = 0
        aqi_free = 0
        aqb_free = 0
        pq2_free = 0
        ptaq_free = 0

        qg_view = qgT.ap().rearrange("(r fb c p) n -> p fb r c n", r=8, fb=32, p=128)
        qu_view = quT.ap().rearrange("(r fb c p) n -> p fb r c n", r=8, fb=32, p=128)
        qd_view = qdT.ap().rearrange("(r d fl p) n -> r d fl p n", r=8, d=4, p=128)

        gring_free = 0
        uring_free = 0
        aq_state = dict(sg_free=0, pq_free=[0, 0], ptaq_free=[0, 0],
                        half_free=[0, 0])

        def aq_gen(hf):
            st = aq_state

            def stage_a(j):
                t, grp = j // 16, j % 16
                par = j % 2
                sg_i = hf * 4 + t
                slab_sl = SLAB[hf][:, t * FF + grp * 512:t * FF + (grp + 1) * 512]
                grs = g_rep[:, grp * 512:(grp + 1) * 512]
                pq = pQ2 if par == 0 else pbc2
                t_a1 = p.step("vector", lambda E, slab_sl=slab_sl, grs=grs: E.tensor_tensor(
                    sgbuf[:], slab_sl, grs, ALU.mult),
                    waits=[(s_dve, rowS_done[sg_i]), (s_act, st["sg_free"])] +
                          ([(s_o[2], tap_slab)] if tap_slab and hf == 0 else []),
                    incs=[(s_dve, 1)])
                t_a2 = p.step("scalar", lambda E, sg_i=sg_i, pq=pq: E.activation(
                    pq[:], sgbuf[:], AF.Copy, bias=MAGIC,
                    scale=st_rowS[:, sg_i:sg_i + 1]),
                    waits=[(s_dve, t_a1), (s_dve, st["pq_free"][par])],
                    incs=[(s_act, 1)])
                st["sg_free"] = t_a2
                t_a3 = p.step("vector", lambda E, pq=pq: E.tensor_scalar(
                    aqi[:], pq[:], -MAGIC, None, ALU.add),
                    waits=[(s_act, t_a2)], incs=[(s_dve, 1)])
                st["pq_free"][par] = t_a3
                t_a4 = p.step("vector", lambda E, par=par: E.tensor_copy(
                    aqbs[par][:], aqi[:]),
                    waits=[(s_pe, st["ptaq_free"][par])], incs=[(s_dve, 1)])
                return t_a4

            def stage_b(j, t_a4):
                t, grp = j // 16, j % 16
                par = j % 2
                slab_sl = SLAB[hf][:, t * FF + grp * 512:t * FF + (grp + 1) * 512]
                for c4 in range(4):
                    p.step("tensor", lambda E, c4=c4, par=par: E.transpose(
                        pTaq[:, par * 512 + c4 * 128:par * 512 + (c4 + 1) * 128],
                        aqbs[par][:, c4 * 128:(c4 + 1) * 128], idt[:]),
                        waits=([(s_dve, t_a4), (s_act, st["half_free"][par])]
                               if c4 == 0 else ()),
                        incs=[(s_pe, 1)])
                tt = pe_flush()
                st["ptaq_free"][par] = tt
                aq_last_write[hf] = p.step("scalar", lambda E, slab_sl=slab_sl, par=par: E.activation(
                    slab_sl, pTaq[:, par * 512:(par + 1) * 512], AF.Copy,
                    bias=0.0, scale=1.0),
                    waits=[(s_pe, tt)], incs=[(s_act, 1)])
                st["half_free"][par] = aq_last_write[hf]

            prev = None
            for j in range(64):
                if prev is not None:
                    stage_b(*prev)
                prev = (j, stage_a(j))
                yield
            stage_b(*prev)
            while True:
                yield

        def emit_gup(hf, driver=None):
            if DEBUG_TAPS and hf == 0:
                for r_ in range(8):
                    p.step("sync", lambda E, r_=r_: E.dma_start(
                        out=d_qg[r_ * 128:(r_ + 1) * 128, :],
                        in_=qgT.ap()[r_ * 8192:r_ * 8192 + 128, :]),
                        waits=[(s_ccg, 1), (s_ccf, 1)], incs=[(s_o[3], 16)])
                    p.step("sync", lambda E, r_=r_: E.dma_start(
                        out=d_qd[r_ * 128:(r_ + 1) * 128, :],
                        in_=qdT.ap()[r_ * 4096:r_ * 4096 + 128, :]),
                        waits=[(s_ccd, 1), (s_ccf, 1)], incs=[(s_o[3], 16)])
            xw = (gate_w + g_gate + [(s_pe, emit_gup.xqt_free)]) if hf == 0 \
                else [(s_pe, emit_gup.xqt_free)]
            t_xqt = p.step("sync", lambda E, hf=hf: E.dma_start(
                out=XQT[:], in_=xqt_spill.ap()[:, hf * 4 * 2048:(hf + 1) * 4 * 2048]),
                waits=xw, incs=[(s_xqt, 16)])
            for fbp in range(16):
                wg = [(s_pe, emit_gup.gring_free)]
                if hf == 0 and fbp == 0:
                    wg = wg + gate_w + g_gate + [(s_ccg, 1), (s_ccf, 1)]
                ld_g = 0
                for sub in range(2):
                    fb = fbp * 2 + sub
                    for clv in range(2):
                        ld_g = p.step("sync", lambda E, fb=fb, sub=sub, clv=clv: E.dma_start(
                            out=gringM[:].rearrange("p (r cl sx n) -> p r cl sx n",
                                                    r=8, cl=2, sx=2)[:, :, clv, sub, :],
                            in_=qg_view[:, fb, :, clv, :]),
                            waits=(wg if sub == 0 and clv == 0 else ()),
                            incs=[(s_lg[0], 16)])
                wu = [(s_pe, emit_gup.uring_free)]
                if hf == 0 and fbp == 0:
                    wu = wu + g_gate + [(s_ccu, 1), (s_ccf, 1)]
                ld_u = 0
                for sub in range(2):
                    fb = fbp * 2 + sub
                    for clv in range(2):
                        ld_u = p.step("sync", lambda E, fb=fb, sub=sub, clv=clv: E.dma_start(
                            out=uringM[:].rearrange("p (r cl sx n) -> p r cl sx n",
                                                    r=8, cl=2, sx=2)[:, :, clv, sub, :],
                            in_=qu_view[:, fb, :, clv, :]),
                            waits=(wu if sub == 0 and clv == 0 else ()),
                            incs=[(s_lu[0], 16)])
                for t in range(4):
                    tpar = t % 2
                    sg_i = hf * 4 + t
                    for c2 in range(16):
                        w = []
                        if c2 == 0:
                            w = [(s_lg[0], ld_g), (s_act, pg_free[tpar]),
                                 (s_xqt, t_xqt)]
                        tmm = p.step("tensor", lambda E, c2=c2, t=t, tpar=tpar: E.matmul(
                            pG[tpar][:],
                            XQT[:, t * 2048 + c2 * 128:t * 2048 + (c2 + 1) * 128],
                            gringM[:, c2 * 512:(c2 + 1) * 512],
                            start=(c2 == 0), stop=(c2 == 15)),
                            waits=w, incs=[(s_pe, 1)])
                    emit_gup.gring_free = tmm
                    for c2 in range(16):
                        w = []
                        if c2 == 0:
                            w = [(s_lu[0], ld_u), (s_dve, pu_free[tpar])]
                        tmu = p.step("tensor", lambda E, c2=c2, t=t, tpar=tpar: E.matmul(
                            pU[tpar][:],
                            XQT[:, t * 2048 + c2 * 128:t * 2048 + (c2 + 1) * 128],
                            uringM[:, c2 * 512:(c2 + 1) * 512],
                            start=(c2 == 0), stop=(c2 == 15)),
                            waits=w, incs=[(s_pe, 1)])
                    emit_gup.uring_free = tmu
                    emit_gup.xqt_free = tmu
                    slab_sl = SLAB[hf][:, t * FF + fbp * 512:t * FF + (fbp + 1) * 512]
                    grs = g_rep[:, fbp * 512:(fbp + 1) * 512]
                    pg_free[tpar] = p.step("scalar", lambda E, tpar=tpar: E.activation(
                        rgbuf[:], pG[tpar][:], AF.Relu, bias=0.0, scale=1.0),
                        waits=[(s_pe, tmm), (s_dve, emit_gup.rg_free)], incs=[(s_act, 1)])
                    t_p2 = p.step("scalar", lambda E: E.activation(
                        rg2buf[:], rgbuf[:], AF.Square, bias=0.0, scale=1.0),
                        waits=[(s_dve, emit_gup.rg2_free)], incs=[(s_act, 1)])
                    t_p3 = p.step("vector", lambda E, slab_sl=slab_sl, tpar=tpar: E.scalar_tensor_tensor(
                        slab_sl, pU[tpar][:], 1.0, rg2buf[:], ALU.mult, ALU.mult),
                        waits=[(s_pe, tmu), (s_act, t_p2)], incs=[(s_dve, 1)])
                    pu_free[tpar] = t_p3
                    emit_gup.rg2_free = t_p3
                    p.step("vector", lambda E, slab_sl=slab_sl, sg_i=sg_i, fbp=fbp: E.bn_stats(
                        gup_bn[:, (sg_i * 16 + fbp) * 6:(sg_i * 16 + fbp + 1) * 6],
                        slab_sl),
                        waits=[(s_dve, t_p3)], incs=[(s_dve, 1)])
                    t_p5 = p.step("vector", lambda E, slab_sl=slab_sl, grs=grs: E.tensor_tensor(
                        rgbuf[:], slab_sl, grs, ALU.mult),
                        waits=[(s_dve, t_p3), (s_dve, t_grep)],
                        incs=[(s_dve, 1)])
                    emit_gup.rg_free = p.step("vector", lambda E, sg_i=sg_i, fbp=fbp: E.reduce_max(
                        gup_mxg[:, sg_i * 16 + fbp:sg_i * 16 + fbp + 1], rgbuf[:],
                        AX.X, apply_absolute_value=True),
                        waits=[(s_dve, t_p5)], incs=[(s_dve, 1)])
                    if driver is not None:
                        next(driver)
            prod_end[hf] = dict(dve=p.now(s_dve), act=p.now(s_act),
                                pool=p.now(s_pool), pe=p.now(s_pe))
            tap = 0
            if DEBUG_TAPS and hf == 0:
                tap = p.step("sync", lambda E: E.dma_start(out=d_slab, in_=SLAB[0][:]),
                             waits=[(s_dve, prod_end[0]["dve"]),
                                    (s_pool, prod_end[0]["pool"]),
                                    (s_act, prod_end[0]["act"])],
                             incs=[(s_o[2], 16)])
            return tap

        emit_gup.gring_free = 0
        emit_gup.uring_free = 0
        emit_gup.xqt_free = 0
        emit_gup.rg_free = 0
        emit_gup.rg2_free = 0
        emit_gup.tg_free = 0

        def emit_stats(hf):
                for t in range(4):
                    sg_i = hf * 4 + t
                    w0 = [(s_act, prod_end[hf]["act"]), (s_pool, prod_end[hf]["pool"]),
                          (s_dve, prod_end[hf]["dve"])]
                    cb = sg_i * 12
                    p.step("vector", lambda E, sg_i=sg_i: E.bn_aggr(
                        bnag[:], gup_bn[:, sg_i * 96:(sg_i + 1) * 96]),
                        waits=w0, incs=[(s_dve, 1)])
                    p.step("vector", lambda E, cb=cb: E.tensor_tensor(
                        st_st[:, cb + 0:cb + 1], bnag[:, 0:1], bnag[:, 0:1], ALU.mult),
                        incs=[(s_dve, 1)])
                    p.step("vector", lambda E, cb=cb: E.tensor_tensor(
                        st_st[:, cb + 0:cb + 1], st_st[:, cb + 0:cb + 1], bnag[:, 1:2],
                        ALU.add), incs=[(s_dve, 1)])
                    p.step("vector", lambda E, sg_i=sg_i, cb=cb: E.reduce_max(
                        st_st[:, cb + 1:cb + 2], gup_mxg[:, sg_i * 16:(sg_i + 1) * 16],
                        AX.X), waits=w0, incs=[(s_dve, 1)])
                    p.step("vector", lambda E, sg_i=sg_i, cb=cb: E.reciprocal(
                        st_st[:, cb + 2:cb + 3], st_s1[:, sg_i:sg_i + 1]), incs=[(s_dve, 1)])
                    p.step("vector", lambda E, cb=cb: E.tensor_tensor(
                        st_st[:, cb + 3:cb + 4], st_st[:, cb + 2:cb + 3],
                        st_st[:, cb + 2:cb + 3], ALU.mult), incs=[(s_dve, 1)])
                    p.step("vector", lambda E, cb=cb: E.tensor_tensor(
                        st_st[:, cb + 3:cb + 4], st_st[:, cb + 3:cb + 4],
                        st_st[:, cb + 2:cb + 3], ALU.mult), incs=[(s_dve, 1)])
                    p.step("vector", lambda E, cb=cb: E.tensor_tensor(
                        sc_a[:, 4:5], mw_rep[:, 0:1], mw_rep[:, 0:1], ALU.mult),
                        incs=[(s_dve, 1)])
                    p.step("vector", lambda E, cb=cb: E.tensor_tensor(
                        sc_a[:, 4:5], sc_a[:, 4:5], mw_rep[:, 1:2], ALU.mult),
                        incs=[(s_dve, 1)])
                    p.step("vector", lambda E, cb=cb: E.tensor_tensor(
                        st_st[:, cb + 3:cb + 4], st_st[:, cb + 3:cb + 4], sc_a[:, 4:5],
                        ALU.mult), incs=[(s_dve, 1)])
                    p.step("vector", lambda E, cb=cb: E.tensor_tensor(
                        st_st[:, cb + 4:cb + 5], st_st[:, cb + 3:cb + 4],
                        st_st[:, cb + 3:cb + 4], ALU.mult), incs=[(s_dve, 1)])
                    p.step("vector", lambda E, cb=cb: E.tensor_tensor(
                        st_st[:, cb + 4:cb + 5], st_st[:, cb + 4:cb + 5],
                        st_st[:, cb + 0:cb + 1], ALU.mult), incs=[(s_dve, 1)])
                    p.step("vector", lambda E, cb=cb: E.tensor_scalar(
                        st_st[:, cb + 4:cb + 5], st_st[:, cb + 4:cb + 5], 1e-6, None,
                        ALU.add), incs=[(s_dve, 1)])
                    t_v = dve_flush(2)
                    p.step("scalar", lambda E, cb=cb: E.activation(
                        st_st[:, cb + 5:cb + 6], st_st[:, cb + 4:cb + 5], AF.Ln,
                        bias=0.0, scale=1.0),
                        waits=[(s_dve, t_v)], incs=[(s_act, 1)])
                    p.step("scalar", lambda E, cb=cb: E.activation(
                        st_st[:, cb + 6:cb + 7], st_st[:, cb + 5:cb + 6], AF.Exp,
                        bias=0.0, scale=-0.5), incs=[(s_act, 1)])
                    t_r = act_flush(1)
                    p.step("vector", lambda E, cb=cb: E.tensor_tensor(
                        st_st[:, cb + 7:cb + 8], st_st[:, cb + 1:cb + 2],
                        st_st[:, cb + 3:cb + 4], ALU.mult),
                        waits=[(s_act, t_r)], incs=[(s_dve, 1)])
                    p.step("vector", lambda E, cb=cb: E.tensor_tensor(
                        st_st[:, cb + 7:cb + 8], st_st[:, cb + 7:cb + 8],
                        st_st[:, cb + 6:cb + 7], ALU.mult), incs=[(s_dve, 1)])
                    p.step("vector", lambda E, cb=cb: E.tensor_scalar(
                        st_st[:, cb + 7:cb + 8], st_st[:, cb + 7:cb + 8], 1e-5, None,
                        ALU.max), incs=[(s_dve, 1)])
                    p.step("vector", lambda E, cb=cb: E.reciprocal(
                        st_st[:, cb + 8:cb + 9], st_st[:, cb + 7:cb + 8]),
                        incs=[(s_dve, 1)])
                    p.step("vector", lambda E, cb=cb: E.tensor_scalar(
                        st_st[:, cb + 9:cb + 10], st_st[:, cb + 8:cb + 9], 127.0, None,
                        ALU.mult), incs=[(s_dve, 1)])
                    p.step("vector", lambda E, cb=cb: E.tensor_tensor(
                        st_st[:, cb + 10:cb + 11], st_st[:, cb + 3:cb + 4],
                        st_st[:, cb + 6:cb + 7], ALU.mult), incs=[(s_dve, 1)])
                    p.step("vector", lambda E, sg_i=sg_i, cb=cb: E.tensor_tensor(
                        st_rowS[:, sg_i:sg_i + 1], st_st[:, cb + 10:cb + 11],
                        st_st[:, cb + 9:cb + 10], ALU.mult), incs=[(s_dve, 1)])
                    rowS_done[sg_i] = dve_flush(2)
                    p.step("vector", lambda E, cb=cb: E.tensor_tensor(
                        st_st[:, cb + 11:cb + 12], st_st[:, cb + 7:cb + 8],
                        mw_rep[:, 2:3], ALU.mult), incs=[(s_dve, 1)])
                    p.step("vector", lambda E, sg_i=sg_i, cb=cb: E.tensor_scalar(
                        st_drow[:, sg_i:sg_i + 1], st_st[:, cb + 11:cb + 12], 1.0 / 127.0,
                        None, ALU.mult), incs=[(s_dve, 1)])
                    drow_done[sg_i] = dve_flush(2)


        tap_slab = emit_gup(0)
        emit_stats(0)
        aq0 = aq_gen(0)
        emit_gup(1, driver=aq0)
        for _ in range(8):
            next(aq0)
        emit_stats(1)
        aq1 = aq_gen(1)
        if DEBUG_TAPS:
            p.step("sync", lambda E: E.dma_start(out=d_aq, in_=SLAB[0][:]),
                   waits=[(s_act, aq_last_write[0])], incs=[(s_o[2], 16)])
            p.step("sync", lambda E: E.dma_start(out=d_rs[:, 0:8], in_=st_rowS[:]),
                   waits=[(s_dve, rowS_done[7])], incs=[(s_o[2], 16)])
            p.step("sync", lambda E: E.dma_start(out=d_rs[:, 8:16], in_=st_drow[:]),
                   waits=[(s_dve, drow_done[7])], incs=[(s_o[2], 16)])
            p.step("sync", lambda E: E.dma_start(out=d_gup[:, 0:128], in_=gup_bn[:, 0:128]),
                   waits=[(s_dve, drow_done[7])], incs=[(s_o[2], 16)])
            p.step("sync", lambda E: E.dma_start(out=d_st[:, 0:96], in_=st_st[:]),
                   waits=[(s_dve, drow_done[7])], incs=[(s_o[2], 16)])
            p.step("sync", lambda E: E.dma_start(out=d_mw[:, 0:4], in_=mw_rep[:]),
                   waits=[(s_dve, drow_done[7])], incs=[(s_o[2], 16)])
            p.step("sync", lambda E: E.dma_start(out=d_mw[:, 4:8], in_=sw_rep[:]),
                   waits=[(s_dve, drow_done[7])], incs=[(s_o[2], 16)])
            p.step("sync", lambda E: E.dma_start(out=d_gup[:, 128:256], in_=gup_mxg[:]),
                   waits=[(s_dve, drow_done[7])], incs=[(s_o[2], 16)])

        s2a.close()
        psgu.close()
        s2b = ExitStack()
        y2 = s2b.enter_context(nc.sbuf_tensor("y2", [128, 4 * D], F32))
        wd_ring = [s2b.enter_context(nc.sbuf_tensor(f"wdr{i}", [128, 512], BF16))
                   for i in range(4)]
        x1st = [s2b.enter_context(nc.sbuf_tensor(f"x1st{i}", [128, 512], F32))
                for i in range(2)]
        psd = ExitStack()
        pD = psd.enter_context(nc.psum_tensor("pD", [128, 2048], F32))

        region_gate = [(s_pe, prod_end[1]["pe"]), (s_dve, prod_end[1]["dve"]),
                       (s_pool, prod_end[1]["pool"]), (s_act, prod_end[1]["act"])]
        wdr_free = [0, 0, 0, 0]
        pd_free = [0] * 4
        x1st_free = [0, 0]
        y2_free = [0] * 4
        for hf in range(2):
            if hf == 1:
                for _ in range(80):
                    next(aq1)
            for ds4 in range(4):
                for fc in range(64):
                    r, floc = fc // 8, fc % 8
                    ring = fc % 4
                    extra = []
                    if hf == 0 and ds4 == 0 and fc < 4:
                        extra = [(s_ccd, 1), (s_ccf, 1)] + region_gate
                    tldw = p.step("sync", lambda E, r=r, ds4=ds4, floc=floc, ring=ring: E.dma_start(
                        out=wd_ring[ring][:], in_=qd_view[r, ds4, floc, :, :]),
                        waits=[(s_pe, wdr_free[ring])] + extra,
                        incs=[(s_lwd[ring], 16)])
                    for t in range(4):
                        w = []
                        if t == 0:
                            w = [(s_lwd[ring], tldw)]
                            if fc == 0:
                                w += [(s_act, aq_last_write[hf])] + \
                                     [(s_dve, pd_free[q]) for q in range(4)]
                        tmm = p.step("tensor", lambda E, t=t, fc=fc, ring=ring, hf=hf: E.matmul(
                            pD[:, t * 512:(t + 1) * 512],
                            SLAB[hf][:, t * FF + fc * 128:t * FF + (fc + 1) * 128],
                            wd_ring[ring][:],
                            start=(fc == 0), stop=(fc == 63)),
                            waits=w, incs=[(s_pe, 1)])
                    wdr_free[ring] = tmm
                    if hf == 0 and fc % 3 == 2:
                        next(aq1)
                for t in range(4):
                    sg_i = hf * 4 + t
                    par = t % 2
                    tldx = p.step("sync", lambda E, sg_i=sg_i, ds4=ds4, par=par: E.dma_start(
                        out=x1st[par][:],
                        in_=x1_spill.ap()[sg_i * 128:(sg_i + 1) * 128,
                                          ds4 * 512:(ds4 + 1) * 512]),
                        waits=[(s_dve, x1st_free[par]), (s_sp1, 16 * 8)] +
                              (region_gate if hf == 0 and ds4 == 0 and t < 2 else []),
                        incs=[(s_lx[par], 16)])
                    pd_free[t] = p.step("vector", lambda E, t=t, ds4=ds4, sg_i=sg_i, par=par: E.scalar_tensor_tensor(
                        y2[:, t * D + ds4 * 512:t * D + (ds4 + 1) * 512],
                        pD[:, t * 512:(t + 1) * 512], st_drow[:, sg_i:sg_i + 1],
                        x1st[par][:], ALU.mult, ALU.add),
                        waits=[(s_pe, tmm), (s_lx[par], tldx), (s_dve, drow_done[sg_i]),
                               (s_o[t], y2_free[t])],
                        incs=[(s_dve, 1)])
                    x1st_free[par] = pd_free[t]

            tap_y2 = 0
            if DEBUG_TAPS and hf == 0:
                tap_y2 = p.step("sync", lambda E: E.dma_start(out=d_y2, in_=y2[:]),
                                waits=[(s_dve, pd_free[0]), (s_dve, pd_free[1]),
                                       (s_dve, pd_free[2]), (s_dve, pd_free[3])],
                                incs=[(s_o[2], 16)])
            for t in range(4):
                sg_i = hf * 4 + t
                ys = y2[:, t * D:(t + 1) * D]
                for q4 in range(4):
                    p.step("vector", lambda E, ys=ys, q4=q4: E.bn_stats(
                        bnln[:, q4 * 6:(q4 + 1) * 6], ys[:, q4 * 512:(q4 + 1) * 512]),
                        waits=[(s_dve, pd_free[t])] +
                              ([(s_o[2], tap_y2)] if tap_y2 and q4 == 0 else []),
                        incs=[(s_dve, 1)])
                p.step("vector", lambda E: E.bn_aggr(bnag[:], bnln[:, 0:24]),
                       incs=[(s_dve, 1)])
                p.step("vector", lambda E: E.tensor_copy(sc_a[:, 1:2], bnag[:, 0:1]),
                       incs=[(s_dve, 1)])
                p.step("vector", lambda E: E.tensor_scalar(
                    sc_a[:, 4:5], bnag[:, 1:2], 1e-5, None, ALU.add), incs=[(s_dve, 1)])
                t_v = dve_flush(2)
                p.step("scalar", lambda E: E.activation(
                    sc_a[:, 5:6], sc_a[:, 4:5], AF.Ln, bias=0.0, scale=1.0),
                    waits=[(s_dve, t_v)], incs=[(s_act, 1)])
                p.step("scalar", lambda E: E.activation(
                    sc_a[:, 6:7], sc_a[:, 5:6], AF.Exp, bias=0.0, scale=-0.5),
                    incs=[(s_act, 1)])
                t_r = act_flush(1)
                t_ln = p.step("vector", lambda E, ys=ys: E.tensor_scalar(
                    ys, ys, sc_a[:, 1:2], sc_a[:, 6:7], ALU.subtract, ALU.mult),
                    waits=[(s_act, t_r)], incs=[(s_dve, 1)])
                y2_free[t] = p.step("sync", lambda E, sg_i=sg_i, ys=ys, t=t: E.dma_start(
                    out=dout[sg_i * 128:(sg_i + 1) * 128, :], in_=ys),
                    waits=[(s_dve, t_ln)], incs=[(s_o[t], 16)])

        p.step("sync", lambda E: E.nop(),
               waits=[(s_o[t], p.now(s_o[t])) for t in range(4)])

        with nc.Block() as block:
            p.emit_all(block)

        psd.close()
        ps2f.close()
        s2b.close()
        s2.close()
        ps2c.close()
        pfl_scope.close()

    return nc



def _rope_tables():
    a32 = np.arange(0, 32, 2).astype(np.float32)
    e_ = a32 / np.float32(32.0)
    base_pow = np.float32(10000.0) ** e_
    inv = (np.float32(1.0) / base_pow).astype(np.float32)
    pos = np.arange(T, dtype=np.float32)
    ang = pos[:, None] * inv[None, :]
    return np.cos(ang).astype(np.float32), np.sin(ang).astype(np.float32)


def _build_masks(chunks):
    m = np.zeros((128, MASK_COLS), np.float32)
    kpos = np.concatenate([np.arange(NPOOL) * 64 + 63, np.arange(1024, 2048)])
    for s, j in enumerate(chunks):
        W = SLOT_W[s]
        q = j * 128 + np.arange(128)
        mm = (kpos[None, :W] <= q[:, None]).astype(np.float32)
        m[:, SLOT_OFF[s]:SLOT_OFF[s] + W] = mm
    return m.astype(ml_dtypes.bfloat16)


def _interleave_qk(w_sem, w_geo, nh):
    out = np.empty((D, nh * 64), np.float32)
    for h in range(nh):
        out[:, h * 64:h * 64 + 32] = w_sem[:, h * 32:(h + 1) * 32]
        out[:, h * 64 + 32:h * 64 + 64] = w_geo[:, h * 32:(h + 1) * 32]
    return out


_PROG_CACHE = {}


def kernel(L0_attn_x, L0_attn_Wq_sem, L0_attn_Wk_sem, L0_attn_Wq_geo, L0_attn_Wk_geo,
           L0_attn_Wv, L0_attn_Wo, L0_attn_logit_scale, L0_ffn_w_gate, L0_ffn_w_up,
           L0_ffn_w_down, L0_ffn_subln_weight):
    x = np.ascontiguousarray(np.asarray(L0_attn_x, np.float32))
    wq = _interleave_qk(np.asarray(L0_attn_Wq_sem, np.float32),
                        np.asarray(L0_attn_Wq_geo, np.float32), H)
    wkv = np.concatenate([
        _interleave_qk(np.asarray(L0_attn_Wk_sem, np.float32),
                       np.asarray(L0_attn_Wk_geo, np.float32), HKV),
        np.asarray(L0_attn_Wv, np.float32)], axis=1)
    wo = np.ascontiguousarray(np.asarray(L0_attn_Wo, np.float32))
    ls = np.asarray(L0_attn_logit_scale, np.float32).reshape(1, H)
    gsub = np.asarray(L0_ffn_subln_weight, np.float32).reshape(1, FF)
    wgT = np.ascontiguousarray(np.asarray(L0_ffn_w_gate, np.float32).T)
    wuT = np.ascontiguousarray(np.asarray(L0_ffn_w_up, np.float32).T)
    wdT = np.ascontiguousarray(np.asarray(L0_ffn_w_down, np.float32).T)

    cos, sin = _rope_tables()
    ident = np.eye(128, dtype=ml_dtypes.bfloat16)
    pmat = np.zeros((1024, NPOOL), np.float32)
    for t in range(1024):
        pmat[t, t // 64] = 1.0 / 64.0
    pmat = pmat.astype(ml_dtypes.bfloat16)
    consts = np.zeros((128, 16), np.float32)
    cvals = [MAGIC, 1.0, 0.0, 1e-5, 1e-6, -0.5, 0.5, 127.0, -MAGIC,
             1.0 / D, 30000.0, 1.49, -1.49, 1.0 / FF, 2.0, -1.0]
    for j, v in enumerate(cvals):
        consts[:, j] = v
    ones_row = np.ones((1, 128), np.float32)
    cosk = np.zeros((128, 16 * 64), np.float32)
    sink = np.zeros((128, 16 * 64), np.float32)
    for c in range(16):
        blk_c = np.tile(cos[c * 128:(c + 1) * 128, :], (1, HKV))
        blk_s = np.tile(sin[c * 128:(c + 1) * 128, :], (1, HKV))
        cosk[:, c * 64:(c + 1) * 64] = blk_c
        sink[:, c * 64:(c + 1) * 64] = blk_s

    in_maps = []
    core_chunks = []
    for core in range(8):
        b = core // 2
        chunks = SET_A if core % 2 == 0 else SET_B
        core_chunks.append(chunks)
        xb_ = x[b]
        x_own = np.concatenate([xb_[j * 128:(j + 1) * 128] for j in chunks], axis=0)
        cosq = np.zeros((128, 8 * 256), np.float32)
        sinq = np.zeros((128, 8 * 256), np.float32)
        for s, j in enumerate(chunks):
            cosq[:, s * 256:(s + 1) * 256] = np.tile(cos[j * 128:(j + 1) * 128], (1, H))
            sinq[:, s * 256:(s + 1) * 256] = np.tile(sin[j * 128:(j + 1) * 128], (1, H))
        in_maps.append({
            "x_full": np.ascontiguousarray(xb_),
            "x_own": np.ascontiguousarray(x_own),
            "wkv": wkv, "wq": wq, "wo": wo, "ls": ls, "gsub": gsub,
            "d_cosq": cosq, "d_sinq": sinq, "d_cosk": cosk, "d_sink": sink,
            "d_masks": _build_masks(chunks),
            "d_ident": ident, "d_pmat": pmat, "d_consts": consts, "ones_row": ones_row,
            "shard_g": np.ascontiguousarray(wgT[core * 256:(core + 1) * 256]),
            "shard_u": np.ascontiguousarray(wuT[core * 256:(core + 1) * 256]),
            "shard_d": np.ascontiguousarray(wdT[core * 1024:(core + 1) * 1024]),
        })

    if "nc" not in _PROG_CACHE:
        _PROG_CACHE["nc"] = build_program()
    nc = _PROG_CACHE["nc"]
    res = run_bass_kernel_spmd(nc, in_maps, core_ids=list(range(8)))

    out = np.empty((B, T, D), np.float32)
    for core in range(8):
        b = core // 2
        y = np.asarray(res.results[core]["y"])
        for s, j in enumerate(core_chunks[core]):
            out[b, j * 128:(j + 1) * 128, :] = y[s * 128:(s + 1) * 128]
    return out


if __name__ == "__main__":
    nc = build_program()
    n = sum(1 for _ in nc.all_instructions())
    print("built ok, instructions:", n)


# version 26
# speedup vs baseline: 1.0414x; 1.0414x over previous
import sys
sys.path.insert(0, "/opt/trn_rl_repo")
import math
from contextlib import ExitStack

import numpy as np
import ml_dtypes

import concourse.bass as bass
import concourse.mybir as mybir
from concourse.bass_utils import run_bass_kernel_spmd

F32 = mybir.dt.float32
BF16 = mybir.dt.bfloat16
I32 = mybir.dt.int32
AF = mybir.ActivationFunctionType
ALU = mybir.AluOpType
AX = mybir.AxisListType

MAGIC = 1.5 * 2.0 ** 23
ISQ32 = 1.0 / math.sqrt(32.0)
DEBUG_TAPS = False

B, T, D = 4, 2048, 2048
H, HKV = 16, 4
NPOOL, NKEY = 16, 1040
FF = 8192

SET_A = [0, 14, 11, 10, 7, 6, 3, 2]
SET_B = [15, 13, 12, 9, 8, 5, 4, 1]
SLOT_W = [1040, 912, 656, 400, 144, 14, 10, 6]
MASK_COLS = sum(SLOT_W)
SLOT_OFF = [0]
for _w in SLOT_W:
    SLOT_OFF.append(SLOT_OFF[-1] + _w)

ENGINES = ("sync", "vector", "scalar", "tensor", "gpsimd")


class _Sem:
    __slots__ = ("name", "h")

    def __init__(self, name, h):
        self.name, self.h = name, h


class Plan:
    SELF_SYNC = ("vector", "scalar", "gpsimd")

    def __init__(self, nc, sem_stack):
        self.nc = nc
        self.sem_stack = sem_stack
        self.steps = {e: [] for e in ENGINES}
        self.counts = {}
        self.self_sems = {}

    def sem(self, name):
        h = self.sem_stack.enter_context(self.nc.semaphore(name))
        self.counts[name] = 0
        return _Sem(name, h)

    def set_self_sems(self, mapping):
        self.self_sems = mapping

    def step(self, eng, emit, waits=(), incs=(), dma=False):
        incs = list(incs)
        n_caller = len(incs)
        if any(n >= 16 for _s, n in incs):
            dma = True
        if eng in self.SELF_SYNC and not dma:
            ssem = self.self_sems[eng]
            if not any(s is ssem for s, n in incs):
                incs.append((ssem, 1))
        tickets = []
        for s, n in incs:
            self.counts[s.name] += n
            tickets.append(self.counts[s.name])
        self.steps[eng].append((emit, tuple(waits), tuple(incs)))
        if n_caller <= 1:
            return tickets[0] if tickets else 0
        return tickets[:n_caller]

    def now(self, sem):
        return self.counts[sem.name]

    def emit_all(self, block):
        def run(eng_name, handle):
            ssem = self.self_sems.get(eng_name)
            running = 0
            for emit, waits, incs in self.steps[eng_name]:
                if ssem is not None and running > 0:
                    handle.wait_ge(ssem.h, running)
                for s, v in waits:
                    if v > 0:
                        handle.wait_ge(s.h, v)
                inst = emit(handle)
                for s, n in incs:
                    inst.then_inc(s.h, n)
                    if s is ssem:
                        running += n

        @block.sync
        def _(e):
            run("sync", e)

        @block.vector
        def _(e):
            run("vector", e)

        @block.scalar
        def _(e):
            run("scalar", e)

        @block.tensor
        def _(e):
            run("tensor", e)

        @block.gpsimd
        def _(e):
            run("gpsimd", e)


def build_program():
    nc = bass.Bass("TRN2", target_bir_lowering=False, debug=False)

    dxf = nc.dram_tensor("x_full", [T, D], F32, kind="ExternalInput").ap()
    dxo = nc.dram_tensor("x_own", [1024, D], F32, kind="ExternalInput").ap()
    dWkv = nc.dram_tensor("wkv", [D, 512], F32, kind="ExternalInput").ap()
    dWq = nc.dram_tensor("wq", [D, 1024], F32, kind="ExternalInput").ap()
    dWo = nc.dram_tensor("wo", [1024, D], F32, kind="ExternalInput").ap()
    dls = nc.dram_tensor("ls", [1, H], F32, kind="ExternalInput").ap()
    dgsub = nc.dram_tensor("gsub", [1, FF], F32, kind="ExternalInput").ap()
    dcosq = nc.dram_tensor("d_cosq", [128, 8 * 256], F32, kind="ExternalInput").ap()
    dsinq = nc.dram_tensor("d_sinq", [128, 8 * 256], F32, kind="ExternalInput").ap()
    dcosk = nc.dram_tensor("d_cosk", [128, 16 * 64], F32, kind="ExternalInput").ap()
    dsink = nc.dram_tensor("d_sink", [128, 16 * 64], F32, kind="ExternalInput").ap()
    dmask = nc.dram_tensor("d_masks", [128, MASK_COLS], BF16, kind="ExternalInput").ap()
    dident = nc.dram_tensor("d_ident", [128, 128], BF16, kind="ExternalInput").ap()
    dpmat = nc.dram_tensor("d_pmat", [1024, NPOOL], BF16, kind="ExternalInput").ap()
    dconsts = nc.dram_tensor("d_consts", [128, 16], F32, kind="ExternalInput").ap()
    dones = nc.dram_tensor("ones_row", [1, 128], F32, kind="ExternalInput").ap()
    dshg = nc.dram_tensor("shard_g", [256, FF], F32, kind="ExternalInput").ap()
    dshu = nc.dram_tensor("shard_u", [256, FF], F32, kind="ExternalInput").ap()
    dshd = nc.dram_tensor("shard_d", [1024, D], F32, kind="ExternalInput").ap()

    dout = nc.dram_tensor("y", [1024, D], F32, kind="ExternalOutput").ap()
    if DEBUG_TAPS:
        d_x1 = nc.dram_tensor("dbg_x1", [1024, D], F32, kind="ExternalOutput").ap()
        d_ao = nc.dram_tensor("dbg_ao", [1024, 1024], BF16, kind="ExternalOutput").ap()
        d_qt = nc.dram_tensor("dbg_qt", [128, 8 * 1024], BF16, kind="ExternalOutput").ap()
        d_kt = nc.dram_tensor("dbg_kt", [128, 4 * NKEY], BF16, kind="ExternalOutput").ap()
        d_pool = nc.dram_tensor("dbg_pool", [NPOOL, 512], BF16, kind="ExternalOutput").ap()
        d_slab = nc.dram_tensor("dbg_slab", [128, 4 * FF], BF16, kind="ExternalOutput").ap()
        d_aq = nc.dram_tensor("dbg_aq", [128, 4 * FF], BF16, kind="ExternalOutput").ap()
        d_y2 = nc.dram_tensor("dbg_y2", [128, 4 * D], F32, kind="ExternalOutput").ap()
        d_rs = nc.dram_tensor("dbg_rs", [128, 16], F32, kind="ExternalOutput").ap()
        d_gup = nc.dram_tensor("dbg_gup", [128, 256], F32, kind="ExternalOutput").ap()
        d_qg = nc.dram_tensor("dbg_qg", [1024, 256], BF16, kind="ExternalOutput").ap()
        d_st = nc.dram_tensor("dbg_st", [128, 96], F32, kind="ExternalOutput").ap()
        d_mw = nc.dram_tensor("dbg_mw", [128, 8], F32, kind="ExternalOutput").ap()
        d_qd = nc.dram_tensor("dbg_qd", [1024, 512], BF16, kind="ExternalOutput").ap()

    ag_g_in = nc.dram_tensor("ag_g_in", [8192, 256], BF16)
    ag_u_in = nc.dram_tensor("ag_u_in", [8192, 256], BF16)
    ag_d_in = nc.dram_tensor("ag_d_in", [4096, 512], BF16)
    qgT = nc.dram_tensor("qgT", [65536, 256], BF16, addr_space="Shared")
    quT = nc.dram_tensor("quT", [65536, 256], BF16, addr_space="Shared")
    qdT = nc.dram_tensor("qdT", [32768, 512], BF16, addr_space="Shared")
    ar_in = nc.dram_tensor("ar_in", [1, 3], F32)
    ar_out = nc.dram_tensor("ar_out", [1, 3], F32, addr_space="Shared")
    ar_out2 = nc.dram_tensor("ar_out2", [1, 3], F32, addr_space="Shared")
    x1_spill = nc.dram_tensor("x1_spill", [1024, D], F32)
    xqt_spill = nc.dram_tensor("xqt_spill", [128, 8 * 2048], BF16)

    sem_stack = ExitStack()
    p = Plan(nc, sem_stack)
    s_pe = p.sem("pe")
    s_dve = p.sem("dve")
    s_act = p.sem("act")
    s_pool = p.sem("pool")
    p.set_self_sems({"vector": s_dve, "scalar": s_act, "gpsimd": s_pool})
    s_ldc1 = p.sem("ldc1")
    s_ldc2 = p.sem("ldc2")
    s_ldc3 = [p.sem("ldc3a"), p.sem("ldc3b")]
    s_ldc4 = p.sem("ldc4")
    s_ldxa = [p.sem("ldxa0"), p.sem("ldxa1")]
    s_ldxb = p.sem("ldxb")
    s_ldwa = [p.sem("ldwa0"), p.sem("ldwa1")]
    s_ldwb = [p.sem("ldwb0"), p.sem("ldwb1")]
    s_lwo = p.sem("lwo")
    s_ktd = p.sem("ktd")
    s_ldsa = [p.sem("ldsa0"), p.sem("ldsa1")]
    s_ldsb = p.sem("ldsb")
    s_car = p.sem("car")
    s_cswm = p.sem("cswm")
    s_cmask = p.sem("cmask")
    s_cag = [p.sem("cag0"), p.sem("cag1")]
    s_ccar = p.sem("ccar")
    s_ccg = p.sem("ccg")
    s_ccu = p.sem("ccu")
    s_ccd = p.sem("ccd")
    s_ccf = p.sem("ccf")
    s_spx = p.sem("spx")
    s_sp1 = p.sem("sp1")
    s_spq = p.sem("spq")
    s_xqt = p.sem("xqt")
    s_lg = [p.sem("lg0"), p.sem("lg1")]
    s_lu = [p.sem("lu0"), p.sem("lu1")]
    s_lwd = [p.sem(f"lwd{i}") for i in range(4)]
    s_lx = [p.sem("lx0"), p.sem("lx1")]
    s_o = [p.sem(f"out{i}") for i in range(4)]

    CONST_VALS = [MAGIC, 1.0, 0.0, 1e-5, 1e-6, -0.5, 0.5, 127.0, -MAGIC,
                  1.0 / D, 30000.0, 1.49, -1.49, 1.0 / FF, 2.0, -1.0]

    ctx = ExitStack()
    with ctx, sem_stack:
        e = ctx.enter_context

        consts = e(nc.sbuf_tensor("consts", [128, 16], F32))
        for j, v in enumerate(CONST_VALS):
            nc.const_aps.register(F32, float(v), consts[:, j:j + 1])
        idt = e(nc.sbuf_tensor("idt", [128, 128], BF16))
        ones_r = e(nc.sbuf_tensor("ones_r", [1, 128], F32))
        ones_c = e(nc.sbuf_tensor("ones_c", [128, 1], F32))
        els_raw = e(nc.sbuf_tensor("els_raw", [1, H], F32))
        els_rep = e(nc.sbuf_tensor("els_rep", [128, H], F32))
        swm = e(nc.sbuf_tensor("swm", [1, 8], F32))
        shsum = e(nc.sbuf_tensor("shsum", [3, 1], F32))
        sw_rep = e(nc.sbuf_tensor("sw_rep", [128, 4], F32))
        mw_rep = e(nc.sbuf_tensor("mw_rep", [128, 4], F32))
        pooled = e(nc.sbuf_tensor("pooled", [NPOOL, 512], BF16))
        partials = e(nc.sbuf_tensor("partials", [128, 52], F32))
        st_s1 = e(nc.sbuf_tensor("st_s1", [128, 8], F32))
        st_rowS = e(nc.sbuf_tensor("st_rowS", [128, 8], F32))
        st_drow = e(nc.sbuf_tensor("st_drow", [128, 8], F32))
        gup_bn = e(nc.sbuf_tensor("gup_bn", [128, 8 * 16 * 6], F32))
        bnln = e(nc.sbuf_tensor("bnln", [128, 24], F32))
        st_st = e(nc.sbuf_tensor("st_st", [128, 96], F32))
        bnag = e(nc.sbuf_tensor("bnag", [128, 2], F32))
        gup_mxg = e(nc.sbuf_tensor("gup_mxg", [128, 8 * 16], F32))
        sc_a = e(nc.sbuf_tensor("sc_a", [128, 8], F32))
        sc_b = e(nc.sbuf_tensor("sc_b", [128, 8], F32))
        sc_c = e(nc.sbuf_tensor("sc_c", [128, 8], F32))
        flb = e(nc.sbuf_tensor("flb", [128, 4], F32))
        flb_init = [0]

        def dve_flush(n=1):
            t = 0
            for _ in range(n):
                t = p.step("vector", lambda E: E.tensor_copy(flb[:, 1:2], flb[:, 0:1]),
                           incs=[(s_dve, 1)])
            return t

        def act_flush(n=1):
            t = 0
            for _ in range(n):
                t = p.step("scalar", lambda E: E.activation(
                    flb[:, 3:4], flb[:, 2:3], AF.Copy, bias=0.0, scale=1.0),
                    waits=[(s_dve, flb_init[0])], incs=[(s_act, 1)])
            return t

        pfl_scope = ExitStack()
        pFL = pfl_scope.enter_context(nc.psum_tensor("pFL", [128, 4], F32))
        pbc_scope = ExitStack()
        pbc = pbc_scope.enter_context(nc.psum_tensor("pbc", [128, 512], F32))

        def pe_flush():
            return p.step("tensor", lambda E: E.matmul(
                pFL[0:1, 0:1], ones_r[0:1, 0:1], ones_r[0:1, 0:1],
                start=True, stop=True), incs=[(s_pe, 1)])

        for dst, src in ((consts[:], dconsts), (idt[:], dident), (ones_r[:], dones),
                         (els_raw[:], dls)):
            p.step("sync", lambda E, dst=dst, src=src: E.dma_start(out=dst, in_=src),
                   incs=[(s_ldc1, 16)])
        t_ldc4 = p.now(s_ldc1)
        p.step("vector", lambda E: E.memset(ones_c[:], 1.0), incs=[(s_dve, 1)])
        t_onec = p.now(s_dve)
        flb_init[0] = p.step("vector", lambda E: E.memset(flb[:], 0.0),
                             incs=[(s_dve, 1)])

        p.step("scalar", lambda E: E.activation(els_raw[:], els_raw[:], AF.Exp,
                                                bias=0.0, scale=1.0),
               waits=[(s_ldc1, t_ldc4)], incs=[(s_act, 1)])
        p.step("vector", lambda E: E.tensor_scalar(els_raw[:], els_raw[:], ISQ32, None,
                                                   ALU.mult),
               waits=[(s_act, p.now(s_act))], incs=[(s_dve, 1)])
        dve_flush(2)
        tt = p.step("tensor", lambda E: E.matmul(pbc[:, 0:H], ones_r[:], els_raw[:],
                                                 start=True, stop=True),
                    waits=[(s_dve, p.now(s_dve))], incs=[(s_pe, 1)])
        tt = pe_flush()
        tt = pe_flush()
        t_elsrep = p.step("vector", lambda E: E.tensor_copy(els_rep[:], pbc[:, 0:H]),
                          waits=[(s_pe, tt)], incs=[(s_dve, 1)])

        sx = ExitStack()
        x_res = sx.enter_context(nc.sbuf_tensor("x_res", [128, 8 * D], F32))

        skv = ExitStack()
        KV = [skv.enter_context(nc.sbuf_tensor(f"kv{c}", [128, 512], BF16)) for c in range(16)]
        KT = [skv.enter_context(nc.sbuf_tensor(f"kt{k}", [128, NKEY], BF16)) for k in range(HKV)]
        QT = [skv.enter_context(nc.sbuf_tensor(f"qt{s}", [128, 1024], BF16)) for s in range(8)]
        squant = ExitStack()
        shstage = [squant.enter_context(nc.sbuf_tensor(f"shst{i}", [128, 1024], F32)) for i in range(2)]
        shq1s = [squant.enter_context(nc.sbuf_tensor(f"shq1{i}", [128, 1024], F32))
                 for i in range(2)]
        shi = squant.enter_context(nc.sbuf_tensor("shi", [128, 1024], I32))
        shbs = [squant.enter_context(nc.sbuf_tensor(f"shb{i}", [128, 1024], BF16))
                for i in range(2)]

        shard_tiles = []
        for widx, dsh in enumerate((dshg, dshu, dshd)):
            v = dsh.rearrange("(r p) f -> r p f", p=128)
            nr = 2 if widx < 2 else 8
            ncb = 8 if widx < 2 else 2
            for r in range(nr):
                for cb in range(ncb):
                    shard_tiles.append((v[r, :, cb * 1024:(cb + 1) * 1024], widx, r, cb))

        def quant_gen():
            last_red = {0: 0, 1: 0}
            for i, (ap_, widx, r, cb) in enumerate(shard_tiles):
                par = i % 2
                tld = p.step("gpsimd", lambda E, ap_=ap_, par=par: E.dma_start(
                    out=shstage[par][:], in_=ap_),
                    waits=[(s_dve, last_red[par])], incs=[(s_ldsa[par], 16)])
                last_red[par] = p.step(
                    "vector", lambda E, i=i, par=par: E.reduce_sum(
                        partials[:, i:i + 1], shstage[par][:], AX.X,
                        apply_absolute_value=True),
                    waits=[(s_ldsa[par], tld)], incs=[(s_dve, 1)], dma=True)
                yield
            for widx in range(3):
                lo, hi = widx * 16, widx * 16 + 16
                p.step("vector", lambda E, widx=widx, lo=lo, hi=hi: E.reduce_sum(
                    sc_a[:, widx:widx + 1], partials[:, lo:hi], AX.X), incs=[(s_dve, 1)])
            t_pred = dve_flush(2)
            tt = p.step("tensor", lambda E: E.matmul(pbc[0:3, 16:17], sc_a[:, 0:3], ones_c[:],
                                                     start=True, stop=True),
                        waits=[(s_dve, t_pred), (s_dve, t_onec)], incs=[(s_pe, 1)])
            tt = pe_flush()
            tt = pe_flush()
            t_shsum = p.step("vector", lambda E: E.tensor_copy(shsum[:], pbc[0:3, 16:17]),
                             waits=[(s_pe, tt)], incs=[(s_dve, 1)])
            t_shsum = dve_flush(2)
            p.step("gpsimd", lambda E: E.dma_start(
                out=ar_in.ap().rearrange("a b -> b a"), in_=shsum[:]),
                waits=[(s_dve, t_shsum)], incs=[(s_car, 16)], dma=True)
            p.step("gpsimd", lambda E: E.collective_compute(
                "AllReduce", ALU.add, replica_groups=[list(range(8))],
                ins=[ar_in.ap().opt()], outs=[ar_out.ap().opt()]),
                waits=[(s_car, 16)], incs=[(s_ccar, 1)], dma=True)
            p.step("gpsimd", lambda E: E.dma_start(out=swm[:, 4:7], in_=ar_out.ap()),
                   waits=[(s_ccar, 1)], incs=[(s_cswm, 16)], dma=True)
            p.step("vector", lambda E: E.tensor_scalar(swm[:, 0:3], swm[:, 4:7],
                                                       1.0 / (2048.0 * 8192.0), 1e-5,
                                                       ALU.mult, ALU.max),
                   waits=[(s_cswm, 16)], incs=[(s_dve, 1)])
            p.step("vector", lambda E: E.reciprocal(swm[:, 4:7], swm[:, 0:3]),
                   incs=[(s_dve, 1)])
            t_sw = dve_flush(2)
            tt = p.step("tensor", lambda E: E.matmul(pbc[:, 20:23], ones_r[:], swm[:, 0:3],
                                                     start=True, stop=True),
                        waits=[(s_dve, t_sw)], incs=[(s_pe, 1)])
            tt2 = p.step("tensor", lambda E: E.matmul(pbc[:, 24:27], ones_r[:], swm[:, 4:7],
                                                      start=True, stop=True), incs=[(s_pe, 1)])
            tt2 = pe_flush()
            tt2 = pe_flush()
            p.step("vector", lambda E: E.tensor_copy(mw_rep[:, 0:3], pbc[:, 20:23]),
                   waits=[(s_pe, tt2)], incs=[(s_dve, 1)])
            t_swrep = p.step("vector", lambda E: E.tensor_copy(sw_rep[:, 0:3], pbc[:, 24:27]),
                             waits=[(s_pe, tt2)], incs=[(s_dve, 1)])

            def ag_dst(widx, r, cb):
                if widx < 2:
                    agt = ag_g_in if widx == 0 else ag_u_in
                    return agt.ap().rearrange("(fb c p) n -> c p fb n", c=2, p=128)[
                        r, :, cb * 4:(cb + 1) * 4, :]
                return ag_d_in.ap().rearrange("(d fl p) n -> fl p d n", fl=8, p=128)[
                    r, :, cb * 2:(cb + 1) * 2, :]

            last_q1 = {0: last_red[0], 1: last_red[1]}
            q3_tick = {0: 0, 1: 0}
            store_tick = {0: 0, 1: 0}
            for i, (ap_, widx, r, cb) in enumerate(shard_tiles):
                par = i % 2
                shb = shbs[par]
                shq1 = shq1s[par]
                tld = p.step("gpsimd", lambda E, ap_=ap_, par=par: E.dma_start(
                    out=shstage[par][:], in_=ap_),
                    waits=[(s_dve, last_q1[par])], incs=[(s_ldsa[par], 16)], dma=True)
                p.step("vector", lambda E, widx=widx, par=par, shq1=shq1: E.tensor_scalar(
                    shq1[:], shstage[par][:], sw_rep[:, widx:widx + 1], 1.49,
                    ALU.mult, ALU.min),
                    waits=[(s_ldsa[par], tld), (s_dve, t_swrep)], incs=[(s_dve, 1)])
                last_q1[par] = p.step("vector", lambda E, shq1=shq1: E.tensor_scalar(
                    shq1[:], shq1[:], -1.49, MAGIC, ALU.max, ALU.add),
                    incs=[(s_dve, 1)])
                q3_tick[par] = p.step("vector", lambda E, shq1=shq1: E.tensor_scalar(
                    shi[:], shq1[:], -MAGIC, None, ALU.add),
                    incs=[(s_dve, 1)])
                tq = p.step("vector", lambda E, shb=shb: E.tensor_copy(shb[:], shi[:]),
                            waits=[(s_cag[par], store_tick[par])], incs=[(s_dve, 1)])
                if widx < 2:
                    src = shb[:].rearrange("p (fb n) -> p fb n", n=256)
                else:
                    src = shb[:].rearrange("p (d n) -> p d n", n=512)
                p.step("gpsimd", lambda E, dst=ag_dst(widx, r, cb), src=src: E.dma_start(
                    out=dst, in_=src),
                    waits=[(s_dve, tq)], incs=[(s_cag[par], 16)], dma=True)
                store_tick[par] = p.now(s_cag[par])
                yield
            t_ag_all = [p.now(s_cag[0]), p.now(s_cag[1])]
            _cc['g'] = p.step("gpsimd", lambda E: E.collective_compute(
                "AllGather", ALU.bypass, replica_groups=[list(range(8))],
                ins=[ag_g_in.ap().opt()], outs=[qgT.ap().opt()]),
                waits=[(s_cag[0], t_ag_all[0]), (s_cag[1], t_ag_all[1])],
                incs=[(s_ccg, 1)], dma=True)
            _cc['u'] = p.step("gpsimd", lambda E: E.collective_compute(
                "AllGather", ALU.bypass, replica_groups=[list(range(8))],
                ins=[ag_u_in.ap().opt()], outs=[quT.ap().opt()]), incs=[(s_ccu, 1)], dma=True)
            _cc['d'] = p.step("gpsimd", lambda E: E.collective_compute(
                "AllGather", ALU.bypass, replica_groups=[list(range(8))],
                ins=[ag_d_in.ap().opt()], outs=[qdT.ap().opt()]), incs=[(s_ccd, 1)], dma=True)
            _cc['fl'] = p.step("gpsimd", lambda E: E.collective_compute(
                "AllReduce", ALU.add, replica_groups=[list(range(8))],
                ins=[ar_in.ap().opt()], outs=[ar_out2.ap().opt()]),
                incs=[(s_ccf, 1)], dma=True)

            while True:
                yield

        qgen = quant_gen()
        _cc = {}
        def _qset():
            pass
        sw1 = ExitStack()
        Wkv_b = [sw1.enter_context(nc.sbuf_tensor(f"wkvb{c}", [128, 512], BF16))
                 for c in range(16)]
        cosk = sw1.enter_context(nc.sbuf_tensor("cosk", [128, 16 * 64], F32))
        sink = sw1.enter_context(nc.sbuf_tensor("sink", [128, 16 * 64], F32))
        pmat = sw1.enter_context(nc.sbuf_tensor("pmat", [128, 8 * NPOOL], BF16))
        wstage1 = [sw1.enter_context(nc.sbuf_tensor(f"wst{i}", [128, 512], F32))
                  for i in range(2)]
        xstage1 = [sw1.enter_context(nc.sbuf_tensor(f"xst{i}", [128, D], F32))
                  for i in range(2)]
        xbst1 = sw1.enter_context(nc.sbuf_tensor("xbst1", [128, D], BF16))
        xtst1 = sw1.enter_context(nc.sbuf_tensor("xtst1", [128, D], BF16))
        rtmpk1 = sw1.enter_context(nc.sbuf_tensor("rtmpk1", [128, 256], BF16))
        rtmpk2 = sw1.enter_context(nc.sbuf_tensor("rtmpk2", [128, 256], BF16))

        for dst, src in ((cosk[:], dcosk), (sink[:], dsink),
                         (pmat[:].rearrange("p (c n) -> p c n", n=NPOOL),
                          dpmat.rearrange("(c p) n -> p c n", p=128))):
            p.step("sync", lambda E, dst=dst, src=src: E.dma_start(out=dst, in_=src),
                   incs=[(s_ldc2, 16)])
        t_ldck = p.now(s_ldc2)

        conv_t = [0, 0]
        for c in range(16):
            par = c % 2
            tld = p.step("sync", lambda E, c=c, par=par: E.dma_start(
                out=wstage1[par][:], in_=dWkv[c * 128:(c + 1) * 128, :]),
                waits=[(s_act, conv_t[par])], incs=[(s_ldwa[par], 16)])
            conv_t[par] = p.step("scalar", lambda E, c=c, par=par: E.activation(
                Wkv_b[c][:], wstage1[par][:], AF.Copy, bias=0.0, scale=1.0),
                waits=[(s_ldwa[par], tld)], incs=[(s_act, 1)])
        t_wkv = p.now(s_act)

        ps1 = ExitStack()
        pQKV1 = [ps1.enter_context(nc.psum_tensor(f"pqkv{i}", [128, 512], F32))
                for i in range(2)]
        pTr1 = ps1.enter_context(nc.psum_tensor("pTr1", [128, 2048], BF16))
        pPool = ps1.enter_context(nc.psum_tensor("pPool", [NPOOL, 512], F32))

        def rope_steps(view_base, cosv, sinv, wait_t, rA, rB, wait_sem=None):
            lo, hi = view_base[:, :, 32:48], view_base[:, :, 48:64]
            nh = lo.shape[1]
            t1v = rA[:, 0:nh * 16].rearrange("p (h d) -> p h d", d=16)
            t2v = rB[:, 0:nh * 16].rearrange("p (h d) -> p h d", d=16)
            p.step("vector", lambda E: E.tensor_tensor(t1v, lo, sinv, ALU.mult),
                   waits=[(wait_sem or s_dve, wait_t)], incs=[(s_dve, 1)])
            p.step("vector", lambda E: E.tensor_tensor(t2v, lo, cosv, ALU.mult),
                   incs=[(s_dve, 1)])
            p.step("vector", lambda E: E.scalar_tensor_tensor(
                lo, hi, -1.0, sinv, ALU.mult, ALU.mult), incs=[(s_dve, 1)])
            p.step("vector", lambda E: E.tensor_tensor(lo, lo, t2v, ALU.add),
                   incs=[(s_dve, 1)])
            p.step("vector", lambda E: E.tensor_tensor(hi, hi, cosv, ALU.mult),
                   incs=[(s_dve, 1)])
            return p.step("vector", lambda E: E.tensor_tensor(hi, hi, t1v, ALU.add),
                          incs=[(s_dve, 1)])

        xst_free = [0, 0]
        xb_free = 0
        xt_free = 0
        pTr_free = 0
        pq_free = [0, 0]
        kv_rope = [0] * 16
        for c in range(16):
            par = c % 2
            tld = p.step("sync", lambda E, c=c, par=par: E.dma_start(
                out=xstage1[par][:], in_=dxf[c * 128:(c + 1) * 128, :]),
                waits=[(s_act, xst_free[par])], incs=[(s_ldxa[par], 16)])
            tcv = p.step("scalar", lambda E, par=par: E.activation(
                xbst1[:], xstage1[par][:], AF.Copy, bias=0.0, scale=1.0),
                waits=[(s_ldxa[par], tld), (s_pe, xb_free)], incs=[(s_act, 1)])
            xst_free[par] = tcv
            for c2 in range(16):
                tt = p.step("tensor", lambda E, c2=c2: E.transpose(
                    pTr1[:, c2 * 128:(c2 + 1) * 128], xbst1[:, c2 * 128:(c2 + 1) * 128],
                    idt[:]),
                    waits=([(s_act, tcv), (s_act, pTr_free)] if c2 == 0 else ()),
                    incs=[(s_pe, 1)])
            xb_free = tt
            tev = p.step("scalar", lambda E: E.activation(
                xtst1[:], pTr1[:], AF.Copy, bias=0.0, scale=1.0),
                waits=[(s_pe, tt), (s_pe, xt_free)], incs=[(s_act, 1)])
            pTr_free = tev
            for c2 in range(16):
                w = ([(s_act, tev), (s_act, t_wkv), (s_act, pq_free[par])]
                     if c2 == 0 else ())
                tmm = p.step("tensor", lambda E, c2=c2, par=par: E.matmul(
                    pQKV1[par][:], xtst1[:, c2 * 128:(c2 + 1) * 128], Wkv_b[c2][:],
                    start=(c2 == 0), stop=(c2 == 15)),
                    waits=w, incs=[(s_pe, 1)])
            xt_free = tmm
            tev2 = p.step("scalar", lambda E, c=c, par=par: E.activation(
                KV[c][:], pQKV1[par][:], AF.Copy, bias=0.0, scale=1.0),
                waits=[(s_pe, tmm), (s_ldc2, t_ldck)], incs=[(s_act, 1)])
            pq_free[par] = tev2
            kvv = KV[c][:, 0:256].rearrange("p (h d) -> p h d", d=64)
            ck = cosk[:, c * 64:(c + 1) * 64].rearrange("p (h d) -> p h d", d=16)
            sk = sink[:, c * 64:(c + 1) * 64].rearrange("p (h d) -> p h d", d=16)
            kv_rope[c] = rope_steps(kvv, ck, sk, tev2, rtmpk1, rtmpk2,
                                    wait_sem=s_act)
            for _ in range(4):
                next(qgen)

        for c in range(8):
            w = [(s_dve, kv_rope[7])] if c == 0 else ()
            tpool = p.step("tensor", lambda E, c=c: E.matmul(
                pPool[:], pmat[:, c * NPOOL:(c + 1) * NPOOL], KV[c][:],
                start=(c == 0), stop=(c == 7)),
                waits=w, incs=[(s_pe, 1)])
        tpool = pe_flush()
        t_pooled = p.step("scalar", lambda E: E.activation(
            pooled[:], pPool[:], AF.Copy, bias=0.0, scale=1.0),
            waits=[(s_pe, tpool)], incs=[(s_act, 1)])

        kt_dup = [0] * HKV
        for k in range(HKV):
            p.step("tensor", lambda E, k=k: E.transpose(
                pTr1[0:64, 1024:1040], pooled[:, k * 64:(k + 1) * 64],
                idt[0:NPOOL, 0:NPOOL]),
                waits=[(s_act, t_pooled), (s_act, pTr_free)] +
                      ([(s_dve, kv_rope[15])] if k == 0 else []),
                incs=[(s_pe, 1)])
            for c in range(8, 16):
                p.step("tensor", lambda E, k=k, c=c: E.transpose(
                    pTr1[0:64, (c - 8) * 128:(c - 7) * 128],
                    KV[c][:, k * 64:(k + 1) * 64], idt[:]),
                    incs=[(s_pe, 1)])
            tt = pe_flush()
            p.step("scalar", lambda E, k=k: E.activation(
                KT[k][0:64, 0:16], pTr1[0:64, 1024:1040], AF.Copy,
                bias=0.0, scale=1.0),
                waits=[(s_pe, tt)], incs=[(s_act, 1)])
            pTr_free = p.step("scalar", lambda E, k=k: E.activation(
                KT[k][0:64, 16:1040], pTr1[0:64, 0:1024], AF.Copy,
                bias=0.0, scale=1.0),
                incs=[(s_act, 1)])
            kt_dup[k] = p.step("sync", lambda E, k=k: E.dma_start(
                out=KT[k][64:128, :], in_=KT[k][0:64, :]),
                waits=[(s_act, pTr_free)], incs=[(s_ktd, 16)])

        t1a_gate = [(s_dve, p.now(s_dve)), (s_pe, p.now(s_pe)),
                    (s_act, p.now(s_act))]
        ps1.close()
        sw1.close()

        sw2 = ExitStack()
        Wq_b = [sw2.enter_context(nc.sbuf_tensor(f"wqb{c}", [128, 1024], BF16))
                for c in range(16)]
        cosqs = [sw2.enter_context(nc.sbuf_tensor(f"cosq{i}", [128, 256], F32))
                 for i in range(2)]
        sinqs = [sw2.enter_context(nc.sbuf_tensor(f"sinq{i}", [128, 256], F32))
                 for i in range(2)]
        wstage2 = [sw2.enter_context(nc.sbuf_tensor(f"wst2{i}", [128, 1024], F32))
                  for i in range(2)]
        xstage2 = sw2.enter_context(nc.sbuf_tensor("xst2", [128, D], F32))
        xbst2 = sw2.enter_context(nc.sbuf_tensor("xbst2", [128, D], BF16))
        xtst2 = sw2.enter_context(nc.sbuf_tensor("xtst2", [128, D], BF16))
        Qb = sw2.enter_context(nc.sbuf_tensor("qb", [128, 1024], BF16))
        rtmpq1 = sw2.enter_context(nc.sbuf_tensor("rtmpa", [128, 256], BF16))
        rtmpq2 = sw2.enter_context(nc.sbuf_tensor("rtmpb", [128, 256], BF16))

        rope_t = [0, 0]

        conv_t = [0, 0]
        for c in range(16):
            par = c % 2
            tld = p.step("sync", lambda E, c=c, par=par: E.dma_start(
                out=wstage2[par][:], in_=dWq[c * 128:(c + 1) * 128, :]),
                waits=[(s_act, conv_t[par])] + (t1a_gate if c < 2 else []),
                incs=[(s_ldwb[par], 16)])
            conv_t[par] = p.step("scalar", lambda E, c=c, par=par: E.activation(
                Wq_b[c][:], wstage2[par][:], AF.Copy, bias=0.0, scale=1.0),
                waits=[(s_ldwb[par], tld)], incs=[(s_act, 1)])
        t_wq = p.now(s_act)

        ps2 = ExitStack()
        pQKV2 = [ps2.enter_context(nc.psum_tensor(f"pq2{i}", [128, 512], F32))
                for i in range(2)]
        pTr2x = ps2.enter_context(nc.psum_tensor("pTr2", [128, 2048], BF16))

        xst2_free = 0
        xb_free = 0
        xt_free = 0
        pTr_free = 0
        pq_free = [0, 0]
        for s in range(8):
            tld = p.step("sync", lambda E, s=s: E.dma_start(
                out=xstage2[:], in_=dxo[s * 128:(s + 1) * 128, :]),
                waits=[(s_act, xst2_free), (s_spx, 16 * s)], incs=[(s_ldxb, 16)])
            p.step("sync", lambda E, s=s: E.dma_start(
                out=x_res[:, s * D:(s + 1) * D], in_=xstage2[:]),
                waits=[(s_ldxb, tld)], incs=[(s_spx, 16)])
            tcv = p.step("scalar", lambda E: E.activation(
                xbst2[:], xstage2[:], AF.Copy, bias=0.0, scale=1.0),
                waits=[(s_ldxb, tld), (s_pe, xb_free)], incs=[(s_act, 1)])
            xst2_free = tcv
            for c2 in range(16):
                tt = p.step("tensor", lambda E, c2=c2: E.transpose(
                    pTr2x[:, c2 * 128:(c2 + 1) * 128], xbst2[:, c2 * 128:(c2 + 1) * 128],
                    idt[:]),
                    waits=([(s_act, tcv), (s_act, pTr_free)] if c2 == 0 else ()),
                    incs=[(s_pe, 1)])
            xb_free = tt
            tev = p.step("scalar", lambda E: E.activation(
                xtst2[:], pTr2x[:], AF.Copy, bias=0.0, scale=1.0),
                waits=[(s_pe, tt), (s_pe, xt_free)], incs=[(s_act, 1)])
            pTr_free = tev
            tq_ev = [0, 0]
            for g in range(2):
                for c2 in range(16):
                    w = ([(s_act, tev), (s_act, t_wq), (s_act, pq_free[g])]
                         if c2 == 0 else ())
                    tmm = p.step("tensor", lambda E, c2=c2, g=g: E.matmul(
                        pQKV2[g][:], xtst2[:, c2 * 128:(c2 + 1) * 128],
                        Wq_b[c2][:, g * 512:(g + 1) * 512],
                        start=(c2 == 0), stop=(c2 == 15)),
                        waits=w, incs=[(s_pe, 1)])
                if g == 1:
                    xt_free = tmm
                tq_ev[g] = p.step("scalar", lambda E, g=g: E.activation(
                    Qb[:, g * 512:(g + 1) * 512], pQKV2[g][:], AF.Copy,
                    bias=0.0, scale=1.0),
                    waits=[(s_pe, tmm)], incs=[(s_act, 1)])
                pq_free[g] = tq_ev[g]
            spar = s % 2
            tld_cq = p.step("sync", lambda E, s=s, spar=spar: E.dma_start(
                out=cosqs[spar][:], in_=dcosq[:, s * 256:(s + 1) * 256]),
                waits=[(s_dve, rope_t[spar])] + (t1a_gate if s < 2 else []),
                incs=[(s_ldc3[spar], 16)])
            tld_cq = p.step("sync", lambda E, s=s, spar=spar: E.dma_start(
                out=sinqs[spar][:], in_=dsinq[:, s * 256:(s + 1) * 256]),
                incs=[(s_ldc3[spar], 16)])
            qv = Qb[:].rearrange("p (h d) -> p h d", d=64)
            cq = cosqs[spar][:].rearrange("p (h d) -> p h d", d=16)
            sq = sinqs[spar][:].rearrange("p (h d) -> p h d", d=16)
            p.step("vector", lambda E, spar=spar: E.nop(),
                   waits=[(s_ldc3[spar], 32 * (s // 2 + 1))], incs=[(s_dve, 1)])
            t_rope = rope_steps(qv, cq, sq, tq_ev[1], rtmpq1, rtmpq2,
                                wait_sem=s_act)
            rope_t[spar] = t_rope
            for _ in range(5):
                next(qgen)
            for hp in range(8):
                tt = p.step("tensor", lambda E, hp=hp: E.transpose(
                    pTr2x[:, hp * 128:(hp + 1) * 128], Qb[:, hp * 128:(hp + 1) * 128],
                    idt[:]),
                    waits=([(s_dve, t_rope), (s_act, pTr_free)] if hp == 0 else ()),
                    incs=[(s_pe, 1)])
            pTr_free = p.step("scalar", lambda E, s=s: E.activation(
                QT[s][:], pTr2x[:, 0:1024], AF.Copy, bias=0.0, scale=1.0),
                waits=[(s_pe, tt)], incs=[(s_act, 1)])
        for _ in range(120):
            next(qgen)
        t_qt = p.now(s_act)
        t1b_gate = [(s_dve, p.now(s_dve)), (s_pe, p.now(s_pe)), (s_spx, 16 * 8),
                    (s_cag[0], 24 * 16), (s_cag[1], 24 * 16),
                    (s_act, p.now(s_act))]

        ps2.close()
        sw2.close()

        squant.close()
        st3 = ExitStack()
        Wo_b = [st3.enter_context(nc.sbuf_tensor(f"wob{c}", [128, 2048], BF16))
                for c in range(8)]
        wostage = st3.enter_context(nc.sbuf_tensor("wost", [128, 2048], F32))
        masks = st3.enter_context(nc.sbuf_tensor("masks", [128, MASK_COLS], BF16))
        scobufs = [st3.enter_context(nc.sbuf_tensor(f"scobuf{i}", [128, NKEY], F32))
                   for i in range(2)]
        probs2 = [st3.enter_context(nc.sbuf_tensor(f"probs{i}", [128, NKEY], BF16))
                  for i in range(2)]
        probsT = st3.enter_context(nc.sbuf_tensor("probsT", [128, 1152], BF16))
        AO = [st3.enter_context(nc.sbuf_tensor(f"ao{s}", [128, 1024], BF16))
              for s in range(8)]
        aoT = st3.enter_context(nc.sbuf_tensor("aoT", [128, 1024], BF16))

        p.step("gpsimd", lambda E: E.dma_start(out=masks[:], in_=dmask),
               waits=t1b_gate, incs=[(s_cmask, 16)], dma=True)
        last_conv = 0
        for c in range(8):
            tld = p.step("sync", lambda E, c=c: E.dma_start(
                out=wostage[:], in_=dWo[c * 128:(c + 1) * 128, :]),
                waits=[(s_act, last_conv)] + (t1b_gate if c == 0 else []),
                incs=[(s_lwo, 16)])
            last_conv = p.step("scalar", lambda E, c=c: E.activation(
                Wo_b[c][:], wostage[:], AF.Copy, bias=0.0, scale=1.0),
                waits=[(s_lwo, tld)], incs=[(s_act, 1)])
        t_wo = p.now(s_act)

        pbc_scope.close()
        ps3 = ExitStack()
        pSL2 = [ps3.enter_context(nc.psum_tensor(f"psl{i}", [128, 1024], F32))
                for i in range(2)]
        pMISC = ps3.enter_context(nc.psum_tensor("pmisc", [128, 512], F32))
        pT3 = ps3.enter_context(nc.psum_tensor("pT3", [128, 1024], BF16))
        pT3b = ps3.enter_context(nc.psum_tensor("pT3b", [128, 128], BF16))

        att = dict(psp_free=[0, 0], psl_free=[0, 0], sco_free=[0, 0],
                   probs_dve=[0, 0], pt3_free=0, ao_free=0)

        def att_scores(z):
            par, wp, wl, moff = z["par"], z["wp"], z["wl"], z["moff"]
            kvh, base, lhs = z["kvh"], z["base"], z["lhs"]
            scobuf = scobufs[par]
            w0 = [(s_dve, att["psp_free"][par])]
            if z["i"] < 2:
                w0 += [(s_act, t_qt), (s_cmask, 16), (s_dve, t_elsrep),
                       (s_ktd, 64)]
            tsp = p.step("tensor", lambda E, lhs=lhs, kvh=kvh, wp=wp, base=base, par=par: E.matmul(
                pMISC[:, par * 16:par * 16 + wp], lhs,
                KT[kvh][base:base + 64, 0:wp],
                start=True, stop=True),
                waits=w0, incs=[(s_pe, 1)])
            tsl = None
            if wl > 0:
                nmm = (wl + 511) // 512
                for m in range(nmm):
                    cw = min(512, wl - m * 512)
                    w1 = [(s_dve, att["psl_free"][par])] if m == 0 else ()
                    tsl = p.step("tensor", lambda E, lhs=lhs, kvh=kvh, m=m, cw=cw, base=base, par=par: E.matmul(
                        pSL2[par][:, m * 512:m * 512 + cw], lhs,
                        KT[kvh][base:base + 64, 16 + m * 512:16 + m * 512 + cw],
                        start=True, stop=True),
                        waits=w1, incs=[(s_pe, 1)])
            att["psp_free"][par] = p.step("vector", lambda E, wp=wp, moff=moff, scobuf=scobuf, par=par: E.scalar_tensor_tensor(
                scobuf[:, 0:wp], pMISC[:, par * 16:par * 16 + wp], 30000.0,
                masks[:, moff:moff + wp], ALU.add, ALU.mult),
                waits=[(s_pe, tsp), (s_act, att["sco_free"][par])],
                incs=[(s_dve, 1)])
            if wl > 0:
                att["psl_free"][par] = p.step("vector", lambda E, wp=wp, wl=wl, moff=moff, scobuf=scobuf, par=par: E.scalar_tensor_tensor(
                    scobuf[:, wp:wp + wl], pSL2[par][:, 0:wl], 30000.0,
                    masks[:, moff + wp:moff + wp + wl], ALU.add, ALU.mult),
                    waits=[(s_pe, tsl)], incs=[(s_dve, 1)])

        def att_softmax(z):
            par, W, h = z["par"], z["W"], z["h"]
            scobuf = scobufs[par]
            probs = probs2[par]
            tnm = p.step("vector", lambda E, W=W, scobuf=scobuf: E.reduce_max(
                sc_a[:, 0:1], scobuf[:, 0:W], AX.X, negate=True), incs=[(s_dve, 1)])
            p.step("vector", lambda E, h=h, par=par: E.tensor_tensor(
                sc_b[:, par:par + 1], sc_a[:, 0:1], els_rep[:, h:h + 1], ALU.mult),
                waits=[(s_dve, tnm)], incs=[(s_dve, 1)])
            tbq = dve_flush(1)
            z["tex"] = p.step("scalar", lambda E, h=h, W=W, par=par, probs=probs, scobuf=scobuf: E.activation(
                probs[:, 0:W], scobuf[:, 0:W], AF.Exp,
                bias=sc_b[:, par:par + 1], scale=els_rep[:, h:h + 1]),
                waits=[(s_dve, tbq), (s_dve, att["probs_dve"][par])],
                incs=[(s_act, 1)])
            att["sco_free"][par] = z["tex"]

        def att_sumrecip(z):
            par, W = z["par"], z["W"]
            sl = (z["i"] % 4) * 2
            probs = probs2[par]
            att["probs_dve"][par] = p.step("vector", lambda E, W=W, probs=probs, sl=sl: E.reduce_sum(
                sc_c[:, sl:sl + 1], probs[:, 0:W], AX.X),
                waits=[(s_act, z["tex"])], incs=[(s_dve, 1)])
            p.step("vector", lambda E, sl=sl: E.reciprocal(
                sc_c[:, sl + 1:sl + 2], sc_c[:, sl:sl + 1]),
                incs=[(s_dve, 1)])

        def att_tail1(z):
            wp, wl, nl, par = z["wp"], z["wl"], z["nl"], z["par"]
            probs = probs2[par]
            p.step("tensor", lambda E, wp=wp, probs=probs: E.transpose(
                pT3b[0:wp, 0:128], probs[:, 0:wp], idt[:]),
                waits=[(s_act, z["tex"]), (s_act, att["pt3_free"])],
                incs=[(s_pe, 1)])
            for m in range(nl):
                p.step("tensor", lambda E, wp=wp, m=m, probs=probs: E.transpose(
                    pT3[:, m * 128:(m + 1) * 128],
                    probs[:, wp + m * 128:wp + (m + 1) * 128], idt[:]),
                    incs=[(s_pe, 1)])
            fl = pe_flush()
            if nl > 0:
                p.step("scalar", lambda E, nl=nl: E.activation(
                    probsT[:, 0:nl * 128], pT3[:, 0:nl * 128], AF.Copy,
                    bias=0.0, scale=1.0),
                    waits=[(s_pe, fl)], incs=[(s_act, 1)])
            z["evac"] = p.step("scalar", lambda E, wp=wp: E.activation(
                probsT[0:wp, 1024:1152], pT3b[0:wp, 0:128], AF.Copy,
                bias=0.0, scale=1.0),
                waits=[(s_pe, fl)], incs=[(s_act, 1)])
            att["pt3_free"] = z["evac"]

        def att_tail2(z):
            s_, h, wp, wl, nl, kvh = z["s"], z["h"], z["wp"], z["wl"], z["nl"], z["kvh"]
            w2 = [(s_act, z["evac"]), (s_dve, att["ao_free"])]
            p.step("tensor", lambda E, wp=wp, kvh=kvh, wl=wl: E.matmul(
                pMISC[:, 64:128], probsT[0:wp, 1024:1152],
                pooled[0:wp, 256 + kvh * 64:256 + (kvh + 1) * 64],
                start=True, stop=(wl == 0)),
                waits=w2, incs=[(s_pe, 1)])
            for m in range(nl):
                p.step("tensor", lambda E, m=m, kvh=kvh, nl=nl: E.matmul(
                    pMISC[:, 64:128], probsT[:, m * 128:(m + 1) * 128],
                    KV[8 + m][:, 256 + kvh * 64:256 + (kvh + 1) * 64],
                    start=False, stop=(m == nl - 1)),
                    incs=[(s_pe, 1)])
            fl = pe_flush()
            sl = (z["i"] % 4) * 2
            att["ao_free"] = p.step("vector", lambda E, s_=s_, h=h, sl=sl: E.tensor_scalar(
                AO[s_][:, h * 64:(h + 1) * 64], pMISC[:, 64:128],
                sc_c[:, sl + 1:sl + 2], None, ALU.mult),
                waits=[(s_pe, fl)], incs=[(s_dve, 1)])

        Z = []
        for s_ in range(8):
            W = SLOT_W[s_]
            wp = min(W, 16)
            wl = W - wp
            nl = wl // 128
            moff = SLOT_OFF[s_]
            for h in range(H):
                i = len(Z)
                z = dict(i=i, s=s_, h=h, W=W, wp=wp, wl=wl, nl=nl, moff=moff,
                         par=i % 2, kvh=h // 4, base=64 * (h % 2))
                z["lhs"] = QT[s_][z["base"]:z["base"] + 64,
                                  (h // 2) * 128:(h // 2 + 1) * 128]
                Z.append(z)
                att_scores(z)
                if i >= 3:
                    att_tail2(Z[i - 3])
                if i >= 2:
                    att_tail1(Z[i - 2])
                att_softmax(z)
                if i >= 1:
                    att_sumrecip(Z[i - 1])
        N = len(Z)
        att_sumrecip(Z[N - 1])
        att_tail2(Z[N - 3])
        att_tail1(Z[N - 2])
        att_tail2(Z[N - 2])
        att_tail1(Z[N - 1])
        att_tail2(Z[N - 1])
        t_attn_done = p.now(s_dve)
        if DEBUG_TAPS:
            for s_ in range(8):
                p.step("sync", lambda E, s_=s_: E.dma_start(
                    out=d_ao[s_ * 128:(s_ + 1) * 128, :], in_=AO[s_][:]),
                    waits=[(s_dve, t_attn_done)], incs=[(s_o[1], 16)])
                p.step("sync", lambda E, s_=s_: E.dma_start(
                    out=d_qt[:, s_ * 1024:(s_ + 1) * 1024], in_=QT[s_][:]),
                    waits=[(s_dve, t_attn_done)], incs=[(s_o[1], 16)])
            for k_ in range(HKV):
                p.step("sync", lambda E, k_=k_: E.dma_start(
                    out=d_kt[:, k_ * NKEY:(k_ + 1) * NKEY], in_=KT[k_][:]),
                    waits=[(s_dve, t_attn_done)], incs=[(s_o[1], 16)])
            p.step("sync", lambda E: E.dma_start(out=d_pool, in_=pooled[:]),
                   waits=[(s_dve, t_attn_done)], incs=[(s_o[1], 16)])
        ps3.close()

        ps3b = ExitStack()
        pWo = ps3b.enter_context(nc.psum_tensor("pWo", [128, 2048], F32))
        pTwo = ps3b.enter_context(nc.psum_tensor("pTwo", [128, 1024], BF16))

        ptwo_free = 0
        pwo_free = [0] * 4
        x1_done = [0] * 8
        s1_done = [0] * 8
        for s in range(8):
            for hp in range(8):
                tt = p.step("tensor", lambda E, s=s, hp=hp: E.transpose(
                    pTwo[:, hp * 128:(hp + 1) * 128], AO[s][:, hp * 128:(hp + 1) * 128],
                    idt[:]),
                    waits=([(s_dve, t_attn_done), (s_act, ptwo_free)] if hp == 0 else ()),
                    incs=[(s_pe, 1)])
            ptwo_free = p.step("scalar", lambda E: E.activation(
                aoT[:], pTwo[:], AF.Copy, bias=0.0, scale=1.0),
                waits=[(s_pe, tt)], incs=[(s_act, 1)])
            for dmq in range(4):
                for c in range(8):
                    w = ([(s_act, ptwo_free), (s_dve, pwo_free[dmq]), (s_act, t_wo)]
                         if c == 0 else ())
                    tmm = p.step("tensor", lambda E, dmq=dmq, c=c: E.matmul(
                        pWo[:, dmq * 512:(dmq + 1) * 512], aoT[:, c * 128:(c + 1) * 128],
                        Wo_b[c][:, dmq * 512:(dmq + 1) * 512],
                        start=(c == 0), stop=(c == 7)),
                        waits=w, incs=[(s_pe, 1)])
                pwo_free[dmq] = p.step("vector", lambda E, s=s, dmq=dmq: E.scalar_tensor_tensor(
                    x_res[:, s * D + dmq * 512:s * D + (dmq + 1) * 512],
                    pWo[:, dmq * 512:(dmq + 1) * 512], 1.0,
                    x_res[:, s * D + dmq * 512:s * D + (dmq + 1) * 512],
                    ALU.mult, ALU.add),
                    waits=[(s_pe, tmm), (s_spx, 16 * 8)], incs=[(s_dve, 1)])
            xs = x_res[:, s * D:(s + 1) * D]
            for q4 in range(4):
                p.step("vector", lambda E, xs=xs, q4=q4: E.bn_stats(
                    bnln[:, q4 * 6:(q4 + 1) * 6], xs[:, q4 * 512:(q4 + 1) * 512]),
                    waits=[(s_dve, pwo_free[3])], incs=[(s_dve, 1)])
            p.step("vector", lambda E: E.bn_aggr(bnag[:], bnln[:, 0:24]),
                   incs=[(s_dve, 1)])
            p.step("vector", lambda E: E.tensor_copy(sc_a[:, 1:2], bnag[:, 0:1]),
                   incs=[(s_dve, 1)])
            p.step("vector", lambda E: E.tensor_scalar(
                sc_a[:, 4:5], bnag[:, 1:2], 1e-5, None, ALU.add), incs=[(s_dve, 1)])
            t_var = p.now(s_dve)
            p.step("scalar", lambda E: E.activation(
                sc_a[:, 5:6], sc_a[:, 4:5], AF.Ln, bias=0.0, scale=1.0),
                waits=[(s_dve, t_var)], incs=[(s_act, 1)])
            t_rst = p.step("scalar", lambda E: E.activation(
                sc_a[:, 6:7], sc_a[:, 5:6], AF.Exp, bias=0.0, scale=-0.5),
                incs=[(s_act, 1)])
            x1_done[s] = p.step("vector", lambda E, xs=xs: E.tensor_scalar(
                xs, xs, sc_a[:, 1:2], sc_a[:, 6:7], ALU.subtract, ALU.mult),
                waits=[(s_act, t_rst)], incs=[(s_dve, 1)])
            p.step("vector", lambda E, xs=xs: E.reduce_max(
                sc_a[:, 7:8], xs, AX.X, apply_absolute_value=True),
                waits=[(s_dve, x1_done[s])], incs=[(s_dve, 1)])
            p.step("vector", lambda E: E.tensor_scalar(
                sc_a[:, 7:8], sc_a[:, 7:8], 1e-5, None, ALU.max), incs=[(s_dve, 1)])
            p.step("vector", lambda E: E.reciprocal(sc_b[:, 0:1], sc_a[:, 7:8]),
                   incs=[(s_dve, 1)])
            p.step("vector", lambda E, s=s: E.tensor_scalar(
                st_s1[:, s:s + 1], sc_b[:, 0:1], 127.0, None, ALU.mult),
                incs=[(s_dve, 1)])
            s1_done[s] = dve_flush(2)
        ps3b.close()
        st3.close()

        st4 = ExitStack()
        t4f = st4.enter_context(nc.sbuf_tensor("t4f", [128, D], F32))
        t4i = st4.enter_context(nc.sbuf_tensor("t4i", [128, D], I32))
        t4b = st4.enter_context(nc.sbuf_tensor("t4b", [128, D], BF16))
        t4t = st4.enter_context(nc.sbuf_tensor("t4t", [128, D], BF16))
        ps4 = ExitStack()
        pT4 = ps4.enter_context(nc.psum_tensor("pT4", [128, 2048], BF16))

        t4t_free = 0
        for s in range(8):
            xs = x_res[:, s * D:(s + 1) * D]
            p.step("sync", lambda E, s=s, xs=xs: E.dma_start(
                out=x1_spill.ap()[s * 128:(s + 1) * 128, :], in_=xs),
                waits=[(s_dve, x1_done[s])], incs=[(s_sp1, 16)])
            if DEBUG_TAPS:
                p.step("sync", lambda E, s=s, xs=xs: E.dma_start(
                    out=d_x1[s * 128:(s + 1) * 128, :], in_=xs),
                    waits=[(s_dve, x1_done[s])], incs=[(s_o[0], 16)])
            tma = p.step("scalar", lambda E, s=s, xs=xs: E.activation(
                t4f[:], xs, AF.Copy, bias=MAGIC, scale=st_s1[:, s:s + 1]),
                waits=[(s_dve, s1_done[s]), (s_dve, t4t_free)], incs=[(s_act, 1)])
            p.step("vector", lambda E: E.tensor_scalar(
                t4i[:], t4f[:], -MAGIC, None, ALU.add),
                waits=[(s_act, tma)], incs=[(s_dve, 1)])
            tqb = p.step("vector", lambda E: E.tensor_copy(t4b[:], t4i[:]),
                         incs=[(s_dve, 1)])
            for c2 in range(16):
                tt = p.step("tensor", lambda E, c2=c2: E.transpose(
                    pT4[:, c2 * 128:(c2 + 1) * 128], t4b[:, c2 * 128:(c2 + 1) * 128],
                    idt[:]),
                    waits=([(s_dve, tqb)] if c2 == 0 else ()), incs=[(s_pe, 1)])
            t4t_free = p.step("vector", lambda E: E.tensor_copy(t4t[:], pT4[:]),
                              waits=[(s_pe, tt), (s_spq, p.now(s_spq))],
                              incs=[(s_dve, 1)])
            p.step("sync", lambda E, s=s: E.dma_start(
                out=xqt_spill.ap()[:, s * 2048:(s + 1) * 2048], in_=t4t[:]),
                waits=[(s_dve, t4t_free)], incs=[(s_spq, 16)])
        gate_w = [(s_sp1, p.now(s_sp1)), (s_spq, p.now(s_spq)),
                  (s_dve, p.now(s_dve)), (s_act, p.now(s_act)),
                  (s_pe, p.now(s_pe)), (s_pool, p.now(s_pool))]
        if DEBUG_TAPS:
            gate_w = gate_w + [(s_o[i], p.now(s_o[i])) for i in range(4)]
        ps4.close()
        st4.close()
        skv.close()
        sx.close()

        s2 = ExitStack()
        SLAB = [s2.enter_context(nc.sbuf_tensor(f"slab{hf}", [128, 4 * FF], BF16))
                for hf in range(2)]
        g_rep = s2.enter_context(nc.sbuf_tensor("g_rep", [128, FF], BF16))
        sgbuf = s2.enter_context(nc.sbuf_tensor("sgbuf", [128, 512], BF16))
        aqi = s2.enter_context(nc.sbuf_tensor("aqi", [128, 512], I32))
        aqbs = [s2.enter_context(nc.sbuf_tensor(f"aqb{i}", [128, 512], BF16))
                for i in range(2)]
        ps2c = ExitStack()
        pbc2 = ps2c.enter_context(nc.psum_tensor("pbc2", [128, 512], F32))
        sgr = ExitStack()
        gsub_sb = sgr.enter_context(nc.sbuf_tensor("gsub_sb", [1, FF], F32))
        p.step("sync", lambda E: E.dma_start(out=gsub_sb[:], in_=dgsub),
               waits=gate_w, incs=[(s_ldc4, 16)])
        for gblk in range(16):
            tt = p.step("tensor", lambda E, gblk=gblk: E.matmul(
                pbc2[:], ones_r[:], gsub_sb[:, gblk * 512:(gblk + 1) * 512],
                start=True, stop=True),
                waits=[(s_ldc4, 16)] + ([(s_dve, 0)] if gblk == 0 else
                                        [(s_dve, p.now(s_dve))]),
                incs=[(s_pe, 1)])
            tt = pe_flush()
            p.step("vector", lambda E, gblk=gblk: E.tensor_copy(
                g_rep[:, gblk * 512:(gblk + 1) * 512], pbc2[:]),
                waits=[(s_pe, tt)], incs=[(s_dve, 1)])
        t_grep = p.now(s_dve)
        g_gate = [(s_pe, p.now(s_pe)), (s_dve, t_grep)]
        sgr.close()

        s2a = ExitStack()
        XQT = s2a.enter_context(nc.sbuf_tensor("XQT", [128, 4 * 2048], BF16))
        gringM = s2a.enter_context(nc.sbuf_tensor("gringM", [128, 16 * 512], BF16))
        uringM = s2a.enter_context(nc.sbuf_tensor("uringM", [128, 16 * 512], BF16))
        rgbuf = s2a.enter_context(nc.sbuf_tensor("rgbuf", [128, 512], F32))
        rg2buf = s2a.enter_context(nc.sbuf_tensor("rg2buf", [128, 512], F32))

        ps2f = ExitStack()
        pTaq = ps2f.enter_context(nc.psum_tensor("pTaq", [128, 1024], BF16))
        pQ2 = ps2f.enter_context(nc.psum_tensor("pQ2", [128, 512], F32))
        psgu = ExitStack()
        pG = [psgu.enter_context(nc.psum_tensor(f"pG{i}", [128, 512], F32)) for i in range(2)]
        pU = [psgu.enter_context(nc.psum_tensor(f"pU{i}", [128, 512], F32)) for i in range(2)]

        gring_free = [0, 0]
        uring_free = [0, 0]
        pg_free = [0, 0]
        pu_free = [0, 0]
        xqt_free = 0
        prod_end = {}
        rowS_done = [0] * 8
        drow_done = [0] * 8
        aq_last_write = [0, 0]
        slab_prod = [[0] * 64 for _ in range(2)]
        rg_free = 0
        rg2_free = 0
        sg_free = 0
        tg_free = 0
        aqi_free = 0
        aqb_free = 0
        pq2_free = 0
        ptaq_free = 0

        qg_view = qgT.ap().rearrange("(r fb c p) n -> p fb r c n", r=8, fb=32, p=128)
        qu_view = quT.ap().rearrange("(r fb c p) n -> p fb r c n", r=8, fb=32, p=128)
        qd_view = qdT.ap().rearrange("(r d fl p) n -> r d fl p n", r=8, d=4, p=128)

        gring_free = 0
        uring_free = 0
        aq_state = dict(sg_free=0, pq_free=[0, 0], ptaq_free=[0, 0],
                        half_free=[0, 0])

        def aq_gen(hf):
            st = aq_state

            def stage_a(j):
                t, grp = j // 16, j % 16
                par = j % 2
                sg_i = hf * 4 + t
                slab_sl = SLAB[hf][:, t * FF + grp * 512:t * FF + (grp + 1) * 512]
                grs = g_rep[:, grp * 512:(grp + 1) * 512]
                pq = pQ2 if par == 0 else pbc2
                t_a1 = p.step("vector", lambda E, slab_sl=slab_sl, grs=grs: E.tensor_tensor(
                    sgbuf[:], slab_sl, grs, ALU.mult),
                    waits=[(s_dve, rowS_done[sg_i]), (s_act, st["sg_free"])] +
                          ([(s_o[2], tap_slab)] if tap_slab and hf == 0 else []),
                    incs=[(s_dve, 1)])
                t_a2 = p.step("scalar", lambda E, sg_i=sg_i, pq=pq: E.activation(
                    pq[:], sgbuf[:], AF.Copy, bias=MAGIC,
                    scale=st_rowS[:, sg_i:sg_i + 1]),
                    waits=[(s_dve, t_a1), (s_dve, st["pq_free"][par])],
                    incs=[(s_act, 1)])
                st["sg_free"] = t_a2
                t_a3 = p.step("vector", lambda E, pq=pq: E.tensor_scalar(
                    aqi[:], pq[:], -MAGIC, None, ALU.add),
                    waits=[(s_act, t_a2)], incs=[(s_dve, 1)])
                st["pq_free"][par] = t_a3
                t_a4 = p.step("vector", lambda E, par=par: E.tensor_copy(
                    aqbs[par][:], aqi[:]),
                    waits=[(s_pe, st["ptaq_free"][par])], incs=[(s_dve, 1)])
                return t_a4

            def stage_b(j, t_a4):
                t, grp = j // 16, j % 16
                par = j % 2
                slab_sl = SLAB[hf][:, t * FF + grp * 512:t * FF + (grp + 1) * 512]
                for c4 in range(4):
                    p.step("tensor", lambda E, c4=c4, par=par: E.transpose(
                        pTaq[:, par * 512 + c4 * 128:par * 512 + (c4 + 1) * 128],
                        aqbs[par][:, c4 * 128:(c4 + 1) * 128], idt[:]),
                        waits=([(s_dve, t_a4), (s_act, st["half_free"][par])]
                               if c4 == 0 else ()),
                        incs=[(s_pe, 1)])
                tt = pe_flush()
                st["ptaq_free"][par] = tt
                aq_last_write[hf] = p.step("scalar", lambda E, slab_sl=slab_sl, par=par: E.activation(
                    slab_sl, pTaq[:, par * 512:(par + 1) * 512], AF.Copy,
                    bias=0.0, scale=1.0),
                    waits=[(s_pe, tt)], incs=[(s_act, 1)])
                st["half_free"][par] = aq_last_write[hf]

            prev = None
            for j in range(64):
                if prev is not None:
                    stage_b(*prev)
                prev = (j, stage_a(j))
                yield
            stage_b(*prev)
            while True:
                yield

        def emit_gup(hf, driver=None):
            if DEBUG_TAPS and hf == 0:
                for r_ in range(8):
                    p.step("sync", lambda E, r_=r_: E.dma_start(
                        out=d_qg[r_ * 128:(r_ + 1) * 128, :],
                        in_=qgT.ap()[r_ * 8192:r_ * 8192 + 128, :]),
                        waits=[(s_ccg, 1), (s_ccf, 1)], incs=[(s_o[3], 16)])
                    p.step("sync", lambda E, r_=r_: E.dma_start(
                        out=d_qd[r_ * 128:(r_ + 1) * 128, :],
                        in_=qdT.ap()[r_ * 4096:r_ * 4096 + 128, :]),
                        waits=[(s_ccd, 1), (s_ccf, 1)], incs=[(s_o[3], 16)])
            xw = (gate_w + g_gate + [(s_pe, emit_gup.xqt_free)]) if hf == 0 \
                else [(s_pe, emit_gup.xqt_free)]
            t_xqt = p.step("sync", lambda E, hf=hf: E.dma_start(
                out=XQT[:], in_=xqt_spill.ap()[:, hf * 4 * 2048:(hf + 1) * 4 * 2048]),
                waits=xw, incs=[(s_xqt, 16)])
            for fbp in range(16):
                wg = [(s_pe, emit_gup.gring_free)]
                if hf == 0 and fbp == 0:
                    wg = wg + gate_w + g_gate + [(s_ccg, 1), (s_ccf, 1)]
                ld_g = 0
                for sub in range(2):
                    fb = fbp * 2 + sub
                    for clv in range(2):
                        ld_g = p.step("sync", lambda E, fb=fb, sub=sub, clv=clv: E.dma_start(
                            out=gringM[:].rearrange("p (r cl sx n) -> p r cl sx n",
                                                    r=8, cl=2, sx=2)[:, :, clv, sub, :],
                            in_=qg_view[:, fb, :, clv, :]),
                            waits=(wg if sub == 0 and clv == 0 else ()),
                            incs=[(s_lg[0], 16)])
                wu = [(s_pe, emit_gup.uring_free)]
                if hf == 0 and fbp == 0:
                    wu = wu + g_gate + [(s_ccu, 1), (s_ccf, 1)]
                ld_u = 0
                for sub in range(2):
                    fb = fbp * 2 + sub
                    for clv in range(2):
                        ld_u = p.step("sync", lambda E, fb=fb, sub=sub, clv=clv: E.dma_start(
                            out=uringM[:].rearrange("p (r cl sx n) -> p r cl sx n",
                                                    r=8, cl=2, sx=2)[:, :, clv, sub, :],
                            in_=qu_view[:, fb, :, clv, :]),
                            waits=(wu if sub == 0 and clv == 0 else ()),
                            incs=[(s_lu[0], 16)])
                for t in range(4):
                    tpar = t % 2
                    sg_i = hf * 4 + t
                    for c2 in range(16):
                        w = []
                        if c2 == 0:
                            w = [(s_lg[0], ld_g), (s_act, pg_free[tpar]),
                                 (s_xqt, t_xqt)]
                        tmm = p.step("tensor", lambda E, c2=c2, t=t, tpar=tpar: E.matmul(
                            pG[tpar][:],
                            XQT[:, t * 2048 + c2 * 128:t * 2048 + (c2 + 1) * 128],
                            gringM[:, c2 * 512:(c2 + 1) * 512],
                            start=(c2 == 0), stop=(c2 == 15)),
                            waits=w, incs=[(s_pe, 1)])
                    emit_gup.gring_free = tmm
                    for c2 in range(16):
                        w = []
                        if c2 == 0:
                            w = [(s_lu[0], ld_u), (s_dve, pu_free[tpar])]
                        tmu = p.step("tensor", lambda E, c2=c2, t=t, tpar=tpar: E.matmul(
                            pU[tpar][:],
                            XQT[:, t * 2048 + c2 * 128:t * 2048 + (c2 + 1) * 128],
                            uringM[:, c2 * 512:(c2 + 1) * 512],
                            start=(c2 == 0), stop=(c2 == 15)),
                            waits=w, incs=[(s_pe, 1)])
                    emit_gup.uring_free = tmu
                    emit_gup.xqt_free = tmu
                    slab_sl = SLAB[hf][:, t * FF + fbp * 512:t * FF + (fbp + 1) * 512]
                    grs = g_rep[:, fbp * 512:(fbp + 1) * 512]
                    pg_free[tpar] = p.step("scalar", lambda E, tpar=tpar: E.activation(
                        rgbuf[:], pG[tpar][:], AF.Relu, bias=0.0, scale=1.0),
                        waits=[(s_pe, tmm), (s_dve, emit_gup.rg_free)], incs=[(s_act, 1)])
                    t_p2 = p.step("scalar", lambda E: E.activation(
                        rg2buf[:], rgbuf[:], AF.Square, bias=0.0, scale=1.0),
                        waits=[(s_dve, emit_gup.rg2_free)], incs=[(s_act, 1)])
                    t_p3 = p.step("vector", lambda E, slab_sl=slab_sl, tpar=tpar: E.scalar_tensor_tensor(
                        slab_sl, pU[tpar][:], 1.0, rg2buf[:], ALU.mult, ALU.mult),
                        waits=[(s_pe, tmu), (s_act, t_p2)], incs=[(s_dve, 1)])
                    pu_free[tpar] = t_p3
                    emit_gup.rg2_free = t_p3
                    p.step("vector", lambda E, slab_sl=slab_sl, sg_i=sg_i, fbp=fbp: E.bn_stats(
                        gup_bn[:, (sg_i * 16 + fbp) * 6:(sg_i * 16 + fbp + 1) * 6],
                        slab_sl),
                        waits=[(s_dve, t_p3)], incs=[(s_dve, 1)])
                    t_p5 = p.step("vector", lambda E, slab_sl=slab_sl, grs=grs: E.tensor_tensor(
                        rgbuf[:], slab_sl, grs, ALU.mult),
                        waits=[(s_dve, t_p3), (s_dve, t_grep)],
                        incs=[(s_dve, 1)])
                    emit_gup.rg_free = p.step("vector", lambda E, sg_i=sg_i, fbp=fbp: E.reduce_max(
                        gup_mxg[:, sg_i * 16 + fbp:sg_i * 16 + fbp + 1], rgbuf[:],
                        AX.X, apply_absolute_value=True),
                        waits=[(s_dve, t_p5)], incs=[(s_dve, 1)])
                    if driver is not None:
                        next(driver)
            prod_end[hf] = dict(dve=p.now(s_dve), act=p.now(s_act),
                                pool=p.now(s_pool), pe=p.now(s_pe))
            tap = 0
            if DEBUG_TAPS and hf == 0:
                tap = p.step("sync", lambda E: E.dma_start(out=d_slab, in_=SLAB[0][:]),
                             waits=[(s_dve, prod_end[0]["dve"]),
                                    (s_pool, prod_end[0]["pool"]),
                                    (s_act, prod_end[0]["act"])],
                             incs=[(s_o[2], 16)])
            return tap

        emit_gup.gring_free = 0
        emit_gup.uring_free = 0
        emit_gup.xqt_free = 0
        emit_gup.rg_free = 0
        emit_gup.rg2_free = 0
        emit_gup.tg_free = 0

        def emit_stats(hf):
                for t in range(4):
                    sg_i = hf * 4 + t
                    w0 = [(s_act, prod_end[hf]["act"]), (s_pool, prod_end[hf]["pool"]),
                          (s_dve, prod_end[hf]["dve"])]
                    cb = sg_i * 12
                    p.step("vector", lambda E, sg_i=sg_i: E.bn_aggr(
                        bnag[:], gup_bn[:, sg_i * 96:(sg_i + 1) * 96]),
                        waits=w0, incs=[(s_dve, 1)])
                    p.step("vector", lambda E, cb=cb: E.tensor_tensor(
                        st_st[:, cb + 0:cb + 1], bnag[:, 0:1], bnag[:, 0:1], ALU.mult),
                        incs=[(s_dve, 1)])
                    p.step("vector", lambda E, cb=cb: E.tensor_tensor(
                        st_st[:, cb + 0:cb + 1], st_st[:, cb + 0:cb + 1], bnag[:, 1:2],
                        ALU.add), incs=[(s_dve, 1)])
                    p.step("vector", lambda E, sg_i=sg_i, cb=cb: E.reduce_max(
                        st_st[:, cb + 1:cb + 2], gup_mxg[:, sg_i * 16:(sg_i + 1) * 16],
                        AX.X), waits=w0, incs=[(s_dve, 1)])
                    p.step("vector", lambda E, sg_i=sg_i, cb=cb: E.reciprocal(
                        st_st[:, cb + 2:cb + 3], st_s1[:, sg_i:sg_i + 1]), incs=[(s_dve, 1)])
                    p.step("vector", lambda E, cb=cb: E.tensor_tensor(
                        st_st[:, cb + 3:cb + 4], st_st[:, cb + 2:cb + 3],
                        st_st[:, cb + 2:cb + 3], ALU.mult), incs=[(s_dve, 1)])
                    p.step("vector", lambda E, cb=cb: E.tensor_tensor(
                        st_st[:, cb + 3:cb + 4], st_st[:, cb + 3:cb + 4],
                        st_st[:, cb + 2:cb + 3], ALU.mult), incs=[(s_dve, 1)])
                    p.step("vector", lambda E, cb=cb: E.tensor_tensor(
                        sc_a[:, 4:5], mw_rep[:, 0:1], mw_rep[:, 0:1], ALU.mult),
                        incs=[(s_dve, 1)])
                    p.step("vector", lambda E, cb=cb: E.tensor_tensor(
                        sc_a[:, 4:5], sc_a[:, 4:5], mw_rep[:, 1:2], ALU.mult),
                        incs=[(s_dve, 1)])
                    p.step("vector", lambda E, cb=cb: E.tensor_tensor(
                        st_st[:, cb + 3:cb + 4], st_st[:, cb + 3:cb + 4], sc_a[:, 4:5],
                        ALU.mult), incs=[(s_dve, 1)])
                    p.step("vector", lambda E, cb=cb: E.tensor_tensor(
                        st_st[:, cb + 4:cb + 5], st_st[:, cb + 3:cb + 4],
                        st_st[:, cb + 3:cb + 4], ALU.mult), incs=[(s_dve, 1)])
                    p.step("vector", lambda E, cb=cb: E.tensor_tensor(
                        st_st[:, cb + 4:cb + 5], st_st[:, cb + 4:cb + 5],
                        st_st[:, cb + 0:cb + 1], ALU.mult), incs=[(s_dve, 1)])
                    p.step("vector", lambda E, cb=cb: E.tensor_scalar(
                        st_st[:, cb + 4:cb + 5], st_st[:, cb + 4:cb + 5], 1e-6, None,
                        ALU.add), incs=[(s_dve, 1)])
                    t_v = dve_flush(2)
                    p.step("scalar", lambda E, cb=cb: E.activation(
                        st_st[:, cb + 5:cb + 6], st_st[:, cb + 4:cb + 5], AF.Ln,
                        bias=0.0, scale=1.0),
                        waits=[(s_dve, t_v)], incs=[(s_act, 1)])
                    p.step("scalar", lambda E, cb=cb: E.activation(
                        st_st[:, cb + 6:cb + 7], st_st[:, cb + 5:cb + 6], AF.Exp,
                        bias=0.0, scale=-0.5), incs=[(s_act, 1)])
                    t_r = act_flush(1)
                    p.step("vector", lambda E, cb=cb: E.tensor_tensor(
                        st_st[:, cb + 7:cb + 8], st_st[:, cb + 1:cb + 2],
                        st_st[:, cb + 3:cb + 4], ALU.mult),
                        waits=[(s_act, t_r)], incs=[(s_dve, 1)])
                    p.step("vector", lambda E, cb=cb: E.tensor_tensor(
                        st_st[:, cb + 7:cb + 8], st_st[:, cb + 7:cb + 8],
                        st_st[:, cb + 6:cb + 7], ALU.mult), incs=[(s_dve, 1)])
                    p.step("vector", lambda E, cb=cb: E.tensor_scalar(
                        st_st[:, cb + 7:cb + 8], st_st[:, cb + 7:cb + 8], 1e-5, None,
                        ALU.max), incs=[(s_dve, 1)])
                    p.step("vector", lambda E, cb=cb: E.reciprocal(
                        st_st[:, cb + 8:cb + 9], st_st[:, cb + 7:cb + 8]),
                        incs=[(s_dve, 1)])
                    p.step("vector", lambda E, cb=cb: E.tensor_scalar(
                        st_st[:, cb + 9:cb + 10], st_st[:, cb + 8:cb + 9], 127.0, None,
                        ALU.mult), incs=[(s_dve, 1)])
                    p.step("vector", lambda E, cb=cb: E.tensor_tensor(
                        st_st[:, cb + 10:cb + 11], st_st[:, cb + 3:cb + 4],
                        st_st[:, cb + 6:cb + 7], ALU.mult), incs=[(s_dve, 1)])
                    p.step("vector", lambda E, sg_i=sg_i, cb=cb: E.tensor_tensor(
                        st_rowS[:, sg_i:sg_i + 1], st_st[:, cb + 10:cb + 11],
                        st_st[:, cb + 9:cb + 10], ALU.mult), incs=[(s_dve, 1)])
                    rowS_done[sg_i] = dve_flush(2)
                    p.step("vector", lambda E, cb=cb: E.tensor_tensor(
                        st_st[:, cb + 11:cb + 12], st_st[:, cb + 7:cb + 8],
                        mw_rep[:, 2:3], ALU.mult), incs=[(s_dve, 1)])
                    p.step("vector", lambda E, sg_i=sg_i, cb=cb: E.tensor_scalar(
                        st_drow[:, sg_i:sg_i + 1], st_st[:, cb + 11:cb + 12], 1.0 / 127.0,
                        None, ALU.mult), incs=[(s_dve, 1)])
                    drow_done[sg_i] = dve_flush(2)


        tap_slab = emit_gup(0)
        emit_stats(0)
        aq0 = aq_gen(0)
        emit_gup(1, driver=aq0)
        for _ in range(8):
            next(aq0)
        emit_stats(1)
        aq1 = aq_gen(1)
        if DEBUG_TAPS:
            p.step("sync", lambda E: E.dma_start(out=d_aq, in_=SLAB[0][:]),
                   waits=[(s_act, aq_last_write[0])], incs=[(s_o[2], 16)])
            p.step("sync", lambda E: E.dma_start(out=d_rs[:, 0:8], in_=st_rowS[:]),
                   waits=[(s_dve, rowS_done[7])], incs=[(s_o[2], 16)])
            p.step("sync", lambda E: E.dma_start(out=d_rs[:, 8:16], in_=st_drow[:]),
                   waits=[(s_dve, drow_done[7])], incs=[(s_o[2], 16)])
            p.step("sync", lambda E: E.dma_start(out=d_gup[:, 0:128], in_=gup_bn[:, 0:128]),
                   waits=[(s_dve, drow_done[7])], incs=[(s_o[2], 16)])
            p.step("sync", lambda E: E.dma_start(out=d_st[:, 0:96], in_=st_st[:]),
                   waits=[(s_dve, drow_done[7])], incs=[(s_o[2], 16)])
            p.step("sync", lambda E: E.dma_start(out=d_mw[:, 0:4], in_=mw_rep[:]),
                   waits=[(s_dve, drow_done[7])], incs=[(s_o[2], 16)])
            p.step("sync", lambda E: E.dma_start(out=d_mw[:, 4:8], in_=sw_rep[:]),
                   waits=[(s_dve, drow_done[7])], incs=[(s_o[2], 16)])
            p.step("sync", lambda E: E.dma_start(out=d_gup[:, 128:256], in_=gup_mxg[:]),
                   waits=[(s_dve, drow_done[7])], incs=[(s_o[2], 16)])

        s2a.close()
        psgu.close()
        s2b = ExitStack()
        y2 = s2b.enter_context(nc.sbuf_tensor("y2", [128, 4 * D], F32))
        wd_ring = [s2b.enter_context(nc.sbuf_tensor(f"wdr{i}", [128, 512], BF16))
                   for i in range(4)]
        x1st = [s2b.enter_context(nc.sbuf_tensor(f"x1st{i}", [128, 512], F32))
                for i in range(2)]
        psd = ExitStack()
        pD = psd.enter_context(nc.psum_tensor("pD", [128, 2048], F32))

        region_gate = [(s_pe, prod_end[1]["pe"]), (s_dve, prod_end[1]["dve"]),
                       (s_pool, prod_end[1]["pool"]), (s_act, prod_end[1]["act"])]
        wdr_free = [0, 0, 0, 0]
        pd_free = [0] * 4
        x1st_free = [0, 0]
        y2_free = [0] * 4
        for hf in range(2):
            if hf == 1:
                for _ in range(80):
                    next(aq1)
            for ds4 in range(4):
                for fc in range(64):
                    r, floc = fc // 8, fc % 8
                    ring = fc % 4
                    extra = []
                    if hf == 0 and ds4 == 0 and fc < 4:
                        extra = [(s_ccd, 1), (s_ccf, 1)] + region_gate
                    tldw = p.step("sync", lambda E, r=r, ds4=ds4, floc=floc, ring=ring: E.dma_start(
                        out=wd_ring[ring][:], in_=qd_view[r, ds4, floc, :, :]),
                        waits=[(s_pe, wdr_free[ring])] + extra,
                        incs=[(s_lwd[ring], 16)])
                    for t in range(4):
                        w = []
                        if t == 0:
                            w = [(s_lwd[ring], tldw)]
                            if fc == 0:
                                w += [(s_act, aq_last_write[hf])] + \
                                     [(s_dve, pd_free[q]) for q in range(4)]
                        tmm = p.step("tensor", lambda E, t=t, fc=fc, ring=ring, hf=hf: E.matmul(
                            pD[:, t * 512:(t + 1) * 512],
                            SLAB[hf][:, t * FF + fc * 128:t * FF + (fc + 1) * 128],
                            wd_ring[ring][:],
                            start=(fc == 0), stop=(fc == 63)),
                            waits=w, incs=[(s_pe, 1)])
                    wdr_free[ring] = tmm
                    if hf == 0 and fc % 4 == 3:
                        next(aq1)
                for t in range(4):
                    sg_i = hf * 4 + t
                    par = t % 2
                    tldx = p.step("sync", lambda E, sg_i=sg_i, ds4=ds4, par=par: E.dma_start(
                        out=x1st[par][:],
                        in_=x1_spill.ap()[sg_i * 128:(sg_i + 1) * 128,
                                          ds4 * 512:(ds4 + 1) * 512]),
                        waits=[(s_dve, x1st_free[par]), (s_sp1, 16 * 8)] +
                              (region_gate if hf == 0 and ds4 == 0 and t < 2 else []),
                        incs=[(s_lx[par], 16)])
                    pd_free[t] = p.step("vector", lambda E, t=t, ds4=ds4, sg_i=sg_i, par=par: E.scalar_tensor_tensor(
                        y2[:, t * D + ds4 * 512:t * D + (ds4 + 1) * 512],
                        pD[:, t * 512:(t + 1) * 512], st_drow[:, sg_i:sg_i + 1],
                        x1st[par][:], ALU.mult, ALU.add),
                        waits=[(s_pe, tmm), (s_lx[par], tldx), (s_dve, drow_done[sg_i]),
                               (s_o[t], y2_free[t])],
                        incs=[(s_dve, 1)])
                    x1st_free[par] = pd_free[t]

            tap_y2 = 0
            if DEBUG_TAPS and hf == 0:
                tap_y2 = p.step("sync", lambda E: E.dma_start(out=d_y2, in_=y2[:]),
                                waits=[(s_dve, pd_free[0]), (s_dve, pd_free[1]),
                                       (s_dve, pd_free[2]), (s_dve, pd_free[3])],
                                incs=[(s_o[2], 16)])
            for t in range(4):
                sg_i = hf * 4 + t
                ys = y2[:, t * D:(t + 1) * D]
                for q4 in range(4):
                    p.step("vector", lambda E, ys=ys, q4=q4: E.bn_stats(
                        bnln[:, q4 * 6:(q4 + 1) * 6], ys[:, q4 * 512:(q4 + 1) * 512]),
                        waits=[(s_dve, pd_free[t])] +
                              ([(s_o[2], tap_y2)] if tap_y2 and q4 == 0 else []),
                        incs=[(s_dve, 1)])
                p.step("vector", lambda E: E.bn_aggr(bnag[:], bnln[:, 0:24]),
                       incs=[(s_dve, 1)])
                p.step("vector", lambda E: E.tensor_copy(sc_a[:, 1:2], bnag[:, 0:1]),
                       incs=[(s_dve, 1)])
                p.step("vector", lambda E: E.tensor_scalar(
                    sc_a[:, 4:5], bnag[:, 1:2], 1e-5, None, ALU.add), incs=[(s_dve, 1)])
                t_v = dve_flush(2)
                p.step("scalar", lambda E: E.activation(
                    sc_a[:, 5:6], sc_a[:, 4:5], AF.Ln, bias=0.0, scale=1.0),
                    waits=[(s_dve, t_v)], incs=[(s_act, 1)])
                p.step("scalar", lambda E: E.activation(
                    sc_a[:, 6:7], sc_a[:, 5:6], AF.Exp, bias=0.0, scale=-0.5),
                    incs=[(s_act, 1)])
                t_r = act_flush(1)
                t_ln = p.step("vector", lambda E, ys=ys: E.tensor_scalar(
                    ys, ys, sc_a[:, 1:2], sc_a[:, 6:7], ALU.subtract, ALU.mult),
                    waits=[(s_act, t_r)], incs=[(s_dve, 1)])
                y2_free[t] = p.step("sync", lambda E, sg_i=sg_i, ys=ys, t=t: E.dma_start(
                    out=dout[sg_i * 128:(sg_i + 1) * 128, :], in_=ys),
                    waits=[(s_dve, t_ln)], incs=[(s_o[t], 16)])

        p.step("sync", lambda E: E.nop(),
               waits=[(s_o[t], p.now(s_o[t])) for t in range(4)])

        with nc.Block() as block:
            p.emit_all(block)

        psd.close()
        ps2f.close()
        s2b.close()
        s2.close()
        ps2c.close()
        pfl_scope.close()

    return nc



def _rope_tables():
    a32 = np.arange(0, 32, 2).astype(np.float32)
    e_ = a32 / np.float32(32.0)
    base_pow = np.float32(10000.0) ** e_
    inv = (np.float32(1.0) / base_pow).astype(np.float32)
    pos = np.arange(T, dtype=np.float32)
    ang = pos[:, None] * inv[None, :]
    return np.cos(ang).astype(np.float32), np.sin(ang).astype(np.float32)


def _build_masks(chunks):
    m = np.zeros((128, MASK_COLS), np.float32)
    kpos = np.concatenate([np.arange(NPOOL) * 64 + 63, np.arange(1024, 2048)])
    for s, j in enumerate(chunks):
        W = SLOT_W[s]
        q = j * 128 + np.arange(128)
        mm = (kpos[None, :W] <= q[:, None]).astype(np.float32)
        m[:, SLOT_OFF[s]:SLOT_OFF[s] + W] = mm
    return m.astype(ml_dtypes.bfloat16)


def _interleave_qk(w_sem, w_geo, nh):
    out = np.empty((D, nh * 64), np.float32)
    for h in range(nh):
        out[:, h * 64:h * 64 + 32] = w_sem[:, h * 32:(h + 1) * 32]
        out[:, h * 64 + 32:h * 64 + 64] = w_geo[:, h * 32:(h + 1) * 32]
    return out


_PROG_CACHE = {}


def kernel(L0_attn_x, L0_attn_Wq_sem, L0_attn_Wk_sem, L0_attn_Wq_geo, L0_attn_Wk_geo,
           L0_attn_Wv, L0_attn_Wo, L0_attn_logit_scale, L0_ffn_w_gate, L0_ffn_w_up,
           L0_ffn_w_down, L0_ffn_subln_weight):
    x = np.ascontiguousarray(np.asarray(L0_attn_x, np.float32))
    wq = _interleave_qk(np.asarray(L0_attn_Wq_sem, np.float32),
                        np.asarray(L0_attn_Wq_geo, np.float32), H)
    wkv = np.concatenate([
        _interleave_qk(np.asarray(L0_attn_Wk_sem, np.float32),
                       np.asarray(L0_attn_Wk_geo, np.float32), HKV),
        np.asarray(L0_attn_Wv, np.float32)], axis=1)
    wo = np.ascontiguousarray(np.asarray(L0_attn_Wo, np.float32))
    ls = np.asarray(L0_attn_logit_scale, np.float32).reshape(1, H)
    gsub = np.asarray(L0_ffn_subln_weight, np.float32).reshape(1, FF)
    wgT = np.ascontiguousarray(np.asarray(L0_ffn_w_gate, np.float32).T)
    wuT = np.ascontiguousarray(np.asarray(L0_ffn_w_up, np.float32).T)
    wdT = np.ascontiguousarray(np.asarray(L0_ffn_w_down, np.float32).T)

    cos, sin = _rope_tables()
    ident = np.eye(128, dtype=ml_dtypes.bfloat16)
    pmat = np.zeros((1024, NPOOL), np.float32)
    for t in range(1024):
        pmat[t, t // 64] = 1.0 / 64.0
    pmat = pmat.astype(ml_dtypes.bfloat16)
    consts = np.zeros((128, 16), np.float32)
    cvals = [MAGIC, 1.0, 0.0, 1e-5, 1e-6, -0.5, 0.5, 127.0, -MAGIC,
             1.0 / D, 30000.0, 1.49, -1.49, 1.0 / FF, 2.0, -1.0]
    for j, v in enumerate(cvals):
        consts[:, j] = v
    ones_row = np.ones((1, 128), np.float32)
    cosk = np.zeros((128, 16 * 64), np.float32)
    sink = np.zeros((128, 16 * 64), np.float32)
    for c in range(16):
        blk_c = np.tile(cos[c * 128:(c + 1) * 128, :], (1, HKV))
        blk_s = np.tile(sin[c * 128:(c + 1) * 128, :], (1, HKV))
        cosk[:, c * 64:(c + 1) * 64] = blk_c
        sink[:, c * 64:(c + 1) * 64] = blk_s

    in_maps = []
    core_chunks = []
    for core in range(8):
        b = core // 2
        chunks = SET_A if core % 2 == 0 else SET_B
        core_chunks.append(chunks)
        xb_ = x[b]
        x_own = np.concatenate([xb_[j * 128:(j + 1) * 128] for j in chunks], axis=0)
        cosq = np.zeros((128, 8 * 256), np.float32)
        sinq = np.zeros((128, 8 * 256), np.float32)
        for s, j in enumerate(chunks):
            cosq[:, s * 256:(s + 1) * 256] = np.tile(cos[j * 128:(j + 1) * 128], (1, H))
            sinq[:, s * 256:(s + 1) * 256] = np.tile(sin[j * 128:(j + 1) * 128], (1, H))
        in_maps.append({
            "x_full": np.ascontiguousarray(xb_),
            "x_own": np.ascontiguousarray(x_own),
            "wkv": wkv, "wq": wq, "wo": wo, "ls": ls, "gsub": gsub,
            "d_cosq": cosq, "d_sinq": sinq, "d_cosk": cosk, "d_sink": sink,
            "d_masks": _build_masks(chunks),
            "d_ident": ident, "d_pmat": pmat, "d_consts": consts, "ones_row": ones_row,
            "shard_g": np.ascontiguousarray(wgT[core * 256:(core + 1) * 256]),
            "shard_u": np.ascontiguousarray(wuT[core * 256:(core + 1) * 256]),
            "shard_d": np.ascontiguousarray(wdT[core * 1024:(core + 1) * 1024]),
        })

    if "nc" not in _PROG_CACHE:
        _PROG_CACHE["nc"] = build_program()
    nc = _PROG_CACHE["nc"]
    res = run_bass_kernel_spmd(nc, in_maps, core_ids=list(range(8)))

    out = np.empty((B, T, D), np.float32)
    for core in range(8):
        b = core // 2
        y = np.asarray(res.results[core]["y"])
        for s, j in enumerate(core_chunks[core]):
            out[b, j * 128:(j + 1) * 128, :] = y[s * 128:(s + 1) * 128]
    return out


if __name__ == "__main__":
    nc = build_program()
    n = sum(1 for _ in nc.all_instructions())
    print("built ok, instructions:", n)


# version 27 (final)
# speedup vs baseline: 1.0688x; 1.0263x over previous
import sys
sys.path.insert(0, "/opt/trn_rl_repo")
import math
from contextlib import ExitStack

import numpy as np
import ml_dtypes

import concourse.bass as bass
import concourse.mybir as mybir
from concourse.bass_utils import run_bass_kernel_spmd

F32 = mybir.dt.float32
BF16 = mybir.dt.bfloat16
I32 = mybir.dt.int32
AF = mybir.ActivationFunctionType
ALU = mybir.AluOpType
AX = mybir.AxisListType

MAGIC = 1.5 * 2.0 ** 23
ISQ32 = 1.0 / math.sqrt(32.0)
DEBUG_TAPS = False

B, T, D = 4, 2048, 2048
H, HKV = 16, 4
NPOOL, NKEY = 16, 1040
FF = 8192

SET_A = [0, 14, 11, 10, 7, 6, 3, 2]
SET_B = [15, 13, 12, 9, 8, 5, 4, 1]
SLOT_W = [1040, 912, 656, 400, 144, 14, 10, 6]
MASK_COLS = sum(SLOT_W)
SLOT_OFF = [0]
for _w in SLOT_W:
    SLOT_OFF.append(SLOT_OFF[-1] + _w)

ENGINES = ("sync", "vector", "scalar", "tensor", "gpsimd")


class _Sem:
    __slots__ = ("name", "h")

    def __init__(self, name, h):
        self.name, self.h = name, h


class Plan:
    SELF_SYNC = ("vector", "scalar", "gpsimd")

    def __init__(self, nc, sem_stack):
        self.nc = nc
        self.sem_stack = sem_stack
        self.steps = {e: [] for e in ENGINES}
        self.counts = {}
        self.self_sems = {}

    def sem(self, name):
        h = self.sem_stack.enter_context(self.nc.semaphore(name))
        self.counts[name] = 0
        return _Sem(name, h)

    def set_self_sems(self, mapping):
        self.self_sems = mapping

    def step(self, eng, emit, waits=(), incs=(), dma=False):
        incs = list(incs)
        n_caller = len(incs)
        if any(n >= 16 for _s, n in incs):
            dma = True
        if eng in self.SELF_SYNC and not dma:
            ssem = self.self_sems[eng]
            if not any(s is ssem for s, n in incs):
                incs.append((ssem, 1))
            waits = [w for w in waits if w[0] is not ssem]
        tickets = []
        for s, n in incs:
            self.counts[s.name] += n
            tickets.append(self.counts[s.name])
        self.steps[eng].append((emit, tuple(waits), tuple(incs)))
        if n_caller <= 1:
            return tickets[0] if tickets else 0
        return tickets[:n_caller]

    def now(self, sem):
        return self.counts[sem.name]

    def emit_all(self, block):
        def run(eng_name, handle):
            ssem = self.self_sems.get(eng_name)
            running = 0
            for emit, waits, incs in self.steps[eng_name]:
                if ssem is not None and running > 0:
                    handle.wait_ge(ssem.h, running)
                for s, v in waits:
                    if v > 0:
                        handle.wait_ge(s.h, v)
                inst = emit(handle)
                for s, n in incs:
                    inst.then_inc(s.h, n)
                    if s is ssem:
                        running += n

        @block.sync
        def _(e):
            run("sync", e)

        @block.vector
        def _(e):
            run("vector", e)

        @block.scalar
        def _(e):
            run("scalar", e)

        @block.tensor
        def _(e):
            run("tensor", e)

        @block.gpsimd
        def _(e):
            run("gpsimd", e)


def build_program():
    nc = bass.Bass("TRN2", target_bir_lowering=False, debug=False)

    dxf = nc.dram_tensor("x_full", [T, D], F32, kind="ExternalInput").ap()
    dxo = nc.dram_tensor("x_own", [1024, D], F32, kind="ExternalInput").ap()
    dWkv = nc.dram_tensor("wkv", [D, 512], F32, kind="ExternalInput").ap()
    dWq = nc.dram_tensor("wq", [D, 1024], F32, kind="ExternalInput").ap()
    dWo = nc.dram_tensor("wo", [1024, D], F32, kind="ExternalInput").ap()
    dls = nc.dram_tensor("ls", [1, H], F32, kind="ExternalInput").ap()
    dgsub = nc.dram_tensor("gsub", [1, FF], F32, kind="ExternalInput").ap()
    dcosq = nc.dram_tensor("d_cosq", [128, 8 * 256], F32, kind="ExternalInput").ap()
    dsinq = nc.dram_tensor("d_sinq", [128, 8 * 256], F32, kind="ExternalInput").ap()
    dcosk = nc.dram_tensor("d_cosk", [128, 16 * 64], F32, kind="ExternalInput").ap()
    dsink = nc.dram_tensor("d_sink", [128, 16 * 64], F32, kind="ExternalInput").ap()
    dmask = nc.dram_tensor("d_masks", [128, MASK_COLS], BF16, kind="ExternalInput").ap()
    dident = nc.dram_tensor("d_ident", [128, 128], BF16, kind="ExternalInput").ap()
    dpmat = nc.dram_tensor("d_pmat", [1024, NPOOL], BF16, kind="ExternalInput").ap()
    dconsts = nc.dram_tensor("d_consts", [128, 16], F32, kind="ExternalInput").ap()
    dones = nc.dram_tensor("ones_row", [1, 128], F32, kind="ExternalInput").ap()
    dshg = nc.dram_tensor("shard_g", [256, FF], F32, kind="ExternalInput").ap()
    dshu = nc.dram_tensor("shard_u", [256, FF], F32, kind="ExternalInput").ap()
    dshd = nc.dram_tensor("shard_d", [1024, D], F32, kind="ExternalInput").ap()

    dout = nc.dram_tensor("y", [1024, D], F32, kind="ExternalOutput").ap()
    if DEBUG_TAPS:
        d_x1 = nc.dram_tensor("dbg_x1", [1024, D], F32, kind="ExternalOutput").ap()
        d_ao = nc.dram_tensor("dbg_ao", [1024, 1024], BF16, kind="ExternalOutput").ap()
        d_qt = nc.dram_tensor("dbg_qt", [128, 8 * 1024], BF16, kind="ExternalOutput").ap()
        d_kt = nc.dram_tensor("dbg_kt", [128, 4 * NKEY], BF16, kind="ExternalOutput").ap()
        d_pool = nc.dram_tensor("dbg_pool", [NPOOL, 512], BF16, kind="ExternalOutput").ap()
        d_slab = nc.dram_tensor("dbg_slab", [128, 4 * FF], BF16, kind="ExternalOutput").ap()
        d_aq = nc.dram_tensor("dbg_aq", [128, 4 * FF], BF16, kind="ExternalOutput").ap()
        d_y2 = nc.dram_tensor("dbg_y2", [128, 4 * D], F32, kind="ExternalOutput").ap()
        d_rs = nc.dram_tensor("dbg_rs", [128, 16], F32, kind="ExternalOutput").ap()
        d_gup = nc.dram_tensor("dbg_gup", [128, 256], F32, kind="ExternalOutput").ap()
        d_qg = nc.dram_tensor("dbg_qg", [1024, 256], BF16, kind="ExternalOutput").ap()
        d_st = nc.dram_tensor("dbg_st", [128, 96], F32, kind="ExternalOutput").ap()
        d_mw = nc.dram_tensor("dbg_mw", [128, 8], F32, kind="ExternalOutput").ap()
        d_qd = nc.dram_tensor("dbg_qd", [1024, 512], BF16, kind="ExternalOutput").ap()

    ag_g_in = nc.dram_tensor("ag_g_in", [8192, 256], BF16)
    ag_u_in = nc.dram_tensor("ag_u_in", [8192, 256], BF16)
    ag_d_in = nc.dram_tensor("ag_d_in", [4096, 512], BF16)
    qgT = nc.dram_tensor("qgT", [65536, 256], BF16, addr_space="Shared")
    quT = nc.dram_tensor("quT", [65536, 256], BF16, addr_space="Shared")
    qdT = nc.dram_tensor("qdT", [32768, 512], BF16, addr_space="Shared")
    ar_in = nc.dram_tensor("ar_in", [1, 3], F32)
    ar_out = nc.dram_tensor("ar_out", [1, 3], F32, addr_space="Shared")
    ar_out2 = nc.dram_tensor("ar_out2", [1, 3], F32, addr_space="Shared")
    x1_spill = nc.dram_tensor("x1_spill", [1024, D], F32)
    xqt_spill = nc.dram_tensor("xqt_spill", [128, 8 * 2048], BF16)

    sem_stack = ExitStack()
    p = Plan(nc, sem_stack)
    s_pe = p.sem("pe")
    s_dve = p.sem("dve")
    s_act = p.sem("act")
    s_pool = p.sem("pool")
    p.set_self_sems({"vector": s_dve, "scalar": s_act, "gpsimd": s_pool})
    s_ldc1 = p.sem("ldc1")
    s_ldc2 = p.sem("ldc2")
    s_ldc3 = [p.sem("ldc3a"), p.sem("ldc3b")]
    s_ldc4 = p.sem("ldc4")
    s_ldxa = [p.sem("ldxa0"), p.sem("ldxa1")]
    s_ldxb = p.sem("ldxb")
    s_ldwa = [p.sem("ldwa0"), p.sem("ldwa1")]
    s_ldwb = [p.sem("ldwb0"), p.sem("ldwb1")]
    s_lwo = p.sem("lwo")
    s_ktd = p.sem("ktd")
    s_ldsa = [p.sem("ldsa0"), p.sem("ldsa1")]
    s_ldsb = p.sem("ldsb")
    s_car = p.sem("car")
    s_cswm = p.sem("cswm")
    s_cmask = p.sem("cmask")
    s_cag = [p.sem("cag0"), p.sem("cag1")]
    s_ccar = p.sem("ccar")
    s_ccg = p.sem("ccg")
    s_ccu = p.sem("ccu")
    s_ccd = p.sem("ccd")
    s_ccf = p.sem("ccf")
    s_spx = p.sem("spx")
    s_sp1 = p.sem("sp1")
    s_spq = p.sem("spq")
    s_xqt = p.sem("xqt")
    s_lg = [p.sem("lg0"), p.sem("lg1")]
    s_lu = [p.sem("lu0"), p.sem("lu1")]
    s_lwd = [p.sem(f"lwd{i}") for i in range(4)]
    s_lx = [p.sem("lx0"), p.sem("lx1")]
    s_o = [p.sem(f"out{i}") for i in range(4)]

    CONST_VALS = [MAGIC, 1.0, 0.0, 1e-5, 1e-6, -0.5, 0.5, 127.0, -MAGIC,
                  1.0 / D, 30000.0, 1.49, -1.49, 1.0 / FF, 2.0, -1.0]

    ctx = ExitStack()
    with ctx, sem_stack:
        e = ctx.enter_context

        consts = e(nc.sbuf_tensor("consts", [128, 16], F32))
        for j, v in enumerate(CONST_VALS):
            nc.const_aps.register(F32, float(v), consts[:, j:j + 1])
        idt = e(nc.sbuf_tensor("idt", [128, 128], BF16))
        ones_r = e(nc.sbuf_tensor("ones_r", [1, 128], F32))
        ones_c = e(nc.sbuf_tensor("ones_c", [128, 1], F32))
        els_raw = e(nc.sbuf_tensor("els_raw", [1, H], F32))
        els_rep = e(nc.sbuf_tensor("els_rep", [128, H], F32))
        swm = e(nc.sbuf_tensor("swm", [1, 8], F32))
        shsum = e(nc.sbuf_tensor("shsum", [3, 1], F32))
        sw_rep = e(nc.sbuf_tensor("sw_rep", [128, 4], F32))
        mw_rep = e(nc.sbuf_tensor("mw_rep", [128, 4], F32))
        pooled = e(nc.sbuf_tensor("pooled", [NPOOL, 512], BF16))
        partials = e(nc.sbuf_tensor("partials", [128, 52], F32))
        st_s1 = e(nc.sbuf_tensor("st_s1", [128, 8], F32))
        st_rowS = e(nc.sbuf_tensor("st_rowS", [128, 8], F32))
        st_drow = e(nc.sbuf_tensor("st_drow", [128, 8], F32))
        gup_bn = e(nc.sbuf_tensor("gup_bn", [128, 8 * 16 * 6], F32))
        bnln = e(nc.sbuf_tensor("bnln", [128, 24], F32))
        st_st = e(nc.sbuf_tensor("st_st", [128, 96], F32))
        bnag = e(nc.sbuf_tensor("bnag", [128, 2], F32))
        gup_mxg = e(nc.sbuf_tensor("gup_mxg", [128, 8 * 16], F32))
        sc_a = e(nc.sbuf_tensor("sc_a", [128, 8], F32))
        sc_b = e(nc.sbuf_tensor("sc_b", [128, 8], F32))
        sc_c = e(nc.sbuf_tensor("sc_c", [128, 8], F32))
        flb = e(nc.sbuf_tensor("flb", [128, 4], F32))
        flb_init = [0]

        def dve_flush(n=1):
            t = 0
            for _ in range(n):
                t = p.step("vector", lambda E: E.tensor_copy(flb[:, 1:2], flb[:, 0:1]),
                           incs=[(s_dve, 1)])
            return t

        def act_flush(n=1):
            t = 0
            for _ in range(n):
                t = p.step("scalar", lambda E: E.activation(
                    flb[:, 3:4], flb[:, 2:3], AF.Copy, bias=0.0, scale=1.0),
                    waits=[(s_dve, flb_init[0])], incs=[(s_act, 1)])
            return t

        pfl_scope = ExitStack()
        pFL = pfl_scope.enter_context(nc.psum_tensor("pFL", [128, 4], F32))
        pbc_scope = ExitStack()
        pbc = pbc_scope.enter_context(nc.psum_tensor("pbc", [128, 512], F32))

        def pe_flush():
            return p.step("tensor", lambda E: E.matmul(
                pFL[0:1, 0:1], ones_r[0:1, 0:1], ones_r[0:1, 0:1],
                start=True, stop=True), incs=[(s_pe, 1)])

        for dst, src in ((consts[:], dconsts), (idt[:], dident), (ones_r[:], dones),
                         (els_raw[:], dls)):
            p.step("sync", lambda E, dst=dst, src=src: E.dma_start(out=dst, in_=src),
                   incs=[(s_ldc1, 16)])
        t_ldc4 = p.now(s_ldc1)
        p.step("vector", lambda E: E.memset(ones_c[:], 1.0), incs=[(s_dve, 1)])
        t_onec = p.now(s_dve)
        flb_init[0] = p.step("vector", lambda E: E.memset(flb[:], 0.0),
                             incs=[(s_dve, 1)])

        p.step("scalar", lambda E: E.activation(els_raw[:], els_raw[:], AF.Exp,
                                                bias=0.0, scale=1.0),
               waits=[(s_ldc1, t_ldc4)], incs=[(s_act, 1)])
        p.step("vector", lambda E: E.tensor_scalar(els_raw[:], els_raw[:], ISQ32, None,
                                                   ALU.mult),
               waits=[(s_act, p.now(s_act))], incs=[(s_dve, 1)])
        dve_flush(2)
        tt = p.step("tensor", lambda E: E.matmul(pbc[:, 0:H], ones_r[:], els_raw[:],
                                                 start=True, stop=True),
                    waits=[(s_dve, p.now(s_dve))], incs=[(s_pe, 1)])
        tt = pe_flush()
        tt = pe_flush()
        t_elsrep = p.step("vector", lambda E: E.tensor_copy(els_rep[:], pbc[:, 0:H]),
                          waits=[(s_pe, tt)], incs=[(s_dve, 1)])

        sx = ExitStack()
        x_res = sx.enter_context(nc.sbuf_tensor("x_res", [128, 8 * D], F32))

        skv = ExitStack()
        KV = [skv.enter_context(nc.sbuf_tensor(f"kv{c}", [128, 512], BF16)) for c in range(16)]
        KT = [skv.enter_context(nc.sbuf_tensor(f"kt{k}", [128, NKEY], BF16)) for k in range(HKV)]
        QT = [skv.enter_context(nc.sbuf_tensor(f"qt{s}", [128, 1024], BF16)) for s in range(8)]
        squant = ExitStack()
        shstage = [squant.enter_context(nc.sbuf_tensor(f"shst{i}", [128, 1024], F32)) for i in range(2)]
        shq1s = [squant.enter_context(nc.sbuf_tensor(f"shq1{i}", [128, 1024], F32))
                 for i in range(2)]
        shi = squant.enter_context(nc.sbuf_tensor("shi", [128, 1024], I32))
        shbs = [squant.enter_context(nc.sbuf_tensor(f"shb{i}", [128, 1024], BF16))
                for i in range(2)]

        shard_tiles = []
        for widx, dsh in enumerate((dshg, dshu, dshd)):
            v = dsh.rearrange("(r p) f -> r p f", p=128)
            nr = 2 if widx < 2 else 8
            ncb = 8 if widx < 2 else 2
            for r in range(nr):
                for cb in range(ncb):
                    shard_tiles.append((v[r, :, cb * 1024:(cb + 1) * 1024], widx, r, cb))

        def quant_gen():
            last_red = {0: 0, 1: 0}
            for i, (ap_, widx, r, cb) in enumerate(shard_tiles):
                par = i % 2
                tld = p.step("gpsimd", lambda E, ap_=ap_, par=par: E.dma_start(
                    out=shstage[par][:], in_=ap_),
                    waits=[(s_dve, last_red[par])], incs=[(s_ldsa[par], 16)])
                last_red[par] = p.step(
                    "vector", lambda E, i=i, par=par: E.reduce_sum(
                        partials[:, i:i + 1], shstage[par][:], AX.X,
                        apply_absolute_value=True),
                    waits=[(s_ldsa[par], tld)], incs=[(s_dve, 1)], dma=True)
                yield
            for widx in range(3):
                lo, hi = widx * 16, widx * 16 + 16
                p.step("vector", lambda E, widx=widx, lo=lo, hi=hi: E.reduce_sum(
                    sc_a[:, widx:widx + 1], partials[:, lo:hi], AX.X), incs=[(s_dve, 1)])
            t_pred = dve_flush(2)
            tt = p.step("tensor", lambda E: E.matmul(pbc[0:3, 16:17], sc_a[:, 0:3], ones_c[:],
                                                     start=True, stop=True),
                        waits=[(s_dve, t_pred), (s_dve, t_onec)], incs=[(s_pe, 1)])
            tt = pe_flush()
            tt = pe_flush()
            t_shsum = p.step("vector", lambda E: E.tensor_copy(shsum[:], pbc[0:3, 16:17]),
                             waits=[(s_pe, tt)], incs=[(s_dve, 1)])
            t_shsum = dve_flush(2)
            p.step("gpsimd", lambda E: E.dma_start(
                out=ar_in.ap().rearrange("a b -> b a"), in_=shsum[:]),
                waits=[(s_dve, t_shsum)], incs=[(s_car, 16)], dma=True)
            p.step("gpsimd", lambda E: E.collective_compute(
                "AllReduce", ALU.add, replica_groups=[list(range(8))],
                ins=[ar_in.ap().opt()], outs=[ar_out.ap().opt()]),
                waits=[(s_car, 16)], incs=[(s_ccar, 1)], dma=True)
            p.step("gpsimd", lambda E: E.dma_start(out=swm[:, 4:7], in_=ar_out.ap()),
                   waits=[(s_ccar, 1)], incs=[(s_cswm, 16)], dma=True)
            p.step("vector", lambda E: E.tensor_scalar(swm[:, 0:3], swm[:, 4:7],
                                                       1.0 / (2048.0 * 8192.0), 1e-5,
                                                       ALU.mult, ALU.max),
                   waits=[(s_cswm, 16)], incs=[(s_dve, 1)])
            p.step("vector", lambda E: E.reciprocal(swm[:, 4:7], swm[:, 0:3]),
                   incs=[(s_dve, 1)])
            t_sw = dve_flush(2)
            tt = p.step("tensor", lambda E: E.matmul(pbc[:, 20:23], ones_r[:], swm[:, 0:3],
                                                     start=True, stop=True),
                        waits=[(s_dve, t_sw)], incs=[(s_pe, 1)])
            tt2 = p.step("tensor", lambda E: E.matmul(pbc[:, 24:27], ones_r[:], swm[:, 4:7],
                                                      start=True, stop=True), incs=[(s_pe, 1)])
            tt2 = pe_flush()
            tt2 = pe_flush()
            p.step("vector", lambda E: E.tensor_copy(mw_rep[:, 0:3], pbc[:, 20:23]),
                   waits=[(s_pe, tt2)], incs=[(s_dve, 1)])
            t_swrep = p.step("vector", lambda E: E.tensor_copy(sw_rep[:, 0:3], pbc[:, 24:27]),
                             waits=[(s_pe, tt2)], incs=[(s_dve, 1)])

            def ag_dst(widx, r, cb):
                if widx < 2:
                    agt = ag_g_in if widx == 0 else ag_u_in
                    return agt.ap().rearrange("(fb c p) n -> c p fb n", c=2, p=128)[
                        r, :, cb * 4:(cb + 1) * 4, :]
                return ag_d_in.ap().rearrange("(d fl p) n -> fl p d n", fl=8, p=128)[
                    r, :, cb * 2:(cb + 1) * 2, :]

            last_q1 = {0: last_red[0], 1: last_red[1]}
            q3_tick = {0: 0, 1: 0}
            store_tick = {0: 0, 1: 0}
            for i, (ap_, widx, r, cb) in enumerate(shard_tiles):
                par = i % 2
                shb = shbs[par]
                shq1 = shq1s[par]
                tld = p.step("gpsimd", lambda E, ap_=ap_, par=par: E.dma_start(
                    out=shstage[par][:], in_=ap_),
                    waits=[(s_dve, last_q1[par])], incs=[(s_ldsa[par], 16)], dma=True)
                p.step("vector", lambda E, widx=widx, par=par, shq1=shq1: E.tensor_scalar(
                    shq1[:], shstage[par][:], sw_rep[:, widx:widx + 1], 1.49,
                    ALU.mult, ALU.min),
                    waits=[(s_ldsa[par], tld), (s_dve, t_swrep)], incs=[(s_dve, 1)])
                last_q1[par] = p.step("vector", lambda E, shq1=shq1: E.tensor_scalar(
                    shq1[:], shq1[:], -1.49, MAGIC, ALU.max, ALU.add),
                    incs=[(s_dve, 1)])
                q3_tick[par] = p.step("vector", lambda E, shq1=shq1: E.tensor_scalar(
                    shi[:], shq1[:], -MAGIC, None, ALU.add),
                    incs=[(s_dve, 1)])
                tq = p.step("vector", lambda E, shb=shb: E.tensor_copy(shb[:], shi[:]),
                            waits=[(s_cag[par], store_tick[par])], incs=[(s_dve, 1)])
                if widx < 2:
                    src = shb[:].rearrange("p (fb n) -> p fb n", n=256)
                else:
                    src = shb[:].rearrange("p (d n) -> p d n", n=512)
                p.step("gpsimd", lambda E, dst=ag_dst(widx, r, cb), src=src: E.dma_start(
                    out=dst, in_=src),
                    waits=[(s_dve, tq)], incs=[(s_cag[par], 16)], dma=True)
                store_tick[par] = p.now(s_cag[par])
                yield
            t_ag_all = [p.now(s_cag[0]), p.now(s_cag[1])]
            _cc['g'] = p.step("gpsimd", lambda E: E.collective_compute(
                "AllGather", ALU.bypass, replica_groups=[list(range(8))],
                ins=[ag_g_in.ap().opt()], outs=[qgT.ap().opt()]),
                waits=[(s_cag[0], t_ag_all[0]), (s_cag[1], t_ag_all[1])],
                incs=[(s_ccg, 1)], dma=True)
            _cc['u'] = p.step("gpsimd", lambda E: E.collective_compute(
                "AllGather", ALU.bypass, replica_groups=[list(range(8))],
                ins=[ag_u_in.ap().opt()], outs=[quT.ap().opt()]), incs=[(s_ccu, 1)], dma=True)
            _cc['d'] = p.step("gpsimd", lambda E: E.collective_compute(
                "AllGather", ALU.bypass, replica_groups=[list(range(8))],
                ins=[ag_d_in.ap().opt()], outs=[qdT.ap().opt()]), incs=[(s_ccd, 1)], dma=True)
            _cc['fl'] = p.step("gpsimd", lambda E: E.collective_compute(
                "AllReduce", ALU.add, replica_groups=[list(range(8))],
                ins=[ar_in.ap().opt()], outs=[ar_out2.ap().opt()]),
                incs=[(s_ccf, 1)], dma=True)

            while True:
                yield

        qgen = quant_gen()
        _cc = {}
        def _qset():
            pass
        sw1 = ExitStack()
        Wkv_b = [sw1.enter_context(nc.sbuf_tensor(f"wkvb{c}", [128, 512], BF16))
                 for c in range(16)]
        cosk = sw1.enter_context(nc.sbuf_tensor("cosk", [128, 16 * 64], F32))
        sink = sw1.enter_context(nc.sbuf_tensor("sink", [128, 16 * 64], F32))
        pmat = sw1.enter_context(nc.sbuf_tensor("pmat", [128, 8 * NPOOL], BF16))
        wstage1 = [sw1.enter_context(nc.sbuf_tensor(f"wst{i}", [128, 512], F32))
                  for i in range(2)]
        xstage1 = [sw1.enter_context(nc.sbuf_tensor(f"xst{i}", [128, D], F32))
                  for i in range(2)]
        xbst1 = sw1.enter_context(nc.sbuf_tensor("xbst1", [128, D], BF16))
        xtst1 = sw1.enter_context(nc.sbuf_tensor("xtst1", [128, D], BF16))
        rtmpk1 = sw1.enter_context(nc.sbuf_tensor("rtmpk1", [128, 256], BF16))
        rtmpk2 = sw1.enter_context(nc.sbuf_tensor("rtmpk2", [128, 256], BF16))

        for dst, src in ((cosk[:], dcosk), (sink[:], dsink),
                         (pmat[:].rearrange("p (c n) -> p c n", n=NPOOL),
                          dpmat.rearrange("(c p) n -> p c n", p=128))):
            p.step("sync", lambda E, dst=dst, src=src: E.dma_start(out=dst, in_=src),
                   incs=[(s_ldc2, 16)])
        t_ldck = p.now(s_ldc2)

        conv_t = [0, 0]
        for c in range(16):
            par = c % 2
            tld = p.step("sync", lambda E, c=c, par=par: E.dma_start(
                out=wstage1[par][:], in_=dWkv[c * 128:(c + 1) * 128, :]),
                waits=[(s_act, conv_t[par])], incs=[(s_ldwa[par], 16)])
            conv_t[par] = p.step("scalar", lambda E, c=c, par=par: E.activation(
                Wkv_b[c][:], wstage1[par][:], AF.Copy, bias=0.0, scale=1.0),
                waits=[(s_ldwa[par], tld)], incs=[(s_act, 1)])
        t_wkv = p.now(s_act)

        ps1 = ExitStack()
        pQKV1 = [ps1.enter_context(nc.psum_tensor(f"pqkv{i}", [128, 512], F32))
                for i in range(2)]
        pTr1 = ps1.enter_context(nc.psum_tensor("pTr1", [128, 2048], BF16))
        pPool = ps1.enter_context(nc.psum_tensor("pPool", [NPOOL, 512], F32))

        def rope_steps(view_base, cosv, sinv, wait_t, rA, rB, wait_sem=None):
            lo, hi = view_base[:, :, 32:48], view_base[:, :, 48:64]
            nh = lo.shape[1]
            t1v = rA[:, 0:nh * 16].rearrange("p (h d) -> p h d", d=16)
            t2v = rB[:, 0:nh * 16].rearrange("p (h d) -> p h d", d=16)
            p.step("vector", lambda E: E.tensor_tensor(t1v, lo, sinv, ALU.mult),
                   waits=[(wait_sem or s_dve, wait_t)], incs=[(s_dve, 1)])
            p.step("vector", lambda E: E.tensor_tensor(t2v, lo, cosv, ALU.mult),
                   incs=[(s_dve, 1)])
            p.step("vector", lambda E: E.scalar_tensor_tensor(
                lo, hi, -1.0, sinv, ALU.mult, ALU.mult), incs=[(s_dve, 1)])
            p.step("vector", lambda E: E.tensor_tensor(lo, lo, t2v, ALU.add),
                   incs=[(s_dve, 1)])
            p.step("vector", lambda E: E.tensor_tensor(hi, hi, cosv, ALU.mult),
                   incs=[(s_dve, 1)])
            return p.step("vector", lambda E: E.tensor_tensor(hi, hi, t1v, ALU.add),
                          incs=[(s_dve, 1)])

        xst_free = [0, 0]
        xb_free = 0
        xt_free = 0
        pTr_free = 0
        pq_free = [0, 0]
        kv_rope = [0] * 16
        for c in range(16):
            par = c % 2
            tld = p.step("sync", lambda E, c=c, par=par: E.dma_start(
                out=xstage1[par][:], in_=dxf[c * 128:(c + 1) * 128, :]),
                waits=[(s_act, xst_free[par])], incs=[(s_ldxa[par], 16)])
            tcv = p.step("scalar", lambda E, par=par: E.activation(
                xbst1[:], xstage1[par][:], AF.Copy, bias=0.0, scale=1.0),
                waits=[(s_ldxa[par], tld), (s_pe, xb_free)], incs=[(s_act, 1)])
            xst_free[par] = tcv
            for c2 in range(16):
                tt = p.step("tensor", lambda E, c2=c2: E.transpose(
                    pTr1[:, c2 * 128:(c2 + 1) * 128], xbst1[:, c2 * 128:(c2 + 1) * 128],
                    idt[:]),
                    waits=([(s_act, tcv), (s_act, pTr_free)] if c2 == 0 else ()),
                    incs=[(s_pe, 1)])
            xb_free = tt
            tev = p.step("scalar", lambda E: E.activation(
                xtst1[:], pTr1[:], AF.Copy, bias=0.0, scale=1.0),
                waits=[(s_pe, tt), (s_pe, xt_free)], incs=[(s_act, 1)])
            pTr_free = tev
            for c2 in range(16):
                w = ([(s_act, tev), (s_act, t_wkv), (s_act, pq_free[par])]
                     if c2 == 0 else ())
                tmm = p.step("tensor", lambda E, c2=c2, par=par: E.matmul(
                    pQKV1[par][:], xtst1[:, c2 * 128:(c2 + 1) * 128], Wkv_b[c2][:],
                    start=(c2 == 0), stop=(c2 == 15)),
                    waits=w, incs=[(s_pe, 1)])
            xt_free = tmm
            tev2 = p.step("scalar", lambda E, c=c, par=par: E.activation(
                KV[c][:], pQKV1[par][:], AF.Copy, bias=0.0, scale=1.0),
                waits=[(s_pe, tmm), (s_ldc2, t_ldck)], incs=[(s_act, 1)])
            pq_free[par] = tev2
            kvv = KV[c][:, 0:256].rearrange("p (h d) -> p h d", d=64)
            ck = cosk[:, c * 64:(c + 1) * 64].rearrange("p (h d) -> p h d", d=16)
            sk = sink[:, c * 64:(c + 1) * 64].rearrange("p (h d) -> p h d", d=16)
            kv_rope[c] = rope_steps(kvv, ck, sk, tev2, rtmpk1, rtmpk2,
                                    wait_sem=s_act)
            for _ in range(4):
                next(qgen)

        for c in range(8):
            w = [(s_dve, kv_rope[7])] if c == 0 else ()
            tpool = p.step("tensor", lambda E, c=c: E.matmul(
                pPool[:], pmat[:, c * NPOOL:(c + 1) * NPOOL], KV[c][:],
                start=(c == 0), stop=(c == 7)),
                waits=w, incs=[(s_pe, 1)])
        tpool = pe_flush()
        t_pooled = p.step("scalar", lambda E: E.activation(
            pooled[:], pPool[:], AF.Copy, bias=0.0, scale=1.0),
            waits=[(s_pe, tpool)], incs=[(s_act, 1)])

        kt_dup = [0] * HKV
        for k in range(HKV):
            p.step("tensor", lambda E, k=k: E.transpose(
                pTr1[0:64, 1024:1040], pooled[:, k * 64:(k + 1) * 64],
                idt[0:NPOOL, 0:NPOOL]),
                waits=[(s_act, t_pooled), (s_act, pTr_free)] +
                      ([(s_dve, kv_rope[15])] if k == 0 else []),
                incs=[(s_pe, 1)])
            for c in range(8, 16):
                p.step("tensor", lambda E, k=k, c=c: E.transpose(
                    pTr1[0:64, (c - 8) * 128:(c - 7) * 128],
                    KV[c][:, k * 64:(k + 1) * 64], idt[:]),
                    incs=[(s_pe, 1)])
            tt = pe_flush()
            p.step("scalar", lambda E, k=k: E.activation(
                KT[k][0:64, 0:16], pTr1[0:64, 1024:1040], AF.Copy,
                bias=0.0, scale=1.0),
                waits=[(s_pe, tt)], incs=[(s_act, 1)])
            pTr_free = p.step("scalar", lambda E, k=k: E.activation(
                KT[k][0:64, 16:1040], pTr1[0:64, 0:1024], AF.Copy,
                bias=0.0, scale=1.0),
                incs=[(s_act, 1)])
            kt_dup[k] = p.step("sync", lambda E, k=k: E.dma_start(
                out=KT[k][64:128, :], in_=KT[k][0:64, :]),
                waits=[(s_act, pTr_free)], incs=[(s_ktd, 16)])

        t1a_gate = [(s_dve, p.now(s_dve)), (s_pe, p.now(s_pe)),
                    (s_act, p.now(s_act))]
        ps1.close()
        sw1.close()

        sw2 = ExitStack()
        Wq_b = [sw2.enter_context(nc.sbuf_tensor(f"wqb{c}", [128, 1024], BF16))
                for c in range(16)]
        cosqs = [sw2.enter_context(nc.sbuf_tensor(f"cosq{i}", [128, 256], F32))
                 for i in range(2)]
        sinqs = [sw2.enter_context(nc.sbuf_tensor(f"sinq{i}", [128, 256], F32))
                 for i in range(2)]
        wstage2 = [sw2.enter_context(nc.sbuf_tensor(f"wst2{i}", [128, 1024], F32))
                  for i in range(2)]
        xstage2 = sw2.enter_context(nc.sbuf_tensor("xst2", [128, D], F32))
        xbst2 = sw2.enter_context(nc.sbuf_tensor("xbst2", [128, D], BF16))
        xtst2 = sw2.enter_context(nc.sbuf_tensor("xtst2", [128, D], BF16))
        Qb = sw2.enter_context(nc.sbuf_tensor("qb", [128, 1024], BF16))
        rtmpq1 = sw2.enter_context(nc.sbuf_tensor("rtmpa", [128, 256], BF16))
        rtmpq2 = sw2.enter_context(nc.sbuf_tensor("rtmpb", [128, 256], BF16))

        rope_t = [0, 0]

        conv_t = [0, 0]
        for c in range(16):
            par = c % 2
            tld = p.step("sync", lambda E, c=c, par=par: E.dma_start(
                out=wstage2[par][:], in_=dWq[c * 128:(c + 1) * 128, :]),
                waits=[(s_act, conv_t[par])] + (t1a_gate if c < 2 else []),
                incs=[(s_ldwb[par], 16)])
            conv_t[par] = p.step("scalar", lambda E, c=c, par=par: E.activation(
                Wq_b[c][:], wstage2[par][:], AF.Copy, bias=0.0, scale=1.0),
                waits=[(s_ldwb[par], tld)], incs=[(s_act, 1)])
        t_wq = p.now(s_act)

        ps2 = ExitStack()
        pQKV2 = [ps2.enter_context(nc.psum_tensor(f"pq2{i}", [128, 512], F32))
                for i in range(2)]
        pTr2x = ps2.enter_context(nc.psum_tensor("pTr2", [128, 2048], BF16))

        xst2_free = 0
        xb_free = 0
        xt_free = 0
        pTr_free = 0
        pq_free = [0, 0]
        for s in range(8):
            tld = p.step("sync", lambda E, s=s: E.dma_start(
                out=xstage2[:], in_=dxo[s * 128:(s + 1) * 128, :]),
                waits=[(s_act, xst2_free), (s_spx, 16 * s)], incs=[(s_ldxb, 16)])
            p.step("sync", lambda E, s=s: E.dma_start(
                out=x_res[:, s * D:(s + 1) * D], in_=xstage2[:]),
                waits=[(s_ldxb, tld)], incs=[(s_spx, 16)])
            tcv = p.step("scalar", lambda E: E.activation(
                xbst2[:], xstage2[:], AF.Copy, bias=0.0, scale=1.0),
                waits=[(s_ldxb, tld), (s_pe, xb_free)], incs=[(s_act, 1)])
            xst2_free = tcv
            for c2 in range(16):
                tt = p.step("tensor", lambda E, c2=c2: E.transpose(
                    pTr2x[:, c2 * 128:(c2 + 1) * 128], xbst2[:, c2 * 128:(c2 + 1) * 128],
                    idt[:]),
                    waits=([(s_act, tcv), (s_act, pTr_free)] if c2 == 0 else ()),
                    incs=[(s_pe, 1)])
            xb_free = tt
            tev = p.step("scalar", lambda E: E.activation(
                xtst2[:], pTr2x[:], AF.Copy, bias=0.0, scale=1.0),
                waits=[(s_pe, tt), (s_pe, xt_free)], incs=[(s_act, 1)])
            pTr_free = tev
            tq_ev = [0, 0]
            for g in range(2):
                for c2 in range(16):
                    w = ([(s_act, tev), (s_act, t_wq), (s_act, pq_free[g])]
                         if c2 == 0 else ())
                    tmm = p.step("tensor", lambda E, c2=c2, g=g: E.matmul(
                        pQKV2[g][:], xtst2[:, c2 * 128:(c2 + 1) * 128],
                        Wq_b[c2][:, g * 512:(g + 1) * 512],
                        start=(c2 == 0), stop=(c2 == 15)),
                        waits=w, incs=[(s_pe, 1)])
                if g == 1:
                    xt_free = tmm
                tq_ev[g] = p.step("scalar", lambda E, g=g: E.activation(
                    Qb[:, g * 512:(g + 1) * 512], pQKV2[g][:], AF.Copy,
                    bias=0.0, scale=1.0),
                    waits=[(s_pe, tmm)], incs=[(s_act, 1)])
                pq_free[g] = tq_ev[g]
            spar = s % 2
            tld_cq = p.step("sync", lambda E, s=s, spar=spar: E.dma_start(
                out=cosqs[spar][:], in_=dcosq[:, s * 256:(s + 1) * 256]),
                waits=[(s_dve, rope_t[spar])] + (t1a_gate if s < 2 else []),
                incs=[(s_ldc3[spar], 16)])
            tld_cq = p.step("sync", lambda E, s=s, spar=spar: E.dma_start(
                out=sinqs[spar][:], in_=dsinq[:, s * 256:(s + 1) * 256]),
                incs=[(s_ldc3[spar], 16)])
            qv = Qb[:].rearrange("p (h d) -> p h d", d=64)
            cq = cosqs[spar][:].rearrange("p (h d) -> p h d", d=16)
            sq = sinqs[spar][:].rearrange("p (h d) -> p h d", d=16)
            p.step("vector", lambda E, spar=spar: E.nop(),
                   waits=[(s_ldc3[spar], 32 * (s // 2 + 1))], incs=[(s_dve, 1)])
            t_rope = rope_steps(qv, cq, sq, tq_ev[1], rtmpq1, rtmpq2,
                                wait_sem=s_act)
            rope_t[spar] = t_rope
            for _ in range(5):
                next(qgen)
            for hp in range(8):
                tt = p.step("tensor", lambda E, hp=hp: E.transpose(
                    pTr2x[:, hp * 128:(hp + 1) * 128], Qb[:, hp * 128:(hp + 1) * 128],
                    idt[:]),
                    waits=([(s_dve, t_rope), (s_act, pTr_free)] if hp == 0 else ()),
                    incs=[(s_pe, 1)])
            pTr_free = p.step("scalar", lambda E, s=s: E.activation(
                QT[s][:], pTr2x[:, 0:1024], AF.Copy, bias=0.0, scale=1.0),
                waits=[(s_pe, tt)], incs=[(s_act, 1)])
        for _ in range(120):
            next(qgen)
        t_qt = p.now(s_act)
        t1b_gate = [(s_dve, p.now(s_dve)), (s_pe, p.now(s_pe)), (s_spx, 16 * 8),
                    (s_cag[0], 24 * 16), (s_cag[1], 24 * 16),
                    (s_act, p.now(s_act))]

        ps2.close()
        sw2.close()

        squant.close()
        st3 = ExitStack()
        Wo_b = [st3.enter_context(nc.sbuf_tensor(f"wob{c}", [128, 2048], BF16))
                for c in range(8)]
        wostage = st3.enter_context(nc.sbuf_tensor("wost", [128, 2048], F32))
        masks = st3.enter_context(nc.sbuf_tensor("masks", [128, MASK_COLS], BF16))
        scobufs = [st3.enter_context(nc.sbuf_tensor(f"scobuf{i}", [128, NKEY], F32))
                   for i in range(2)]
        probs2 = [st3.enter_context(nc.sbuf_tensor(f"probs{i}", [128, NKEY], BF16))
                  for i in range(2)]
        probsT = st3.enter_context(nc.sbuf_tensor("probsT", [128, 1152], BF16))
        AO = [st3.enter_context(nc.sbuf_tensor(f"ao{s}", [128, 1024], BF16))
              for s in range(8)]
        aoT = st3.enter_context(nc.sbuf_tensor("aoT", [128, 1024], BF16))

        p.step("gpsimd", lambda E: E.dma_start(out=masks[:], in_=dmask),
               waits=t1b_gate, incs=[(s_cmask, 16)], dma=True)
        last_conv = 0
        for c in range(8):
            tld = p.step("sync", lambda E, c=c: E.dma_start(
                out=wostage[:], in_=dWo[c * 128:(c + 1) * 128, :]),
                waits=[(s_act, last_conv)] + (t1b_gate if c == 0 else []),
                incs=[(s_lwo, 16)])
            last_conv = p.step("scalar", lambda E, c=c: E.activation(
                Wo_b[c][:], wostage[:], AF.Copy, bias=0.0, scale=1.0),
                waits=[(s_lwo, tld)], incs=[(s_act, 1)])
        t_wo = p.now(s_act)

        pbc_scope.close()
        ps3 = ExitStack()
        pSL2 = [ps3.enter_context(nc.psum_tensor(f"psl{i}", [128, 1024], F32))
                for i in range(2)]
        pMISC = ps3.enter_context(nc.psum_tensor("pmisc", [128, 512], F32))
        pT3 = ps3.enter_context(nc.psum_tensor("pT3", [128, 1024], BF16))
        pT3b = ps3.enter_context(nc.psum_tensor("pT3b", [128, 128], BF16))

        att = dict(psp_free=[0, 0], psl_free=[0, 0], sco_free=[0, 0],
                   probs_dve=[0, 0], pt3_free=0, ao_free=0)

        def att_scores(z):
            par, wp, wl, moff = z["par"], z["wp"], z["wl"], z["moff"]
            kvh, base, lhs = z["kvh"], z["base"], z["lhs"]
            scobuf = scobufs[par]
            w0 = [(s_dve, att["psp_free"][par])]
            if z["i"] < 2:
                w0 += [(s_act, t_qt), (s_cmask, 16), (s_dve, t_elsrep),
                       (s_ktd, 64)]
            tsp = p.step("tensor", lambda E, lhs=lhs, kvh=kvh, wp=wp, base=base, par=par: E.matmul(
                pMISC[:, par * 16:par * 16 + wp], lhs,
                KT[kvh][base:base + 64, 0:wp],
                start=True, stop=True),
                waits=w0, incs=[(s_pe, 1)])
            tsl = None
            if wl > 0:
                nmm = (wl + 511) // 512
                for m in range(nmm):
                    cw = min(512, wl - m * 512)
                    w1 = [(s_dve, att["psl_free"][par])] if m == 0 else ()
                    tsl = p.step("tensor", lambda E, lhs=lhs, kvh=kvh, m=m, cw=cw, base=base, par=par: E.matmul(
                        pSL2[par][:, m * 512:m * 512 + cw], lhs,
                        KT[kvh][base:base + 64, 16 + m * 512:16 + m * 512 + cw],
                        start=True, stop=True),
                        waits=w1, incs=[(s_pe, 1)])
            att["psp_free"][par] = p.step("vector", lambda E, wp=wp, moff=moff, scobuf=scobuf, par=par: E.scalar_tensor_tensor(
                scobuf[:, 0:wp], pMISC[:, par * 16:par * 16 + wp], 30000.0,
                masks[:, moff:moff + wp], ALU.add, ALU.mult),
                waits=[(s_pe, tsp), (s_act, att["sco_free"][par])],
                incs=[(s_dve, 1)])
            if wl > 0:
                att["psl_free"][par] = p.step("vector", lambda E, wp=wp, wl=wl, moff=moff, scobuf=scobuf, par=par: E.scalar_tensor_tensor(
                    scobuf[:, wp:wp + wl], pSL2[par][:, 0:wl], 30000.0,
                    masks[:, moff + wp:moff + wp + wl], ALU.add, ALU.mult),
                    waits=[(s_pe, tsl)], incs=[(s_dve, 1)])

        def att_softmax(z):
            par, W, h = z["par"], z["W"], z["h"]
            scobuf = scobufs[par]
            probs = probs2[par]
            tnm = p.step("vector", lambda E, W=W, scobuf=scobuf: E.reduce_max(
                sc_a[:, 0:1], scobuf[:, 0:W], AX.X, negate=True), incs=[(s_dve, 1)])
            p.step("vector", lambda E, h=h, par=par: E.tensor_tensor(
                sc_b[:, par:par + 1], sc_a[:, 0:1], els_rep[:, h:h + 1], ALU.mult),
                waits=[(s_dve, tnm)], incs=[(s_dve, 1)])
            tbq = dve_flush(1)
            z["tex"] = p.step("scalar", lambda E, h=h, W=W, par=par, probs=probs, scobuf=scobuf: E.activation(
                probs[:, 0:W], scobuf[:, 0:W], AF.Exp,
                bias=sc_b[:, par:par + 1], scale=els_rep[:, h:h + 1]),
                waits=[(s_dve, tbq), (s_dve, att["probs_dve"][par])],
                incs=[(s_act, 1)])
            att["sco_free"][par] = z["tex"]

        def att_sumrecip(z):
            par, W = z["par"], z["W"]
            sl = (z["i"] % 4) * 2
            probs = probs2[par]
            att["probs_dve"][par] = p.step("vector", lambda E, W=W, probs=probs, sl=sl: E.reduce_sum(
                sc_c[:, sl:sl + 1], probs[:, 0:W], AX.X),
                waits=[(s_act, z["tex"])], incs=[(s_dve, 1)])
            p.step("vector", lambda E, sl=sl: E.reciprocal(
                sc_c[:, sl + 1:sl + 2], sc_c[:, sl:sl + 1]),
                incs=[(s_dve, 1)])

        def att_tail1(z):
            wp, wl, nl, par = z["wp"], z["wl"], z["nl"], z["par"]
            probs = probs2[par]
            p.step("tensor", lambda E, wp=wp, probs=probs: E.transpose(
                pT3b[0:wp, 0:128], probs[:, 0:wp], idt[:]),
                waits=[(s_act, z["tex"]), (s_act, att["pt3_free"])],
                incs=[(s_pe, 1)])
            for m in range(nl):
                p.step("tensor", lambda E, wp=wp, m=m, probs=probs: E.transpose(
                    pT3[:, m * 128:(m + 1) * 128],
                    probs[:, wp + m * 128:wp + (m + 1) * 128], idt[:]),
                    incs=[(s_pe, 1)])
            fl = pe_flush()
            if nl > 0:
                p.step("scalar", lambda E, nl=nl: E.activation(
                    probsT[:, 0:nl * 128], pT3[:, 0:nl * 128], AF.Copy,
                    bias=0.0, scale=1.0),
                    waits=[(s_pe, fl)], incs=[(s_act, 1)])
            z["evac"] = p.step("scalar", lambda E, wp=wp: E.activation(
                probsT[0:wp, 1024:1152], pT3b[0:wp, 0:128], AF.Copy,
                bias=0.0, scale=1.0),
                waits=[(s_pe, fl)], incs=[(s_act, 1)])
            att["pt3_free"] = z["evac"]

        def att_tail2(z):
            s_, h, wp, wl, nl, kvh = z["s"], z["h"], z["wp"], z["wl"], z["nl"], z["kvh"]
            w2 = [(s_act, z["evac"]), (s_dve, att["ao_free"])]
            p.step("tensor", lambda E, wp=wp, kvh=kvh, wl=wl: E.matmul(
                pMISC[:, 64:128], probsT[0:wp, 1024:1152],
                pooled[0:wp, 256 + kvh * 64:256 + (kvh + 1) * 64],
                start=True, stop=(wl == 0)),
                waits=w2, incs=[(s_pe, 1)])
            for m in range(nl):
                p.step("tensor", lambda E, m=m, kvh=kvh, nl=nl: E.matmul(
                    pMISC[:, 64:128], probsT[:, m * 128:(m + 1) * 128],
                    KV[8 + m][:, 256 + kvh * 64:256 + (kvh + 1) * 64],
                    start=False, stop=(m == nl - 1)),
                    incs=[(s_pe, 1)])
            fl = pe_flush()
            sl = (z["i"] % 4) * 2
            att["ao_free"] = p.step("vector", lambda E, s_=s_, h=h, sl=sl: E.tensor_scalar(
                AO[s_][:, h * 64:(h + 1) * 64], pMISC[:, 64:128],
                sc_c[:, sl + 1:sl + 2], None, ALU.mult),
                waits=[(s_pe, fl)], incs=[(s_dve, 1)])

        Z = []
        for s_ in range(8):
            W = SLOT_W[s_]
            wp = min(W, 16)
            wl = W - wp
            nl = wl // 128
            moff = SLOT_OFF[s_]
            for h in range(H):
                i = len(Z)
                z = dict(i=i, s=s_, h=h, W=W, wp=wp, wl=wl, nl=nl, moff=moff,
                         par=i % 2, kvh=h // 4, base=64 * (h % 2))
                z["lhs"] = QT[s_][z["base"]:z["base"] + 64,
                                  (h // 2) * 128:(h // 2 + 1) * 128]
                Z.append(z)
                att_scores(z)
                if i >= 3:
                    att_tail2(Z[i - 3])
                if i >= 2:
                    att_tail1(Z[i - 2])
                att_softmax(z)
                if i >= 1:
                    att_sumrecip(Z[i - 1])
        N = len(Z)
        att_sumrecip(Z[N - 1])
        att_tail2(Z[N - 3])
        att_tail1(Z[N - 2])
        att_tail2(Z[N - 2])
        att_tail1(Z[N - 1])
        att_tail2(Z[N - 1])
        t_attn_done = p.now(s_dve)
        if DEBUG_TAPS:
            for s_ in range(8):
                p.step("sync", lambda E, s_=s_: E.dma_start(
                    out=d_ao[s_ * 128:(s_ + 1) * 128, :], in_=AO[s_][:]),
                    waits=[(s_dve, t_attn_done)], incs=[(s_o[1], 16)])
                p.step("sync", lambda E, s_=s_: E.dma_start(
                    out=d_qt[:, s_ * 1024:(s_ + 1) * 1024], in_=QT[s_][:]),
                    waits=[(s_dve, t_attn_done)], incs=[(s_o[1], 16)])
            for k_ in range(HKV):
                p.step("sync", lambda E, k_=k_: E.dma_start(
                    out=d_kt[:, k_ * NKEY:(k_ + 1) * NKEY], in_=KT[k_][:]),
                    waits=[(s_dve, t_attn_done)], incs=[(s_o[1], 16)])
            p.step("sync", lambda E: E.dma_start(out=d_pool, in_=pooled[:]),
                   waits=[(s_dve, t_attn_done)], incs=[(s_o[1], 16)])
        ps3.close()

        ps3b = ExitStack()
        pWo = ps3b.enter_context(nc.psum_tensor("pWo", [128, 2048], F32))
        pTwo = ps3b.enter_context(nc.psum_tensor("pTwo", [128, 1024], BF16))

        ptwo_free = 0
        pwo_free = [0] * 4
        x1_done = [0] * 8
        s1_done = [0] * 8
        for s in range(8):
            for hp in range(8):
                tt = p.step("tensor", lambda E, s=s, hp=hp: E.transpose(
                    pTwo[:, hp * 128:(hp + 1) * 128], AO[s][:, hp * 128:(hp + 1) * 128],
                    idt[:]),
                    waits=([(s_dve, t_attn_done), (s_act, ptwo_free)] if hp == 0 else ()),
                    incs=[(s_pe, 1)])
            ptwo_free = p.step("scalar", lambda E: E.activation(
                aoT[:], pTwo[:], AF.Copy, bias=0.0, scale=1.0),
                waits=[(s_pe, tt)], incs=[(s_act, 1)])
            for dmq in range(4):
                for c in range(8):
                    w = ([(s_act, ptwo_free), (s_dve, pwo_free[dmq]), (s_act, t_wo)]
                         if c == 0 else ())
                    tmm = p.step("tensor", lambda E, dmq=dmq, c=c: E.matmul(
                        pWo[:, dmq * 512:(dmq + 1) * 512], aoT[:, c * 128:(c + 1) * 128],
                        Wo_b[c][:, dmq * 512:(dmq + 1) * 512],
                        start=(c == 0), stop=(c == 7)),
                        waits=w, incs=[(s_pe, 1)])
                pwo_free[dmq] = p.step("vector", lambda E, s=s, dmq=dmq: E.scalar_tensor_tensor(
                    x_res[:, s * D + dmq * 512:s * D + (dmq + 1) * 512],
                    pWo[:, dmq * 512:(dmq + 1) * 512], 1.0,
                    x_res[:, s * D + dmq * 512:s * D + (dmq + 1) * 512],
                    ALU.mult, ALU.add),
                    waits=[(s_pe, tmm), (s_spx, 16 * 8)], incs=[(s_dve, 1)])
            xs = x_res[:, s * D:(s + 1) * D]
            for q4 in range(4):
                p.step("vector", lambda E, xs=xs, q4=q4: E.bn_stats(
                    bnln[:, q4 * 6:(q4 + 1) * 6], xs[:, q4 * 512:(q4 + 1) * 512]),
                    waits=[(s_dve, pwo_free[3])], incs=[(s_dve, 1)])
            p.step("vector", lambda E: E.bn_aggr(bnag[:], bnln[:, 0:24]),
                   incs=[(s_dve, 1)])
            p.step("vector", lambda E: E.tensor_copy(sc_a[:, 1:2], bnag[:, 0:1]),
                   incs=[(s_dve, 1)])
            p.step("vector", lambda E: E.tensor_scalar(
                sc_a[:, 4:5], bnag[:, 1:2], 1e-5, None, ALU.add), incs=[(s_dve, 1)])
            t_var = p.now(s_dve)
            p.step("scalar", lambda E: E.activation(
                sc_a[:, 5:6], sc_a[:, 4:5], AF.Ln, bias=0.0, scale=1.0),
                waits=[(s_dve, t_var)], incs=[(s_act, 1)])
            t_rst = p.step("scalar", lambda E: E.activation(
                sc_a[:, 6:7], sc_a[:, 5:6], AF.Exp, bias=0.0, scale=-0.5),
                incs=[(s_act, 1)])
            x1_done[s] = p.step("vector", lambda E, xs=xs: E.tensor_scalar(
                xs, xs, sc_a[:, 1:2], sc_a[:, 6:7], ALU.subtract, ALU.mult),
                waits=[(s_act, t_rst)], incs=[(s_dve, 1)])
            p.step("vector", lambda E, xs=xs: E.reduce_max(
                sc_a[:, 7:8], xs, AX.X, apply_absolute_value=True),
                waits=[(s_dve, x1_done[s])], incs=[(s_dve, 1)])
            p.step("vector", lambda E: E.tensor_scalar(
                sc_a[:, 7:8], sc_a[:, 7:8], 1e-5, None, ALU.max), incs=[(s_dve, 1)])
            p.step("vector", lambda E: E.reciprocal(sc_b[:, 0:1], sc_a[:, 7:8]),
                   incs=[(s_dve, 1)])
            p.step("vector", lambda E, s=s: E.tensor_scalar(
                st_s1[:, s:s + 1], sc_b[:, 0:1], 127.0, None, ALU.mult),
                incs=[(s_dve, 1)])
            s1_done[s] = dve_flush(2)
        ps3b.close()
        st3.close()

        st4 = ExitStack()
        t4f = st4.enter_context(nc.sbuf_tensor("t4f", [128, D], F32))
        t4i = st4.enter_context(nc.sbuf_tensor("t4i", [128, D], I32))
        t4b = st4.enter_context(nc.sbuf_tensor("t4b", [128, D], BF16))
        t4t = st4.enter_context(nc.sbuf_tensor("t4t", [128, D], BF16))
        ps4 = ExitStack()
        pT4 = ps4.enter_context(nc.psum_tensor("pT4", [128, 2048], BF16))

        t4t_free = 0
        for s in range(8):
            xs = x_res[:, s * D:(s + 1) * D]
            p.step("sync", lambda E, s=s, xs=xs: E.dma_start(
                out=x1_spill.ap()[s * 128:(s + 1) * 128, :], in_=xs),
                waits=[(s_dve, x1_done[s])], incs=[(s_sp1, 16)])
            if DEBUG_TAPS:
                p.step("sync", lambda E, s=s, xs=xs: E.dma_start(
                    out=d_x1[s * 128:(s + 1) * 128, :], in_=xs),
                    waits=[(s_dve, x1_done[s])], incs=[(s_o[0], 16)])
            tma = p.step("scalar", lambda E, s=s, xs=xs: E.activation(
                t4f[:], xs, AF.Copy, bias=MAGIC, scale=st_s1[:, s:s + 1]),
                waits=[(s_dve, s1_done[s]), (s_dve, t4t_free)], incs=[(s_act, 1)])
            p.step("vector", lambda E: E.tensor_scalar(
                t4i[:], t4f[:], -MAGIC, None, ALU.add),
                waits=[(s_act, tma)], incs=[(s_dve, 1)])
            tqb = p.step("vector", lambda E: E.tensor_copy(t4b[:], t4i[:]),
                         incs=[(s_dve, 1)])
            for c2 in range(16):
                tt = p.step("tensor", lambda E, c2=c2: E.transpose(
                    pT4[:, c2 * 128:(c2 + 1) * 128], t4b[:, c2 * 128:(c2 + 1) * 128],
                    idt[:]),
                    waits=([(s_dve, tqb)] if c2 == 0 else ()), incs=[(s_pe, 1)])
            t4t_free = p.step("vector", lambda E: E.tensor_copy(t4t[:], pT4[:]),
                              waits=[(s_pe, tt), (s_spq, p.now(s_spq))],
                              incs=[(s_dve, 1)])
            p.step("sync", lambda E, s=s: E.dma_start(
                out=xqt_spill.ap()[:, s * 2048:(s + 1) * 2048], in_=t4t[:]),
                waits=[(s_dve, t4t_free)], incs=[(s_spq, 16)])
        gate_w = [(s_sp1, p.now(s_sp1)), (s_spq, p.now(s_spq)),
                  (s_dve, p.now(s_dve)), (s_act, p.now(s_act)),
                  (s_pe, p.now(s_pe)), (s_pool, p.now(s_pool))]
        if DEBUG_TAPS:
            gate_w = gate_w + [(s_o[i], p.now(s_o[i])) for i in range(4)]
        ps4.close()
        st4.close()
        skv.close()
        sx.close()

        s2 = ExitStack()
        SLAB = [s2.enter_context(nc.sbuf_tensor(f"slab{hf}", [128, 4 * FF], BF16))
                for hf in range(2)]
        g_rep = s2.enter_context(nc.sbuf_tensor("g_rep", [128, FF], BF16))
        sgbuf = s2.enter_context(nc.sbuf_tensor("sgbuf", [128, 512], BF16))
        aqi = s2.enter_context(nc.sbuf_tensor("aqi", [128, 512], I32))
        aqbs = [s2.enter_context(nc.sbuf_tensor(f"aqb{i}", [128, 512], BF16))
                for i in range(2)]
        ps2c = ExitStack()
        pbc2 = ps2c.enter_context(nc.psum_tensor("pbc2", [128, 512], F32))
        sgr = ExitStack()
        gsub_sb = sgr.enter_context(nc.sbuf_tensor("gsub_sb", [1, FF], F32))
        p.step("sync", lambda E: E.dma_start(out=gsub_sb[:], in_=dgsub),
               waits=gate_w, incs=[(s_ldc4, 16)])
        for gblk in range(16):
            tt = p.step("tensor", lambda E, gblk=gblk: E.matmul(
                pbc2[:], ones_r[:], gsub_sb[:, gblk * 512:(gblk + 1) * 512],
                start=True, stop=True),
                waits=[(s_ldc4, 16)] + ([(s_dve, 0)] if gblk == 0 else
                                        [(s_dve, p.now(s_dve))]),
                incs=[(s_pe, 1)])
            tt = pe_flush()
            p.step("vector", lambda E, gblk=gblk: E.tensor_copy(
                g_rep[:, gblk * 512:(gblk + 1) * 512], pbc2[:]),
                waits=[(s_pe, tt)], incs=[(s_dve, 1)])
        t_grep = p.now(s_dve)
        g_gate = [(s_pe, p.now(s_pe)), (s_dve, t_grep)]
        sgr.close()

        s2a = ExitStack()
        XQT = s2a.enter_context(nc.sbuf_tensor("XQT", [128, 4 * 2048], BF16))
        gringM = s2a.enter_context(nc.sbuf_tensor("gringM", [128, 16 * 512], BF16))
        uringM = s2a.enter_context(nc.sbuf_tensor("uringM", [128, 16 * 512], BF16))
        rgbuf = s2a.enter_context(nc.sbuf_tensor("rgbuf", [128, 512], F32))
        rg2buf = s2a.enter_context(nc.sbuf_tensor("rg2buf", [128, 512], F32))

        ps2f = ExitStack()
        pTaq = ps2f.enter_context(nc.psum_tensor("pTaq", [128, 1024], BF16))
        pQ2 = ps2f.enter_context(nc.psum_tensor("pQ2", [128, 512], F32))
        psgu = ExitStack()
        pG = [psgu.enter_context(nc.psum_tensor(f"pG{i}", [128, 512], F32)) for i in range(2)]
        pU = [psgu.enter_context(nc.psum_tensor(f"pU{i}", [128, 512], F32)) for i in range(2)]

        gring_free = [0, 0]
        uring_free = [0, 0]
        pg_free = [0, 0]
        pu_free = [0, 0]
        xqt_free = 0
        prod_end = {}
        rowS_done = [0] * 8
        drow_done = [0] * 8
        aq_last_write = [0, 0]
        slab_prod = [[0] * 64 for _ in range(2)]
        rg_free = 0
        rg2_free = 0
        sg_free = 0
        tg_free = 0
        aqi_free = 0
        aqb_free = 0
        pq2_free = 0
        ptaq_free = 0

        qg_view = qgT.ap().rearrange("(r fb c p) n -> p fb r c n", r=8, fb=32, p=128)
        qu_view = quT.ap().rearrange("(r fb c p) n -> p fb r c n", r=8, fb=32, p=128)
        qd_view = qdT.ap().rearrange("(r d fl p) n -> r d fl p n", r=8, d=4, p=128)

        gring_free = 0
        uring_free = 0
        aq_state = dict(sg_free=0, pq_free=[0, 0], ptaq_free=[0, 0],
                        half_free=[0, 0])

        def aq_gen(hf):
            st = aq_state

            def stage_a(j):
                t, grp = j // 16, j % 16
                par = j % 2
                sg_i = hf * 4 + t
                slab_sl = SLAB[hf][:, t * FF + grp * 512:t * FF + (grp + 1) * 512]
                grs = g_rep[:, grp * 512:(grp + 1) * 512]
                pq = pQ2 if par == 0 else pbc2
                t_a1 = p.step("vector", lambda E, slab_sl=slab_sl, grs=grs: E.tensor_tensor(
                    sgbuf[:], slab_sl, grs, ALU.mult),
                    waits=[(s_dve, rowS_done[sg_i]), (s_act, st["sg_free"])] +
                          ([(s_o[2], tap_slab)] if tap_slab and hf == 0 else []),
                    incs=[(s_dve, 1)])
                t_a2 = p.step("scalar", lambda E, sg_i=sg_i, pq=pq: E.activation(
                    pq[:], sgbuf[:], AF.Copy, bias=MAGIC,
                    scale=st_rowS[:, sg_i:sg_i + 1]),
                    waits=[(s_dve, t_a1), (s_dve, st["pq_free"][par])],
                    incs=[(s_act, 1)])
                st["sg_free"] = t_a2
                t_a3 = p.step("vector", lambda E, pq=pq: E.tensor_scalar(
                    aqi[:], pq[:], -MAGIC, None, ALU.add),
                    waits=[(s_act, t_a2)], incs=[(s_dve, 1)])
                st["pq_free"][par] = t_a3
                t_a4 = p.step("vector", lambda E, par=par: E.tensor_copy(
                    aqbs[par][:], aqi[:]),
                    waits=[(s_pe, st["ptaq_free"][par])], incs=[(s_dve, 1)])
                return t_a4

            def stage_b(j, t_a4):
                t, grp = j // 16, j % 16
                par = j % 2
                slab_sl = SLAB[hf][:, t * FF + grp * 512:t * FF + (grp + 1) * 512]
                for c4 in range(4):
                    p.step("tensor", lambda E, c4=c4, par=par: E.transpose(
                        pTaq[:, par * 512 + c4 * 128:par * 512 + (c4 + 1) * 128],
                        aqbs[par][:, c4 * 128:(c4 + 1) * 128], idt[:]),
                        waits=([(s_dve, t_a4), (s_act, st["half_free"][par])]
                               if c4 == 0 else ()),
                        incs=[(s_pe, 1)])
                tt = pe_flush()
                st["ptaq_free"][par] = tt
                aq_last_write[hf] = p.step("scalar", lambda E, slab_sl=slab_sl, par=par: E.activation(
                    slab_sl, pTaq[:, par * 512:(par + 1) * 512], AF.Copy,
                    bias=0.0, scale=1.0),
                    waits=[(s_pe, tt)], incs=[(s_act, 1)])
                st["half_free"][par] = aq_last_write[hf]

            prev = None
            for j in range(64):
                if prev is not None:
                    stage_b(*prev)
                prev = (j, stage_a(j))
                yield
            stage_b(*prev)
            while True:
                yield

        def emit_gup(hf, driver=None):
            if DEBUG_TAPS and hf == 0:
                for r_ in range(8):
                    p.step("sync", lambda E, r_=r_: E.dma_start(
                        out=d_qg[r_ * 128:(r_ + 1) * 128, :],
                        in_=qgT.ap()[r_ * 8192:r_ * 8192 + 128, :]),
                        waits=[(s_ccg, 1), (s_ccf, 1)], incs=[(s_o[3], 16)])
                    p.step("sync", lambda E, r_=r_: E.dma_start(
                        out=d_qd[r_ * 128:(r_ + 1) * 128, :],
                        in_=qdT.ap()[r_ * 4096:r_ * 4096 + 128, :]),
                        waits=[(s_ccd, 1), (s_ccf, 1)], incs=[(s_o[3], 16)])
            xw = (gate_w + g_gate + [(s_pe, emit_gup.xqt_free)]) if hf == 0 \
                else [(s_pe, emit_gup.xqt_free)]
            t_xqt = p.step("sync", lambda E, hf=hf: E.dma_start(
                out=XQT[:], in_=xqt_spill.ap()[:, hf * 4 * 2048:(hf + 1) * 4 * 2048]),
                waits=xw, incs=[(s_xqt, 16)])
            for fbp in range(16):
                wg = [(s_pe, emit_gup.gring_free)]
                if hf == 0 and fbp == 0:
                    wg = wg + gate_w + g_gate + [(s_ccg, 1), (s_ccf, 1)]
                ld_g = 0
                for sub in range(2):
                    fb = fbp * 2 + sub
                    for clv in range(2):
                        ld_g = p.step("sync", lambda E, fb=fb, sub=sub, clv=clv: E.dma_start(
                            out=gringM[:].rearrange("p (r cl sx n) -> p r cl sx n",
                                                    r=8, cl=2, sx=2)[:, :, clv, sub, :],
                            in_=qg_view[:, fb, :, clv, :]),
                            waits=(wg if sub == 0 and clv == 0 else ()),
                            incs=[(s_lg[0], 16)])
                wu = [(s_pe, emit_gup.uring_free)]
                if hf == 0 and fbp == 0:
                    wu = wu + g_gate + [(s_ccu, 1), (s_ccf, 1)]
                ld_u = 0
                for sub in range(2):
                    fb = fbp * 2 + sub
                    for clv in range(2):
                        ld_u = p.step("sync", lambda E, fb=fb, sub=sub, clv=clv: E.dma_start(
                            out=uringM[:].rearrange("p (r cl sx n) -> p r cl sx n",
                                                    r=8, cl=2, sx=2)[:, :, clv, sub, :],
                            in_=qu_view[:, fb, :, clv, :]),
                            waits=(wu if sub == 0 and clv == 0 else ()),
                            incs=[(s_lu[0], 16)])
                for t in range(4):
                    tpar = t % 2
                    sg_i = hf * 4 + t
                    for c2 in range(16):
                        w = []
                        if c2 == 0:
                            w = [(s_lg[0], ld_g), (s_act, pg_free[tpar]),
                                 (s_xqt, t_xqt)]
                        tmm = p.step("tensor", lambda E, c2=c2, t=t, tpar=tpar: E.matmul(
                            pG[tpar][:],
                            XQT[:, t * 2048 + c2 * 128:t * 2048 + (c2 + 1) * 128],
                            gringM[:, c2 * 512:(c2 + 1) * 512],
                            start=(c2 == 0), stop=(c2 == 15)),
                            waits=w, incs=[(s_pe, 1)])
                    emit_gup.gring_free = tmm
                    for c2 in range(16):
                        w = []
                        if c2 == 0:
                            w = [(s_lu[0], ld_u), (s_dve, pu_free[tpar])]
                        tmu = p.step("tensor", lambda E, c2=c2, t=t, tpar=tpar: E.matmul(
                            pU[tpar][:],
                            XQT[:, t * 2048 + c2 * 128:t * 2048 + (c2 + 1) * 128],
                            uringM[:, c2 * 512:(c2 + 1) * 512],
                            start=(c2 == 0), stop=(c2 == 15)),
                            waits=w, incs=[(s_pe, 1)])
                    emit_gup.uring_free = tmu
                    emit_gup.xqt_free = tmu
                    slab_sl = SLAB[hf][:, t * FF + fbp * 512:t * FF + (fbp + 1) * 512]
                    grs = g_rep[:, fbp * 512:(fbp + 1) * 512]
                    pg_free[tpar] = p.step("scalar", lambda E, tpar=tpar: E.activation(
                        rgbuf[:], pG[tpar][:], AF.Relu, bias=0.0, scale=1.0),
                        waits=[(s_pe, tmm), (s_dve, emit_gup.rg_free)], incs=[(s_act, 1)])
                    t_p2 = p.step("scalar", lambda E: E.activation(
                        rg2buf[:], rgbuf[:], AF.Square, bias=0.0, scale=1.0),
                        waits=[(s_dve, emit_gup.rg2_free)], incs=[(s_act, 1)])
                    t_p3 = p.step("vector", lambda E, slab_sl=slab_sl, tpar=tpar: E.scalar_tensor_tensor(
                        slab_sl, pU[tpar][:], 1.0, rg2buf[:], ALU.mult, ALU.mult),
                        waits=[(s_pe, tmu), (s_act, t_p2)], incs=[(s_dve, 1)])
                    pu_free[tpar] = t_p3
                    emit_gup.rg2_free = t_p3
                    p.step("vector", lambda E, slab_sl=slab_sl, sg_i=sg_i, fbp=fbp: E.bn_stats(
                        gup_bn[:, (sg_i * 16 + fbp) * 6:(sg_i * 16 + fbp + 1) * 6],
                        slab_sl),
                        waits=[(s_dve, t_p3)], incs=[(s_dve, 1)])
                    t_p5 = p.step("vector", lambda E, slab_sl=slab_sl, grs=grs: E.tensor_tensor(
                        rgbuf[:], slab_sl, grs, ALU.mult),
                        waits=[(s_dve, t_p3), (s_dve, t_grep)],
                        incs=[(s_dve, 1)])
                    emit_gup.rg_free = p.step("vector", lambda E, sg_i=sg_i, fbp=fbp: E.reduce_max(
                        gup_mxg[:, sg_i * 16 + fbp:sg_i * 16 + fbp + 1], rgbuf[:],
                        AX.X, apply_absolute_value=True),
                        waits=[(s_dve, t_p5)], incs=[(s_dve, 1)])
                    if driver is not None:
                        next(driver)
            prod_end[hf] = dict(dve=p.now(s_dve), act=p.now(s_act),
                                pool=p.now(s_pool), pe=p.now(s_pe))
            tap = 0
            if DEBUG_TAPS and hf == 0:
                tap = p.step("sync", lambda E: E.dma_start(out=d_slab, in_=SLAB[0][:]),
                             waits=[(s_dve, prod_end[0]["dve"]),
                                    (s_pool, prod_end[0]["pool"]),
                                    (s_act, prod_end[0]["act"])],
                             incs=[(s_o[2], 16)])
            return tap

        emit_gup.gring_free = 0
        emit_gup.uring_free = 0
        emit_gup.xqt_free = 0
        emit_gup.rg_free = 0
        emit_gup.rg2_free = 0
        emit_gup.tg_free = 0

        def emit_stats(hf):
                for t in range(4):
                    sg_i = hf * 4 + t
                    w0 = [(s_act, prod_end[hf]["act"]), (s_pool, prod_end[hf]["pool"]),
                          (s_dve, prod_end[hf]["dve"])]
                    cb = sg_i * 12
                    p.step("vector", lambda E, sg_i=sg_i: E.bn_aggr(
                        bnag[:], gup_bn[:, sg_i * 96:(sg_i + 1) * 96]),
                        waits=w0, incs=[(s_dve, 1)])
                    p.step("vector", lambda E, cb=cb: E.tensor_tensor(
                        st_st[:, cb + 0:cb + 1], bnag[:, 0:1], bnag[:, 0:1], ALU.mult),
                        incs=[(s_dve, 1)])
                    p.step("vector", lambda E, cb=cb: E.tensor_tensor(
                        st_st[:, cb + 0:cb + 1], st_st[:, cb + 0:cb + 1], bnag[:, 1:2],
                        ALU.add), incs=[(s_dve, 1)])
                    p.step("vector", lambda E, sg_i=sg_i, cb=cb: E.reduce_max(
                        st_st[:, cb + 1:cb + 2], gup_mxg[:, sg_i * 16:(sg_i + 1) * 16],
                        AX.X), waits=w0, incs=[(s_dve, 1)])
                    p.step("vector", lambda E, sg_i=sg_i, cb=cb: E.reciprocal(
                        st_st[:, cb + 2:cb + 3], st_s1[:, sg_i:sg_i + 1]), incs=[(s_dve, 1)])
                    p.step("vector", lambda E, cb=cb: E.tensor_tensor(
                        st_st[:, cb + 3:cb + 4], st_st[:, cb + 2:cb + 3],
                        st_st[:, cb + 2:cb + 3], ALU.mult), incs=[(s_dve, 1)])
                    p.step("vector", lambda E, cb=cb: E.tensor_tensor(
                        st_st[:, cb + 3:cb + 4], st_st[:, cb + 3:cb + 4],
                        st_st[:, cb + 2:cb + 3], ALU.mult), incs=[(s_dve, 1)])
                    p.step("vector", lambda E, cb=cb: E.tensor_tensor(
                        sc_a[:, 4:5], mw_rep[:, 0:1], mw_rep[:, 0:1], ALU.mult),
                        incs=[(s_dve, 1)])
                    p.step("vector", lambda E, cb=cb: E.tensor_tensor(
                        sc_a[:, 4:5], sc_a[:, 4:5], mw_rep[:, 1:2], ALU.mult),
                        incs=[(s_dve, 1)])
                    p.step("vector", lambda E, cb=cb: E.tensor_tensor(
                        st_st[:, cb + 3:cb + 4], st_st[:, cb + 3:cb + 4], sc_a[:, 4:5],
                        ALU.mult), incs=[(s_dve, 1)])
                    p.step("vector", lambda E, cb=cb: E.tensor_tensor(
                        st_st[:, cb + 4:cb + 5], st_st[:, cb + 3:cb + 4],
                        st_st[:, cb + 3:cb + 4], ALU.mult), incs=[(s_dve, 1)])
                    p.step("vector", lambda E, cb=cb: E.tensor_tensor(
                        st_st[:, cb + 4:cb + 5], st_st[:, cb + 4:cb + 5],
                        st_st[:, cb + 0:cb + 1], ALU.mult), incs=[(s_dve, 1)])
                    p.step("vector", lambda E, cb=cb: E.tensor_scalar(
                        st_st[:, cb + 4:cb + 5], st_st[:, cb + 4:cb + 5], 1e-6, None,
                        ALU.add), incs=[(s_dve, 1)])
                    t_v = dve_flush(2)
                    p.step("scalar", lambda E, cb=cb: E.activation(
                        st_st[:, cb + 5:cb + 6], st_st[:, cb + 4:cb + 5], AF.Ln,
                        bias=0.0, scale=1.0),
                        waits=[(s_dve, t_v)], incs=[(s_act, 1)])
                    p.step("scalar", lambda E, cb=cb: E.activation(
                        st_st[:, cb + 6:cb + 7], st_st[:, cb + 5:cb + 6], AF.Exp,
                        bias=0.0, scale=-0.5), incs=[(s_act, 1)])
                    t_r = act_flush(1)
                    p.step("vector", lambda E, cb=cb: E.tensor_tensor(
                        st_st[:, cb + 7:cb + 8], st_st[:, cb + 1:cb + 2],
                        st_st[:, cb + 3:cb + 4], ALU.mult),
                        waits=[(s_act, t_r)], incs=[(s_dve, 1)])
                    p.step("vector", lambda E, cb=cb: E.tensor_tensor(
                        st_st[:, cb + 7:cb + 8], st_st[:, cb + 7:cb + 8],
                        st_st[:, cb + 6:cb + 7], ALU.mult), incs=[(s_dve, 1)])
                    p.step("vector", lambda E, cb=cb: E.tensor_scalar(
                        st_st[:, cb + 7:cb + 8], st_st[:, cb + 7:cb + 8], 1e-5, None,
                        ALU.max), incs=[(s_dve, 1)])
                    p.step("vector", lambda E, cb=cb: E.reciprocal(
                        st_st[:, cb + 8:cb + 9], st_st[:, cb + 7:cb + 8]),
                        incs=[(s_dve, 1)])
                    p.step("vector", lambda E, cb=cb: E.tensor_scalar(
                        st_st[:, cb + 9:cb + 10], st_st[:, cb + 8:cb + 9], 127.0, None,
                        ALU.mult), incs=[(s_dve, 1)])
                    p.step("vector", lambda E, cb=cb: E.tensor_tensor(
                        st_st[:, cb + 10:cb + 11], st_st[:, cb + 3:cb + 4],
                        st_st[:, cb + 6:cb + 7], ALU.mult), incs=[(s_dve, 1)])
                    p.step("vector", lambda E, sg_i=sg_i, cb=cb: E.tensor_tensor(
                        st_rowS[:, sg_i:sg_i + 1], st_st[:, cb + 10:cb + 11],
                        st_st[:, cb + 9:cb + 10], ALU.mult), incs=[(s_dve, 1)])
                    rowS_done[sg_i] = dve_flush(2)
                    p.step("vector", lambda E, cb=cb: E.tensor_tensor(
                        st_st[:, cb + 11:cb + 12], st_st[:, cb + 7:cb + 8],
                        mw_rep[:, 2:3], ALU.mult), incs=[(s_dve, 1)])
                    p.step("vector", lambda E, sg_i=sg_i, cb=cb: E.tensor_scalar(
                        st_drow[:, sg_i:sg_i + 1], st_st[:, cb + 11:cb + 12], 1.0 / 127.0,
                        None, ALU.mult), incs=[(s_dve, 1)])
                    drow_done[sg_i] = dve_flush(2)


        tap_slab = emit_gup(0)
        emit_stats(0)
        aq0 = aq_gen(0)
        emit_gup(1, driver=aq0)
        for _ in range(8):
            next(aq0)
        emit_stats(1)
        aq1 = aq_gen(1)
        if DEBUG_TAPS:
            p.step("sync", lambda E: E.dma_start(out=d_aq, in_=SLAB[0][:]),
                   waits=[(s_act, aq_last_write[0])], incs=[(s_o[2], 16)])
            p.step("sync", lambda E: E.dma_start(out=d_rs[:, 0:8], in_=st_rowS[:]),
                   waits=[(s_dve, rowS_done[7])], incs=[(s_o[2], 16)])
            p.step("sync", lambda E: E.dma_start(out=d_rs[:, 8:16], in_=st_drow[:]),
                   waits=[(s_dve, drow_done[7])], incs=[(s_o[2], 16)])
            p.step("sync", lambda E: E.dma_start(out=d_gup[:, 0:128], in_=gup_bn[:, 0:128]),
                   waits=[(s_dve, drow_done[7])], incs=[(s_o[2], 16)])
            p.step("sync", lambda E: E.dma_start(out=d_st[:, 0:96], in_=st_st[:]),
                   waits=[(s_dve, drow_done[7])], incs=[(s_o[2], 16)])
            p.step("sync", lambda E: E.dma_start(out=d_mw[:, 0:4], in_=mw_rep[:]),
                   waits=[(s_dve, drow_done[7])], incs=[(s_o[2], 16)])
            p.step("sync", lambda E: E.dma_start(out=d_mw[:, 4:8], in_=sw_rep[:]),
                   waits=[(s_dve, drow_done[7])], incs=[(s_o[2], 16)])
            p.step("sync", lambda E: E.dma_start(out=d_gup[:, 128:256], in_=gup_mxg[:]),
                   waits=[(s_dve, drow_done[7])], incs=[(s_o[2], 16)])

        s2a.close()
        psgu.close()
        s2b = ExitStack()
        y2 = s2b.enter_context(nc.sbuf_tensor("y2", [128, 4 * D], F32))
        wd_ring = [s2b.enter_context(nc.sbuf_tensor(f"wdr{i}", [128, 512], BF16))
                   for i in range(4)]
        x1st = [s2b.enter_context(nc.sbuf_tensor(f"x1st{i}", [128, 512], F32))
                for i in range(2)]
        psd = ExitStack()
        pD = psd.enter_context(nc.psum_tensor("pD", [128, 2048], F32))

        region_gate = [(s_pe, prod_end[1]["pe"]), (s_dve, prod_end[1]["dve"]),
                       (s_pool, prod_end[1]["pool"]), (s_act, prod_end[1]["act"])]
        wdr_free = [0, 0, 0, 0]
        pd_free = [0] * 4
        x1st_free = [0, 0]
        y2_free = [0] * 4
        for hf in range(2):
            if hf == 1:
                for _ in range(80):
                    next(aq1)
            for ds4 in range(4):
                for fc in range(64):
                    r, floc = fc // 8, fc % 8
                    ring = fc % 4
                    extra = []
                    if hf == 0 and ds4 == 0 and fc < 4:
                        extra = [(s_ccd, 1), (s_ccf, 1)] + region_gate
                    tldw = p.step("sync", lambda E, r=r, ds4=ds4, floc=floc, ring=ring: E.dma_start(
                        out=wd_ring[ring][:], in_=qd_view[r, ds4, floc, :, :]),
                        waits=[(s_pe, wdr_free[ring])] + extra,
                        incs=[(s_lwd[ring], 16)])
                    for t in range(4):
                        w = []
                        if t == 0:
                            w = [(s_lwd[ring], tldw)]
                            if fc == 0:
                                w += [(s_act, aq_last_write[hf])] + \
                                     [(s_dve, pd_free[q]) for q in range(4)]
                        tmm = p.step("tensor", lambda E, t=t, fc=fc, ring=ring, hf=hf: E.matmul(
                            pD[:, t * 512:(t + 1) * 512],
                            SLAB[hf][:, t * FF + fc * 128:t * FF + (fc + 1) * 128],
                            wd_ring[ring][:],
                            start=(fc == 0), stop=(fc == 63)),
                            waits=w, incs=[(s_pe, 1)])
                    wdr_free[ring] = tmm
                    if hf == 0 and fc % 4 == 3:
                        next(aq1)
                for t in range(4):
                    sg_i = hf * 4 + t
                    par = t % 2
                    tldx = p.step("sync", lambda E, sg_i=sg_i, ds4=ds4, par=par: E.dma_start(
                        out=x1st[par][:],
                        in_=x1_spill.ap()[sg_i * 128:(sg_i + 1) * 128,
                                          ds4 * 512:(ds4 + 1) * 512]),
                        waits=[(s_dve, x1st_free[par]), (s_sp1, 16 * 8)] +
                              (region_gate if hf == 0 and ds4 == 0 and t < 2 else []),
                        incs=[(s_lx[par], 16)])
                    pd_free[t] = p.step("vector", lambda E, t=t, ds4=ds4, sg_i=sg_i, par=par: E.scalar_tensor_tensor(
                        y2[:, t * D + ds4 * 512:t * D + (ds4 + 1) * 512],
                        pD[:, t * 512:(t + 1) * 512], st_drow[:, sg_i:sg_i + 1],
                        x1st[par][:], ALU.mult, ALU.add),
                        waits=[(s_pe, tmm), (s_lx[par], tldx), (s_dve, drow_done[sg_i]),
                               (s_o[t], y2_free[t])],
                        incs=[(s_dve, 1)])
                    x1st_free[par] = pd_free[t]

            tap_y2 = 0
            if DEBUG_TAPS and hf == 0:
                tap_y2 = p.step("sync", lambda E: E.dma_start(out=d_y2, in_=y2[:]),
                                waits=[(s_dve, pd_free[0]), (s_dve, pd_free[1]),
                                       (s_dve, pd_free[2]), (s_dve, pd_free[3])],
                                incs=[(s_o[2], 16)])
            for t in range(4):
                sg_i = hf * 4 + t
                ys = y2[:, t * D:(t + 1) * D]
                for q4 in range(4):
                    p.step("vector", lambda E, ys=ys, q4=q4: E.bn_stats(
                        bnln[:, q4 * 6:(q4 + 1) * 6], ys[:, q4 * 512:(q4 + 1) * 512]),
                        waits=[(s_dve, pd_free[t])] +
                              ([(s_o[2], tap_y2)] if tap_y2 and q4 == 0 else []),
                        incs=[(s_dve, 1)])
                p.step("vector", lambda E: E.bn_aggr(bnag[:], bnln[:, 0:24]),
                       incs=[(s_dve, 1)])
                p.step("vector", lambda E: E.tensor_copy(sc_a[:, 1:2], bnag[:, 0:1]),
                       incs=[(s_dve, 1)])
                p.step("vector", lambda E: E.tensor_scalar(
                    sc_a[:, 4:5], bnag[:, 1:2], 1e-5, None, ALU.add), incs=[(s_dve, 1)])
                t_v = dve_flush(2)
                p.step("scalar", lambda E: E.activation(
                    sc_a[:, 5:6], sc_a[:, 4:5], AF.Ln, bias=0.0, scale=1.0),
                    waits=[(s_dve, t_v)], incs=[(s_act, 1)])
                p.step("scalar", lambda E: E.activation(
                    sc_a[:, 6:7], sc_a[:, 5:6], AF.Exp, bias=0.0, scale=-0.5),
                    incs=[(s_act, 1)])
                t_r = act_flush(1)
                t_ln = p.step("vector", lambda E, ys=ys: E.tensor_scalar(
                    ys, ys, sc_a[:, 1:2], sc_a[:, 6:7], ALU.subtract, ALU.mult),
                    waits=[(s_act, t_r)], incs=[(s_dve, 1)])
                y2_free[t] = p.step("sync", lambda E, sg_i=sg_i, ys=ys, t=t: E.dma_start(
                    out=dout[sg_i * 128:(sg_i + 1) * 128, :], in_=ys),
                    waits=[(s_dve, t_ln)], incs=[(s_o[t], 16)])

        p.step("sync", lambda E: E.nop(),
               waits=[(s_o[t], p.now(s_o[t])) for t in range(4)])

        with nc.Block() as block:
            p.emit_all(block)

        psd.close()
        ps2f.close()
        s2b.close()
        s2.close()
        ps2c.close()
        pfl_scope.close()

    return nc



def _rope_tables():
    a32 = np.arange(0, 32, 2).astype(np.float32)
    e_ = a32 / np.float32(32.0)
    base_pow = np.float32(10000.0) ** e_
    inv = (np.float32(1.0) / base_pow).astype(np.float32)
    pos = np.arange(T, dtype=np.float32)
    ang = pos[:, None] * inv[None, :]
    return np.cos(ang).astype(np.float32), np.sin(ang).astype(np.float32)


def _build_masks(chunks):
    m = np.zeros((128, MASK_COLS), np.float32)
    kpos = np.concatenate([np.arange(NPOOL) * 64 + 63, np.arange(1024, 2048)])
    for s, j in enumerate(chunks):
        W = SLOT_W[s]
        q = j * 128 + np.arange(128)
        mm = (kpos[None, :W] <= q[:, None]).astype(np.float32)
        m[:, SLOT_OFF[s]:SLOT_OFF[s] + W] = mm
    return m.astype(ml_dtypes.bfloat16)


def _interleave_qk(w_sem, w_geo, nh):
    out = np.empty((D, nh * 64), np.float32)
    for h in range(nh):
        out[:, h * 64:h * 64 + 32] = w_sem[:, h * 32:(h + 1) * 32]
        out[:, h * 64 + 32:h * 64 + 64] = w_geo[:, h * 32:(h + 1) * 32]
    return out


_PROG_CACHE = {}


def kernel(L0_attn_x, L0_attn_Wq_sem, L0_attn_Wk_sem, L0_attn_Wq_geo, L0_attn_Wk_geo,
           L0_attn_Wv, L0_attn_Wo, L0_attn_logit_scale, L0_ffn_w_gate, L0_ffn_w_up,
           L0_ffn_w_down, L0_ffn_subln_weight):
    x = np.ascontiguousarray(np.asarray(L0_attn_x, np.float32))
    wq = _interleave_qk(np.asarray(L0_attn_Wq_sem, np.float32),
                        np.asarray(L0_attn_Wq_geo, np.float32), H)
    wkv = np.concatenate([
        _interleave_qk(np.asarray(L0_attn_Wk_sem, np.float32),
                       np.asarray(L0_attn_Wk_geo, np.float32), HKV),
        np.asarray(L0_attn_Wv, np.float32)], axis=1)
    wo = np.ascontiguousarray(np.asarray(L0_attn_Wo, np.float32))
    ls = np.asarray(L0_attn_logit_scale, np.float32).reshape(1, H)
    gsub = np.asarray(L0_ffn_subln_weight, np.float32).reshape(1, FF)
    wgT = np.ascontiguousarray(np.asarray(L0_ffn_w_gate, np.float32).T)
    wuT = np.ascontiguousarray(np.asarray(L0_ffn_w_up, np.float32).T)
    wdT = np.ascontiguousarray(np.asarray(L0_ffn_w_down, np.float32).T)

    cos, sin = _rope_tables()
    ident = np.eye(128, dtype=ml_dtypes.bfloat16)
    pmat = np.zeros((1024, NPOOL), np.float32)
    for t in range(1024):
        pmat[t, t // 64] = 1.0 / 64.0
    pmat = pmat.astype(ml_dtypes.bfloat16)
    consts = np.zeros((128, 16), np.float32)
    cvals = [MAGIC, 1.0, 0.0, 1e-5, 1e-6, -0.5, 0.5, 127.0, -MAGIC,
             1.0 / D, 30000.0, 1.49, -1.49, 1.0 / FF, 2.0, -1.0]
    for j, v in enumerate(cvals):
        consts[:, j] = v
    ones_row = np.ones((1, 128), np.float32)
    cosk = np.zeros((128, 16 * 64), np.float32)
    sink = np.zeros((128, 16 * 64), np.float32)
    for c in range(16):
        blk_c = np.tile(cos[c * 128:(c + 1) * 128, :], (1, HKV))
        blk_s = np.tile(sin[c * 128:(c + 1) * 128, :], (1, HKV))
        cosk[:, c * 64:(c + 1) * 64] = blk_c
        sink[:, c * 64:(c + 1) * 64] = blk_s

    in_maps = []
    core_chunks = []
    for core in range(8):
        b = core // 2
        chunks = SET_A if core % 2 == 0 else SET_B
        core_chunks.append(chunks)
        xb_ = x[b]
        x_own = np.concatenate([xb_[j * 128:(j + 1) * 128] for j in chunks], axis=0)
        cosq = np.zeros((128, 8 * 256), np.float32)
        sinq = np.zeros((128, 8 * 256), np.float32)
        for s, j in enumerate(chunks):
            cosq[:, s * 256:(s + 1) * 256] = np.tile(cos[j * 128:(j + 1) * 128], (1, H))
            sinq[:, s * 256:(s + 1) * 256] = np.tile(sin[j * 128:(j + 1) * 128], (1, H))
        in_maps.append({
            "x_full": np.ascontiguousarray(xb_),
            "x_own": np.ascontiguousarray(x_own),
            "wkv": wkv, "wq": wq, "wo": wo, "ls": ls, "gsub": gsub,
            "d_cosq": cosq, "d_sinq": sinq, "d_cosk": cosk, "d_sink": sink,
            "d_masks": _build_masks(chunks),
            "d_ident": ident, "d_pmat": pmat, "d_consts": consts, "ones_row": ones_row,
            "shard_g": np.ascontiguousarray(wgT[core * 256:(core + 1) * 256]),
            "shard_u": np.ascontiguousarray(wuT[core * 256:(core + 1) * 256]),
            "shard_d": np.ascontiguousarray(wdT[core * 1024:(core + 1) * 1024]),
        })

    if "nc" not in _PROG_CACHE:
        _PROG_CACHE["nc"] = build_program()
    nc = _PROG_CACHE["nc"]
    res = run_bass_kernel_spmd(nc, in_maps, core_ids=list(range(8)))

    out = np.empty((B, T, D), np.float32)
    for core in range(8):
        b = core // 2
        y = np.asarray(res.results[core]["y"])
        for s, j in enumerate(core_chunks[core]):
            out[b, j * 128:(j + 1) * 128, :] = y[s * 128:(s + 1) * 128]
    return out


if __name__ == "__main__":
    nc = build_program()
    n = sum(1 for _ in nc.all_instructions())
    print("built ok, instructions:", n)
